# version 25
# baseline (speedup 1.0000x reference)
import numpy as np
from contextlib import ExitStack

import concourse.bass as bass
import concourse.mybir as mybir
from concourse.bass_utils import run_bass_kernel_spmd

F32 = mybir.dt.float32
BF16 = mybir.dt.bfloat16
AF = mybir.ActivationFunctionType
ALU = mybir.AluOpType
AX = mybir.AxisListType

T = 2048
DM = 1024
DI = 2048
DS = 128
NH = 16
HD = 128
DCONV = 4
CH = 256
NCH = T // CH
CONVD = DI + 2 * DS
DPROJ = 2 * DI + 2 * DS + NH
NMT = 35
EPS = 1e-5
NKT = DM // 128
NTT = T // 128
NJT = DI // 128


class Emitter:

    ENGINES = ("tensor", "vector", "scalar", "gpsimd", "sync")

    def __init__(self, nc, ctx):
        self.nc = nc
        self.ctx = ctx
        self.sems = {}
        self.q = None
        self._opened = False

    def sem(self, name):
        if name not in self.sems:
            h = self.ctx.enter_context(self.nc.semaphore(name))
            self.sems[name] = [h, 0]
        return self.sems[name]

    def plan_inc(self, name, k=1):
        s = self.sem(name)
        s[1] += k
        return s[1]

    def val(self, name):
        return self.sem(name)[1]

    def begin(self):
        assert self.q is None
        self.q = {e: [] for e in self.ENGINES}
        self._since_drain = {e: 0 for e in self.ENGINES}

    def end(self, name=None):
        q = self.q
        self.q = None
        with self.nc.Block(name) as blk:
            for e in self.ENGINES:
                if q[e]:
                    getattr(blk, e)(lambda eng, ops=tuple(q[e]): [op(eng) for op in ops])

    def add(self, engine, fn):
        self.q[engine].append(fn)

    def wait(self, engine, sem_name, v=None):
        v = self.val(sem_name) if v is None else v
        h = self.sem(sem_name)[0]
        self.q[engine].append(lambda eng: eng.wait_ge(h, v))

    def drain(self, engine):
        self.q[engine].append(lambda eng: eng.drain())
        self._since_drain[engine] = 0

    def op(self, engine, fn, inc=None, inck=1, autodrain=False, pre_drain=False):
        if pre_drain:
            self.drain(engine)
        if inc is None:
            self.q[engine].append(lambda eng: fn(eng))
            return None
        v = self.plan_inc(inc, inck)
        h = self.sem(inc)[0]
        self.q[engine].append(lambda eng: fn(eng).then_inc(h, inck))
        return v

    def dma(self, engine, out_ap, in_ap, inc, **kw):
        v = self.plan_inc(inc, 16)
        h = self.sem(inc)[0]
        self.q[engine].append(
            lambda eng: eng.dma_start(out=out_ap, in_=in_ap, **kw).then_inc(h, 16))
        return v


def bcast_read(shape_free_ap_dims, src_tensor, src_offset_elems, free_dims):
    return bass.AP(src_tensor, src_offset_elems, [[0, 128]] + list(free_dims))


def build_program(dbg_taps=()):
    nc = bass.Bass("TRN2", target_bir_lowering=False)
    dt = nc.dram_tensor

    u = dt("u", [T, DM], F32, kind="ExternalInput")
    uT = dt("uT", [DM, T], F32, kind="ExternalInput")
    W_inT = dt("W_inT", [DM, DPROJ], F32, kind="ExternalInput")
    conv_w_c = dt("conv_w_c", [128, DCONV, 18], F32, kind="ExternalInput")
    conv_b_c = dt("conv_b_c", [128, 18], F32, kind="ExternalInput")
    dt_bias = dt("dt_bias", [NH, 1], F32, kind="ExternalInput")
    A_log = dt("A_log", [NH, 1], F32, kind="ExternalInput")
    Dvec = dt("Dvec", [1, NH], F32, kind="ExternalInput")
    nw_c = dt("nw_c", [128, NJT], F32, kind="ExternalInput")
    W_outT = dt("W_outT", [DI, DM], F32, kind="ExternalInput")
    W_base = dt("W_base", [DM, DM], F32, kind="ExternalInput")
    b_base = dt("b_base", [1, DM], F32, kind="ExternalInput")
    Wd1 = dt("Wd1", [8, DM, 256], F32, kind="ExternalInput")
    Wd2 = dt("Wd2", [8, 256, DM], F32, kind="ExternalInput")
    bd1_c = dt("bd1_c", [128, 2, 8], F32, kind="ExternalInput")
    bd2_c = dt("bd2_c", [128, 8, 8], F32, kind="ExternalInput")
    Wa_down_r = dt("Wa_down_r", [DM, 8, 64], F32, kind="ExternalInput")
    Wa_up_r = dt("Wa_up_r", [64, 8, DM], F32, kind="ExternalInput")
    ba_down_c = dt("ba_down_c", [64, 8], F32, kind="ExternalInput")
    ba_up = dt("ba_up", [8, DM], F32, kind="ExternalInput")

    out = dt("out", [T, DM], F32, kind="ExternalOutput")

    z_sp = dt("z_sp", [DI, T], BF16)
    xc_sp = dt("xc_sp", [CONVD, T], BF16)
    y3_sp = dt("y3_sp", [DI, T], BF16)
    wout_bf = dt("wout_bf", [DI, DM], BF16)
    wbase_bfd = dt("wbase_bfd", [DM, DM], BF16)
    wadn_bfd = dt("wadn_bfd", [DM, 8, 64], BF16)
    waup_bfd = dt("waup_bfd", [64, 8, DM], BF16)
    baup_bfd = dt("baup_bfd", [8, DM], BF16)
    xt_sp = dt("xt_sp", [T, DI], BF16)
    bt_sp = dt("bt_sp", [T, DS], BF16)
    rows_sp = dt("rows_sp", [40, T], F32)

    dbg = {}
    for name, shape, dtp in (
        ("xn", [DM, T], BF16), ("dtv", [NH, T], F32), ("acs", [NH, T], F32),
        ("xconv", [CONVD, T], BF16), ("z", [DI, T], BF16), ("s", [T, DM], BF16),
        ("g", [T, DM], BF16), ("y2", [DI, T], BF16),
    ):
        if name in dbg_taps:
            dbg[name] = dt("dbg_" + name, shape, dtp, kind="ExternalOutput")

    ctx = ExitStack()
    em = Emitter(nc, ctx)
    sb = lambda n_, s_, dtp=F32: ctx.enter_context(nc.sbuf_tensor(n_, s_, dtp))
    ps = lambda n_, s_, dtp=F32: ctx.enter_context(nc.psum_tensor(n_, s_, dtp))
    pctx = [None]

    def popen():
        pctx[0] = ExitStack()

    def pclose():
        pctx[0].close()
        pctx[0] = None

    sbp = lambda n_, s_, dtp=F32: pctx[0].enter_context(nc.sbuf_tensor(n_, s_, dtp))
    psp = lambda n_, s_, dtp=F32: pctx[0].enter_context(nc.psum_tensor(n_, s_, dtp))

    const_tiles = []

    def need_const(v):
        try:
            nc.const_aps.tensor(v, [128, 1], F32)
        except AssertionError:
            h = ctx.enter_context(nc.sbuf_tensor("cst_%d" % len(const_tiles), [128, 1], F32))
            nc.const_aps.register(F32, v, h[:])
            const_tiles.append((h, v))
    for v in (EPS, -0.5, -1.0, 1.0, 0.0):
        need_const(v)

    onescol_bf = sb("onescol_bf", [128, 1], BF16)
    onesrow_bf = sb("onesrow_bf", [1, 128], BF16)
    id128_bf = sb("id128_bf", [128, 128], BF16)
    id16_f = sb("id16_f", [16, 16], F32)
    id128_f = sb("id128_f", [128, 128], F32)
    convw = sb("convw", [128, DCONV, 18], F32)
    convb = sb("convb", [128, 18], F32)
    dtb_col = sb("dtb_col", [NH, 1], F32)
    alog_col = sb("alog_col", [NH, 1], F32)
    negA = sb("negA", [NH, 1], F32)
    nwcol = sb("nwcol", [128, NJT], F32)
    Dbc = sb("Dbc", [128, NH], F32)
    diagD = sb("diagD", [128, NH, 128], BF16)
    cd_fm = sb("cd_fm", [NH, NCH], F32)
    bd1_cs = sb("bd1_cs", [128, 2, 8], F32)
    rstd_all = sb("rstd_all", [128, NTT], F32)
    bd2_cs = sb("bd2_cs", [128, 8, 8], F32)
    negid = sb("negid", [128, 128], BF16)
    dt_tm = sb("dt_tm", [128, NTT, NH], F32)
    acs_tm = sb("acs_tm", [128, NTT, NH], F32)
    dec_tm = sb("dec_tm", [128, NTT, NH], F32)
    ddc_tm = sb("ddc_tm", [128, NTT, NH], F32)
    ldt_tm = sb("ldt_tm", [128, NTT, NH], F32)
    cd_bc = sb("cd_bc", [128, NH, NCH], F32)
    binmask = sb("binmask", [128, 2, CH], BF16)

    s_d = "s_d"
    s_h = "s_h"
    s_p = "s_p"
    s_v = "s_v"
    s_a = "s_a"
    s_g = "s_g"

    xn_fm = ctx.enter_context(nc.sbuf_tensor("xn_fm", [128, NKT, T], BF16, side="right"))
    popen()
    em.begin()
    g_setup_done = None

    def _setup(g):
        for h, v in const_tiles:
            g.memset(h[:], v)
        g.memset(onescol_bf[:], 1.0)
        g.memset(onesrow_bf[:], 1.0)
        g.memset(id128_bf[:], 1.0)
        g.drain()
        g.affine_select(id128_bf[:], id128_bf[:], [[-1, 128]], ALU.is_equal, 0.0,
                        base=0, channel_multiplier=1)
        g.memset(id16_f[:], 1.0)
        g.drain()
        g.affine_select(id16_f[:], id16_f[:], [[-1, 16]], ALU.is_equal, 0.0,
                        base=0, channel_multiplier=1)
        g.memset(id128_f[:], 1.0)
        g.drain()
        g.affine_select(id128_f[:], id128_f[:], [[-1, 128]], ALU.is_equal, 0.0,
                        base=0, channel_multiplier=1)
        g.memset(binmask[:], 1.0)
        g.drain()
        g.affine_select(binmask[:, 0, :], binmask[:, 0, :], [[1, CH]], ALU.is_ge, 0.0,
                        base=0, channel_multiplier=-1)
        g.drain()
        return g.affine_select(binmask[:, 1, :], binmask[:, 1, :], [[1, CH]], ALU.is_ge, 0.0,
                               base=-128, channel_multiplier=-1)
    g_setup_done = em.op("gpsimd", _setup, inc=s_g)

    em.dma("sync", convw[:], conv_w_c[:, :, :], s_h)
    em.dma("sync", convb[:], conv_b_c[:, :], s_h)
    em.dma("sync", bd1_cs[:, :, :], bd1_c[:, :, :], s_h)
    em.dma("sync", bd2_cs[:, :, :], bd2_c[:, :, :], s_h)
    em.dma("sync", dtb_col[:], dt_bias[:, :], s_h)
    em.dma("sync", alog_col[:], A_log[:, :], s_h)
    em.dma("sync", nwcol[:], nw_c[:, :], s_h)
    v_dbc = em.dma("sync", Dbc[:], bass.AP(Dvec, 0, [[0, 128], [1, NH]]), s_h)

    u_f32 = sbp("u_f32", [128, NKT, T], F32)
    usq = sbp("usq", [128, 2, T], F32)
    onescol_f0 = sbp("onescol_f0", [128, 1], F32)
    v_o1 = em.op("gpsimd", lambda e: e.memset(onescol_f0[:], 1.0), inc=s_g)

    kv = {}
    for k in range(NKT):
        qe = "sync" if k % 2 == 0 else "scalar"
        sm = "s_u%d" % k
        kv[k] = (sm, em.dma(qe, u_f32[:, k, :],
                            bass.AP(uT, k * 128 * T, [[T, 128], [1, T]]), sm))

    psStat = psp("psStat", [1, 2048], F32)
    psStat2 = psp("psStat2", [1, 2048], F32)
    em.wait("tensor", s_g, g_setup_done)
    em.wait("tensor", s_g, v_o1)
    sq_mm_v = {}
    for k in range(NKT):
        em.wait("tensor", kv[k][0], kv[k][1])

        def _mean_k(t, k=k):
            last = None
            for nt in range(4):
                last = t.matmul(psStat[0:1, nt * 512:(nt + 1) * 512], onescol_f0[:],
                                u_f32[:, k, nt * 512:(nt + 1) * 512],
                                start=(k == 0), stop=(k == NKT - 1))
            return last
        em.op("tensor", _mean_k, inc=s_p if k == NKT - 1 else None)
        em.wait("scalar", kv[k][0], kv[k][1])
        if k >= 2:
            em.wait("scalar", s_p, sq_mm_v[k - 2])
        vsq = em.op("scalar", lambda e, k=k: e.activation(
            usq[:, k % 2, :], u_f32[:, k, :], AF.Square), inc=s_a)
        em.wait("tensor", s_a, vsq)

        def _sq_k(t, k=k):
            last = None
            for nt in range(4):
                last = t.matmul(psStat2[0:1, nt * 512:(nt + 1) * 512], onescol_f0[:],
                                usq[:, k % 2, nt * 512:(nt + 1) * 512],
                                start=(k == 0), stop=(k == NKT - 1))
            return last
        sq_mm_v[k] = em.op("tensor", _sq_k, inc=s_p)
    v_stat_mms = em.val(s_p)
    mean_row = sbp("mean_row", [1, T], F32)
    em.wait("vector", s_p, v_stat_mms)
    v_meanrow = em.op("vector", lambda e: e.tensor_scalar_mul(mean_row[:], psStat[0:1, :], 1.0 / DM),
                      inc=s_v)
    rstd_row = sbp("rstd_row", [1, T], F32)
    em.op("vector", lambda e: e.scalar_tensor_tensor(
        rstd_row[:], mean_row[:], -1.0, mean_row[:], ALU.mult, ALU.mult),
        pre_drain=True)
    v_var = em.op("vector", lambda e: e.scalar_tensor_tensor(
        rstd_row[:], psStat2[0:1, :], 1.0 / DM, rstd_row[:], ALU.mult, ALU.add), inc=s_v,
        pre_drain=True)
    em.wait("scalar", s_v, v_var)
    em.wait("scalar", s_g, g_setup_done)
    em.op("scalar", lambda e: e.activation(rstd_row[:], rstd_row[:], AF.Ln, bias=EPS))
    v_rstd = em.op("scalar", lambda e: e.activation(rstd_row[:], rstd_row[:], AF.Exp, scale=-0.5),
                   inc=s_a, pre_drain=True)
    em.wait("sync", s_v, v_meanrow)
    em.wait("sync", s_a, v_rstd)
    em.dma("sync", rows_sp[0:1, :], mean_row[:], s_h)
    v_rows = em.dma("sync", rows_sp[1:2, :], rstd_row[:], s_h)
    mean_bc = sbp("mean_bc", [128, T], F32)
    rstd_bc = sbp("rstd_bc", [128, T], F32)
    em.wait("sync", s_h, v_rows)
    em.dma("sync", mean_bc[:], bass.AP(rows_sp, 0, [[0, 128], [1, T]]), s_h)
    v_bc = em.dma("sync", rstd_bc[:], bass.AP(rows_sp, T, [[0, 128], [1, T]]), s_h)

    em.wait("vector", s_h, v_bc)
    em.wait("gpsimd", s_h, v_bc)
    vsubk = {}
    for k in range(NKT):
        vsubk[k] = em.op("vector", lambda e, k=k: e.tensor_sub(
            u_f32[:, k, :], u_f32[:, k, :], mean_bc[:]), inc=s_v)
    for k in range(NKT):
        em.wait("gpsimd", s_v, vsubk[k])
        em.op("gpsimd", lambda e, k=k: e.tensor_mul(
            xn_fm[:, k, :], u_f32[:, k, :], rstd_bc[:]), inc=s_g)
    v_xn_v = em.val(s_v)
    v_xn_g = em.val(s_g)
    em.wait("sync", s_v, v_xn_v)
    em.wait("sync", s_g, v_xn_g)
    if "xn" in dbg:
        for k in range(NKT):
            em.dma("sync", dbg["xn"][k * 128:(k + 1) * 128, :], xn_fm[:, k, :], s_h)
    em.end("ph0_ln")
    pclose()

    mctx = ExitStack()
    sbm = lambda n_, s_, dtp=F32: mctx.enter_context(nc.sbuf_tensor(n_, s_, dtp))
    dt_fm = sbm("dt_fm", [NH, T], F32)
    acs_fm = sbm("acs_fm", [NH, T], F32)
    row16a = sbm("row16a", [NH, T], F32)
    row16b = sbm("row16b", [NH, T], F32)
    chunkmask = sbm("chunkmask", [NH, T], F32)

    popen()
    em.begin()
    GW = 4
    NG = 9
    wbuf = sbp("wbuf", [128, 3, NKT, GW * 128], BF16)
    wstg = sbp("wstg", [128, 2, NKT, GW * 128], F32)
    ztile = sbp("ztile", [128, 2, T], BF16)
    xbc_sb = sbp("xbc_sb", [128, 2, 4 + T], BF16)
    tapp = sbp("tapp", [128, 2, 4, T], BF16)
    xctile = sbp("xctile", [128, 2, T], BF16)
    psA = psp("psA", [128, T], F32)
    psB = psp("psB", [128, T], F32)
    psum = (psA, psB)

    def _cmask(g):
        g.memset(chunkmask[:], 1.0)
        g.drain()
        last = None
        for c in range(NCH):
            last = g.memset(chunkmask[:, c * CH:c * CH + 1], 0.0)
        return last
    v_cmask = em.op("gpsimd", _cmask, inc=s_g)

    em.wait("vector", s_h)
    def _diagd(v):
        last = None
        for h in range(NH):
            last = v.tensor_scalar(diagD[:, h, :], id128_bf[:], Dbc[:, h:h + 1], None, ALU.mult)
        return last
    v_diagd = em.op("vector", _diagd, inc=s_v)

    def wload(gi):
        sl = gi % 2
        j0 = gi * GW * 128
        w = min(GW * 128, DPROJ - j0)
        em.dma("sync", wstg[:, sl, 0:4, 0:w],
               bass.AP(W_inT, j0, [[DPROJ, 128], [DPROJ * 128, 4], [1, w]]),
               "s_q%d" % sl)
        return em.dma("scalar", wstg[:, sl, 4:8, 0:w],
                      bass.AP(W_inT, 4 * 128 * DPROJ + j0,
                              [[DPROJ, 128], [DPROJ * 128, 4], [1, w]]),
                      "s_q%d" % sl)

    wv = {}
    castv = {}
    em.dma("gpsimd", wout_bf.ap(), W_outT.ap(), "s_wc")
    em.dma("gpsimd", wbase_bfd.ap(), W_base.ap(), "s_wc")
    em.dma("gpsimd", wadn_bfd.ap(), Wa_down_r.ap(), "s_wc")
    em.dma("gpsimd", waup_bfd.ap(), Wa_up_r.ap(), "s_wc")
    em.dma("gpsimd", baup_bfd.ap(), ba_up.ap(), "s_wc")

    def wcast(gi):
        sl = gi % 2
        bsl = gi % 3
        w = min(GW * 128, DPROJ - gi * GW * 128)
        em.wait("vector", "s_q%d" % sl, wv[gi])
        if gi >= 3:
            em.wait("vector", s_p, pe_done[(gi - 2) * GW - 1])
        castv[gi] = em.op("vector", lambda e, bsl=bsl, sl=sl, w=w: e.tensor_copy(
            wbuf[:, bsl, :, 0:w], wstg[:, sl, :, 0:w]), inc=s_v)

    wv[0] = wload(0)
    wv[1] = wload(1)
    wcast(0)
    wcast(1)
    em.wait("sync", s_v, castv[0])
    em.wait("scalar", s_v, castv[0])
    wv[2] = wload(2)
    wcast(2)

    pe_done = {}
    evac_done = {}
    silu_done = {}
    tap_done = {}
    v_dtevac_a = v_dtevac_v = None

    def emit_silu(mt):
        esl = mt % 2
        cm = mt - 16
        em.wait("scalar", tap_done[mt][0], tap_done[mt][1])
        if cm >= 2:
            em.wait("scalar", "s_x%d" % esl)
        silu_done[mt] = em.op(
            "scalar", lambda e, esl=esl, cm=cm: e.activation(
                xctile[:, esl, :], tapp[:, esl, 1, :], AF.Silu,
                bias=convb[:, cm:cm + 1]), inc=s_a)
        em.wait("sync", s_a, silu_done[mt])
        em.dma("sync", xc_sp[cm * 128:(cm + 1) * 128, :], xctile[:, esl, :], "s_x%d" % esl)
        if "xconv" in dbg:
            em.dma("sync", dbg["xconv"][cm * 128:(cm + 1) * 128, :], xctile[:, esl, :],
                   "s_x%d" % esl)

    for mt in range(NMT):
        gi = mt // GW
        sl = gi % 2
        moff = (mt % GW) * 128
        mw = 128 if mt < NMT - 1 else 16
        pslot = psum[mt % 2]
        em.wait("tensor", s_v, castv[gi])
        if mt == 0:
            em.wait("tensor", s_v, v_xn_v)
            em.wait("tensor", s_g, v_xn_g)
        if mt >= 2 and evac_done.get(mt - 2):
            sname, val = evac_done[mt - 2]
            em.wait("tensor", sname, val)

        def _mms(t, bsl=gi % 3, moff=moff, mw=mw, pslot=pslot):
            last = None
            for kt in range(NKT):
                for nt in range(4):
                    last = t.matmul(pslot[0:mw, nt * 512:(nt + 1) * 512],
                                    wbuf[:, bsl, kt, moff:moff + mw],
                                    xn_fm[:, kt, nt * 512:(nt + 1) * 512],
                                    start=(kt == 0), stop=(kt == NKT - 1))
            return last
        pe_done[mt] = em.op("tensor", _mms, inc=s_p)
        if mt % GW == 0 and gi >= 1 and gi + 2 < NG:
            em.wait("sync", s_v, castv[gi])
            em.wait("scalar", s_v, castv[gi])
            wv[gi + 2] = wload(gi + 2)
            wcast(gi + 2)

        esl = mt % 2
        if mt < 16:
            em.wait("scalar", s_p, pe_done[mt])
            if mt >= 2:
                em.wait("scalar", "s_z%d" % esl)
            silu_done[mt] = em.op(
                "scalar", lambda e, esl=esl, pslot=pslot: e.activation(
                    ztile[:, esl, :], pslot[:, :], AF.Silu), inc=s_a)
            evac_done[mt] = (s_a, silu_done[mt])
            em.wait("sync", s_a, silu_done[mt])
            em.dma("sync", z_sp[mt * 128:(mt + 1) * 128, :], ztile[:, esl, :], "s_z%d" % esl)
            if "z" in dbg:
                em.dma("sync", dbg["z"][mt * 128:(mt + 1) * 128, :], ztile[:, esl, :],
                       "s_z%d" % esl)
        elif mt < 34:
            cm = mt - 16
            if cm >= 2 and tap_done.get(mt - 2):
                sname, val = tap_done[mt - 2]
                em.wait("vector", sname, val)
            if cm == 0:
                em.op("gpsimd", lambda g: g.memset(xbc_sb[:, 0, 0:4], 0.0))
                em.op("gpsimd", lambda g: g.memset(xbc_sb[:, 1, 0:4], 0.0), inc=s_g)
            em.wait("scalar", s_p, pe_done[mt])
            ev = em.op("scalar", lambda e, esl=esl, pslot=pslot: e.activation(
                xbc_sb[:, esl, 4:4 + T], pslot[0:128, :], AF.Copy), inc=s_a)
            evac_done[mt] = (s_a, ev)
            if cm == 0:
                em.wait("vector", s_g)
            em.wait("vector", s_a, ev)
            for k in range(4):
                em.op("vector", lambda e, esl=esl, cm=cm, k=k: e.tensor_scalar(
                    tapp[:, esl, k, :], xbc_sb[:, esl, 1 + k:1 + k + T],
                    convw[:, k, cm:cm + 1], None, ALU.mult),
                    inc=s_v if k == 3 else None)
            v_pp = em.val(s_v)
            em.wait("gpsimd", s_v, v_pp)
            if cm >= 2:
                em.wait("gpsimd", s_a)
            v_ts = em.op("gpsimd", lambda g, esl=esl: g.tensor_add(
                tapp[:, esl, 0, :], tapp[:, esl, 0, :], tapp[:, esl, 1, :]), inc=s_g)
            em.op("vector", lambda e, esl=esl: e.tensor_add(
                tapp[:, esl, 2, :], tapp[:, esl, 2, :], tapp[:, esl, 3, :]),
                pre_drain=True)
            em.wait("vector", s_g, v_ts)
            tv = em.op("vector", lambda e, esl=esl: e.tensor_add(
                tapp[:, esl, 1, :], tapp[:, esl, 0, :], tapp[:, esl, 2, :]), inc=s_v,
                pre_drain=True)
            tap_done[mt] = (s_v, tv)
            if mt > 16:
                emit_silu(mt - 1)
            if mt == 33:
                emit_silu(33)
        else:
            em.wait("scalar", s_p, pe_done[mt])
            em.wait("scalar", s_h, em.val(s_h))
            v_dtevac_a = em.op("scalar", lambda e, pslot=pslot: e.activation(
                row16a[:, :], pslot[0:16, :], AF.Abs, bias=dtb_col[:, 0:1]), inc=s_a)
            em.wait("vector", s_p, pe_done[mt])
            v_dtevac_v = em.op("vector", lambda e, pslot=pslot: e.tensor_scalar(
                dt_fm[:, :], pslot[0:16, :], dtb_col[:, 0:1], 0.0, ALU.add, ALU.max), inc=s_v)
    for sem_name in ("s_z0", "s_z1", "s_x0", "s_x1"):
        em.wait("sync", sem_name)
    em.end("ph1_inproj")
    pclose()

    popen()
    em.begin()
    psTx = psp("psTx", [128, 16], F32)
    psTy = psp("psTy", [128, 16], F32)
    psTd = (psTx, psTy)
    em.op("scalar", lambda e: e.activation(row16b[:, :], row16a[:, :], AF.Exp, scale=-1.0),
          pre_drain=True)
    va3 = em.op("scalar", lambda e: e.activation(row16b[:, :], row16b[:, :], AF.Ln, bias=1.0),
                inc=s_a, pre_drain=True)
    em.wait("vector", s_a, va3)
    vdt = em.op("vector", lambda e: e.tensor_add(dt_fm[:, :], dt_fm[:, :], row16b[:, :]),
                inc=s_v, pre_drain=True)
    vna = em.op("scalar", lambda e: e.activation(negA[:, :], alog_col[:, :], AF.Exp), inc=s_a)
    em.wait("vector", s_a, vna)
    em.op("vector", lambda e: e.tensor_scalar(
        row16a[:, :], dt_fm[:, :], negA[:, 0:1], -1.0, ALU.mult, ALU.mult), pre_drain=True)
    vacs = em.op("vector", lambda e: e.tensor_tensor_scan(
        acs_fm[:, :], chunkmask[:, :], row16a[:, :], 0.0, ALU.mult, ALU.add), inc=s_v,
        pre_drain=True)
    em.wait("sync", s_v, vacs)
    em.dma("sync", bass.AP(rows_sp, 2 * T, [[CH, NH], [NH * CH, NCH], [1, CH]]),
           acs_fm[:, :].rearrange("p (c q) -> p c q", c=NCH), s_h)
    em.wait("scalar", s_v, vacs)
    for c in range(NCH):
        em.op("scalar", lambda e, c=c: e.activation(
            row16a[:, c * CH:(c + 1) * CH], acs_fm[:, c * CH:(c + 1) * CH],
            AF.Exp, scale=-1.0, bias=acs_fm[:, c * CH + CH - 1:c * CH + CH]))
    vcd = em.op("scalar", lambda e: e.activation(
        cd_fm[:, :], bass.AP(acs_fm, CH - 1, [[T, NH], [CH, NCH]]), AF.Exp), inc=s_a)
    em.wait("sync", s_a, vcd)
    v_cd_sp = em.dma("sync", bass.AP(rows_sp, 34 * T, [[8, NH], [1, NCH]]), cd_fm[:, :], s_h)
    em.wait("sync", s_h, v_cd_sp)
    em.dma("sync", cd_bc[:, :, :], bass.AP(rows_sp, 34 * T, [[0, 128], [8, NH], [1, NCH]]), s_h)
    tsrc = (dt_fm, acs_fm, row16a)
    tdst = (dt_tm, acs_tm, dec_tm)
    em.wait("tensor", s_v, vdt)
    em.wait("tensor", s_v, vacs)
    em.wait("tensor", s_a, vcd)
    tp_done = {}
    for idx in range(NTT * 3):
        tt, wch = idx // 3, idx % 3
        bk = idx % 2
        if idx >= 2:
            em.wait("tensor", s_v, tp_done[idx - 2])
        pv = em.op("tensor", lambda t, tt=tt, wch=wch, bk=bk: t.transpose(
            psTd[bk][:, :], tsrc[wch][:, tt * 128:(tt + 1) * 128],
            id16_f[:]), inc=s_p)
        em.wait("vector", s_p, pv)
        tp_done[idx] = em.op("vector", lambda e, tt=tt, wch=wch, bk=bk: e.tensor_copy(
            tdst[wch][:, tt, :], psTd[bk][:, :]), inc=s_v)
    em.wait("vector", s_v, tp_done[NTT * 3 - 1])
    v_ddc = em.op("vector", lambda v: v.tensor_mul(
        ddc_tm.ap().rearrange("p a b -> p (a b)"),
        dec_tm.ap().rearrange("p a b -> p (a b)"),
        dt_tm.ap().rearrange("p a b -> p (a b)")), inc=s_v, pre_drain=True)
    em.wait("scalar", s_v, v_ddc)
    v_ldt = em.op("scalar", lambda s: s.activation(
        ldt_tm.ap().rearrange("p a b -> p (a b)"),
        dt_tm.ap().rearrange("p a b -> p (a b)"), AF.Ln), inc=s_a)
    if "dtv" in dbg:
        em.wait("sync", s_v, vdt)
        em.dma("sync", dbg["dtv"][:, :], dt_fm[:, :], s_h)
    if "acs" in dbg:
        em.wait("sync", s_v, vacs)
        em.dma("sync", dbg["acs"][:, :], acs_fm[:, :], s_h)
    em.wait("sync", s_h, em.val(s_h))
    em.end("ph1b_prep")
    pclose()
    mctx.close()

    popen()
    em.begin()
    xcm = sbp("xcm", [128, 2, T], BF16)
    xstage = sbp("xstage", [128, 2, NTT, 128], BF16)
    psTq = [psp("psTq%d" % i, [128, 128], BF16) for i in range(4)]
    ld_done = {}
    tr_ev = {}
    st_done = {}
    for m in range(17):
        sl = m % 2
        sem_l = "s_tc%d" % sl
        if m >= 2:
            em.wait("sync", s_p, tr_ev[(m - 2, NTT - 1)][0])
        ld_done[m] = em.dma("sync", xcm[:, sl, :],
                            xc_sp[m * 128:(m + 1) * 128, :], sem_l)
        em.wait("tensor", sem_l, ld_done[m])
        for tt in range(NTT):
            gi = (m * NTT + tt)
            bk = gi % 4
            if gi >= 4:
                pk = ((gi - 4) // NTT, (gi - 4) % NTT)
                em.wait("tensor", tr_ev[pk][2], tr_ev[pk][1])
            pv = em.op("tensor", lambda t, sl=sl, tt=tt, bk=bk: t.transpose(
                psTq[bk][:, :], xcm[:, sl, tt * 128:(tt + 1) * 128], id128_bf[:]),
                inc=s_p)
            if gi % 2 == 0:
                em.wait("scalar", s_p, pv)
                if m >= 2 and tt <= 1:
                    em.wait("scalar", "s_ts%d" % sl)
                ev = em.op("scalar", lambda s, sl=sl, tt=tt, bk=bk: s.activation(
                    xstage[:, sl, tt, :], psTq[bk][:, :], AF.Copy), inc=s_a)
                tr_ev[(m, tt)] = (pv, ev, s_a)
            else:
                em.wait("vector", s_p, pv)
                if m >= 2 and tt <= 1:
                    em.wait("vector", "s_ts%d" % sl)
                ev = em.op("vector", lambda v, sl=sl, tt=tt, bk=bk: v.tensor_copy(
                    xstage[:, sl, tt, :], psTq[bk][:, :]), inc=s_v)
                tr_ev[(m, tt)] = (pv, ev, s_v)
        em.wait("scalar", tr_ev[(m, NTT - 2)][2], tr_ev[(m, NTT - 2)][1])
        em.wait("scalar", tr_ev[(m, NTT - 1)][2], tr_ev[(m, NTT - 1)][1])
        if m < 16:
            st_done[m] = em.dma("scalar",
                bass.AP(xt_sp, m * 128, [[DI, 128], [128 * DI, NTT], [1, 128]]),
                xstage[:, sl, :, :], "s_ts%d" % sl)
        else:
            st_done[m] = em.dma("scalar",
                bass.AP(bt_sp, 0, [[DS, 128], [128 * DS, NTT], [1, 128]]),
                xstage[:, sl, :, :], "s_ts%d" % sl)
    em.wait("sync", "s_ts0")
    em.wait("sync", "s_ts1")
    em.end("ph1c_transpose")
    pclose()

    popen()
    em.begin()
    xt_c = sbp("xt_c", [128, 2, 2, DI], BF16)
    bc_c = sbp("bc_c", [128, 2, 2, CH], BF16)
    bt_c = sbp("bt_c", [128, 2, 2, DS], BF16)
    sz_c = sbp("sz_c", [128, 2, NH, CH], BF16)
    csb_c = sbp("csb_c", [128, 2, NH, CH], F32)
    bd = sbp("bd", [128, 4, 2, 128], BF16)
    mcbt = sbp("mcbt", [128, 2, 2, CH], BF16)
    sc_f32 = sbp("sc_f32", [128, 2, 2, CH], F32)
    exps = sbp("exps", [128, 2, 2, CH], BF16)
    eabx = sbp("eabx", [128, 2, CH], BF16)
    scoresT = sbp("scoresT", [128, 4, 2, CH], BF16)
    cexp = sbp("cexp", [128, 4, CH], BF16)
    prevf = sbp("prevf", [128, NH * 128], F32)
    prevb = sbp("prevb", [128, NH * 128], BF16)
    NY2 = NH if "y2" in dbg else 4
    y2b = sbp("y2b", [128, NY2, CH], BF16)
    y3b = sbp("y3b", [128, 2, NH, CH], BF16)
    sqb = sbp("sqb", [128, NH, CH], BF16)
    rstd_tmp = sbp("rstd_tmp", [128, 1], F32)
    wd1buf = sbp("wd1buf", [128, 2, NKT, 256], BF16)
    wd2buf = sbp("wd2buf", [128, 2, 2, DM], BF16)
    h_sb = sbp("h_sb", [128, 2, T], BF16)
    sqscr = sbp("sqscr", [128, 1024], BF16)
    rsub_sb = sbp("rsub_sb", [128, 2, 512], BF16)
    sqacc = sbp("sqacc", [128, 128], F32)
    onescol_f = sbp("onescol_f", [128, 1], F32)
    loss_row = sbp("loss_row", [1, 128], F32)
    losses8 = sbp("losses8", [1, 8], F32)
    minv = sbp("minv", [1, 1], F32)
    maskr = sbp("maskr", [1, 8], F32)
    psYS = psp("psYS", [128, 1024], F32)
    psCBT = psp("psCBT", [128, 512], F32)
    psH2 = psp("psH2", [128, 1024], F32)
    psR2 = psp("psR2", [128, 2, 512], F32)

    em.op("gpsimd", lambda e: e.memset(prevf[:], 0.0))
    v_prevmem = em.op("gpsimd", lambda e: e.memset(prevb[:], 0.0), inc=s_g)
    v_of = em.op("gpsimd", lambda e: e.memset(onescol_f[:], 1.0), inc=s_g)
    v_negid = em.op("gpsimd", lambda g: g.tensor_scalar(
        negid[:, :], id128_bf[:, :], -1.0, None, ALU.mult), inc=s_g)

    def chload(c):
        sl = c % 2
        sem = "s_c%d" % sl
        em.dma("sync", xt_c[:, sl, :, :],
               bass.AP(xt_sp, c * CH * DI, [[DI, 128], [128 * DI, 2], [1, DI]]), sem)
        em.dma("sync", bt_c[:, sl, :, :],
               bass.AP(bt_sp, c * CH * DS, [[DS, 128], [128 * DS, 2], [1, DS]]), sem)
        em.dma("sync", bc_c[:, sl, :, :],
               bass.AP(xc_sp, 16 * 128 * T + c * CH, [[T, 128], [128 * T, 2], [1, CH]]),
               sem)
        em.dma("sync", sz_c[:, sl, :, :],
               bass.AP(z_sp, c * CH, [[T, 128], [128 * T, NH], [1, CH]]), sem)
        return em.dma("sync", csb_c[:, sl, :, :],
                      bass.AP(rows_sp, 2 * T + c * NH * CH, [[0, 128], [1, NH * CH]]),
                      sem)

    def wdload(e):
        sl = e % 2
        em.dma("gpsimd", wd1buf[:, sl, :, :],
               bass.AP(Wd1, e * DM * 256, [[256, 128], [128 * 256, NKT], [1, 256]]),
               "s_d1%d" % sl)
        return em.dma("gpsimd", wd2buf[:, sl, :, :],
                      bass.AP(Wd2, e * 256 * DM, [[DM, 128], [128 * DM, 2], [1, DM]]),
                      "s_d2%d" % sl)

    ch_ld = {}
    ch_ld[0] = chload(0)
    ch_ld[1] = chload(1)
    wdv = {0: wdload(0), 1: wdload(1)}

    y2ev = {}
    scan_done = {}
    cast_done = {}
    mm_done = {}
    sqv = {}
    y3v = {}
    scorem = {}
    cexpv = {}
    bdv = {}
    lastexp = []
    lastscorem = []
    v_rstd = {}
    hev3 = {}
    hro = {}
    rro = {}
    rro_sub = {}
    sqv3 = {}
    sqpair = {}

    def ffn_h_round(e, r):
        mtH, th = r % 2, r // 2
        sl = e % 2
        em.wait("tensor", "s_d1%d" % sl, wdv[e])
        if (e, r) != (0, 0):
            pk = (e, r - 1) if r else (e - 1, 3)
            em.wait("tensor", s_a, hev3[pk])
        if e >= 1 and r == 0:
            em.wait("tensor", s_p, rro[(e - 1, 31)])

        def _hmm(t, sl=sl, mtH=mtH, th=th):
            last = None
            for kt in range(NKT):
                for nt in range(2):
                    last = t.matmul(psH2[:, nt * 512:(nt + 1) * 512],
                                    wd1buf[:, sl, kt, mtH * 128:(mtH + 1) * 128],
                                    xn_fm[:, kt, th * 1024 + nt * 512:th * 1024 + (nt + 1) * 512],
                                    start=(kt == 0), stop=(kt == NKT - 1))
            return last
        hro[(e, r)] = em.op("tensor", _hmm, inc=s_p)
        em.wait("scalar", s_p, hro[(e, r)])
        hev3[(e, r)] = em.op("scalar", lambda s_, e=e, mtH=mtH, th=th: s_.activation(
            h_sb[:, mtH, th * 1024:(th + 1) * 1024], psH2[:, :], AF.Relu,
            bias=bd1_cs[:, mtH, e:e + 1]), inc=s_a)

    def ffn_r_round(e, ri):
        mtD, q = ri // 4, ri % 4
        th = q // 2
        gk = e * 32 + ri
        rsl = gk % 2
        sl = e % 2
        em.wait("tensor", "s_d2%d" % sl, wdv[e])
        em.wait("tensor", s_a, hev3[(e, 2 * th + 1)])
        if gk == 0:
            em.wait("tensor", s_g, v_negid)
        if gk >= 2:
            em.wait("tensor", s_a, sqpair[(gk - 2) // 2])

        def _rmm(t, sl=sl, mtD=mtD, q=q, rsl=rsl):
            t.matmul(psR2[:, rsl, :], wd2buf[:, sl, 0, mtD * 128:(mtD + 1) * 128],
                     h_sb[:, 0, q * 512:(q + 1) * 512], start=True, stop=False)
            t.matmul(psR2[:, rsl, :], wd2buf[:, sl, 1, mtD * 128:(mtD + 1) * 128],
                     h_sb[:, 1, q * 512:(q + 1) * 512], start=False, stop=False)
            return t.matmul(psR2[:, rsl, :], negid[:, :],
                            xn_fm[:, mtD, q * 512:(q + 1) * 512], start=False, stop=True)
        rro[(e, ri)] = em.op("tensor", _rmm, inc=s_p)
        rro_sub[(e, ri)] = rro[(e, ri)]
        if gk % 2 == 1:
            em.wait("scalar", s_p, rro[(e, ri)])
            if gk >= 3:
                em.wait("scalar", s_a, sqpair[(gk - 2) // 2])
            col = e * 16 + mtD * 2 + q // 2
            sqpair[gk // 2] = em.op("scalar", lambda s, e=e, mtD=mtD, col=col:
                s.activation(sqscr[:, :], psR2[:, :, :], AF.Square,
                             bias=bd2_cs[:, mtD, e:e + 1],
                             accum_out=sqacc[:, col:col + 1]), inc=s_a)
            sqv3[(e, ri)] = sqpair[gk // 2]

    def ffn_step(e, h):
        if h < 4:
            ffn_h_round(e, h)
        else:
            i0 = 3 * (h - 4) if h < 12 else 24 + 2 * (h - 12)
            n = 3 if h < 12 else 2
            for ri in range(i0, i0 + n):
                ffn_r_round(e, ri)

    def emit_consumers(c, h, sl):
        gi = c * NH + h
        em.wait("vector", s_p, mm_done[(c, h)])
        scan_done[(c, h)] = em.op("vector", lambda e, h=h, c=c, ys=(h % 2) * 512:
            e.scalar_tensor_tensor(prevf[:, h * 128:(h + 1) * 128],
                                   prevf[:, h * 128:(h + 1) * 128],
                                   cd_bc[:, h, c:c + 1], psYS[:, ys + 256:ys + 384],
                                   ALU.mult, ALU.add), inc=s_v)
        em.wait("gpsimd", s_v, scan_done[(c, h)])
        cast_done[(c, h)] = em.op("gpsimd", lambda e, h=h: e.tensor_copy(
            prevb[:, h * 128:(h + 1) * 128], prevf[:, h * 128:(h + 1) * 128]), inc=s_g)
        if gi >= NY2:
            pk4 = (gi - NY2)
            em.wait("vector", s_a, sqv[(pk4 // NH, pk4 % NH)])
            em.wait("vector", s_g, y3v[(pk4 // NH, pk4 % NH)])
        y2ev[(c, h)] = em.op("vector", lambda e, sl=sl, h=h, hy=h % NY2, ys=(h % 2) * 512:
            e.tensor_mul(y2b[:, hy, :], psYS[:, ys:ys + CH], sz_c[:, sl, h, :]), inc=s_v)
        em.wait("scalar", s_v, y2ev[(c, h)])
        sqv[(c, h)] = em.op("scalar", lambda e, h=h, hy=h % NY2: e.activation(
            sqb[:, h, :], y2b[:, hy, :], AF.Square), inc=s_a)
        em.wait("gpsimd", s_v, y2ev[(c, h)])
        if h == 0 and c >= 2:
            em.wait("gpsimd", "s_y3")
        y3v[(c, h)] = em.op("gpsimd", lambda e, h=h, hy=h % NY2, ysl=c % 2:
            e.tensor_tensor(y3b[:, ysl, h, :], y2b[:, hy, :],
                            nwcol[:, h:h + 1].broadcast_to([128, CH]), ALU.mult),
            inc=s_g)

    def produce(c, h):
        sl = c % 2
        sem_c = "s_c%d" % sl
        hs = h % 4
        hs2 = h % 2
        gi = c * NH + h
        if h == 0:
            em.wait("vector", sem_c, ch_ld[c])
            em.wait("gpsimd", sem_c, ch_ld[c])
            em.wait("scalar", sem_c, ch_ld[c])
        if gi >= 4:
            pk = gi - 4
            em.wait("vector", s_p, mm_done[(pk // NH, pk % NH)])
            em.wait("gpsimd", s_p, mm_done[(pk // NH, pk % NH)])

        def _bd(v, sl=sl, h=h, hs=hs, c=c):
            v.tensor_scalar(bd[:, hs, 0, :], bt_c[:, sl, 0, :],
                            ddc_tm[:, 2 * c, h:h + 1], None, ALU.mult)
            return v.tensor_scalar(bd[:, hs, 1, :], bt_c[:, sl, 1, :],
                                   ddc_tm[:, 2 * c + 1, h:h + 1], None, ALU.mult)
        bdv[(c, h)] = em.op("vector", _bd, inc=s_v)
        if gi >= 2:
            pk2 = gi - 2
            em.wait("scalar", s_g, cexpv[(pk2 // NH, pk2 % NH)])
        v_eabx = em.op("scalar", lambda s_, sl=sl, h=h, hs2=hs2: s_.activation(
            eabx[:, hs2, :], csb_c[:, sl, h, :], AF.Exp), inc=s_a)
        em.wait("gpsimd", s_a, v_eabx)
        cexpv[(c, h)] = em.op("gpsimd", lambda e_, sl=sl, h=h, hs=hs, hs2=hs2:
            e_.tensor_mul(cexp[:, hs, :], eabx[:, hs2, :], bc_c[:, sl, 1, :]),
            inc=s_g)
        if len(lastexp) >= 2:
            em.wait("vector", s_a, lastexp[-2])
        em.op("vector", lambda e_, sl=sl, h=h, c=c, hs2=hs2: e_.tensor_scalar(
            sc_f32[:, hs2, 0, :], csb_c[:, sl, h, :], acs_tm[:, 2 * c, h:h + 1], 0.0,
            ALU.subtract, ALU.min))
        v_diff = em.op("vector", lambda e_, sl=sl, h=h, c=c, hs2=hs2: e_.tensor_scalar(
            sc_f32[:, hs2, 1, :], csb_c[:, sl, h, :], acs_tm[:, 2 * c + 1, h:h + 1],
            0.0, ALU.subtract, ALU.min), inc=s_v)
        em.wait("scalar", s_v, v_diff)
        if len(lastscorem) >= 2:
            em.wait("scalar", s_g, lastscorem[-2])
        if c == 0 and h == 0:
            em.wait("scalar", s_a, v_ldt)
        em.op("scalar", lambda e_, hs2=hs2, h=h, c=c: e_.activation(
            exps[:, hs2, 0, :], sc_f32[:, hs2, 0, :], AF.Exp,
            bias=ldt_tm[:, 2 * c, h:h + 1]))
        v_exp = em.op("scalar", lambda e_, hs2=hs2, h=h, c=c: e_.activation(
            exps[:, hs2, 1, :], sc_f32[:, hs2, 1, :], AF.Exp,
            bias=ldt_tm[:, 2 * c + 1, h:h + 1]), inc=s_a)
        lastexp.append(v_exp)
        em.wait("gpsimd", s_a, v_exp)
        if h == 0:
            em.wait("gpsimd", s_v, cbt_mult[c])
        scorem[(c, h)] = em.op("gpsimd", lambda e_, hs=hs, hs2=hs2, msl=c % 2:
            e_.tensor_mul(scoresT[:, hs, :, :], exps[:, hs2, :, :], mcbt[:, msl, :, :]),
            inc=s_g)
        lastscorem.append(scorem[(c, h)])

    def emit_cbt(c):
        sl = c % 2
        sem_c = "s_c%d" % sl
        em.wait("tensor", sem_c, ch_ld[c])
        if c >= 2:
            em.wait("vector", s_g, scorem[(c - 2, NH - 1)])
        for st in range(2):
            if c >= 1 or st == 1:
                em.wait("tensor", s_v, cbt_mult[c] if st == 1 else cbt_mult[c - 1])

            def _cbt1(t, sl=sl, st=st):
                return t.matmul(psCBT[:, 0:CH], bc_c[:, sl, 0, st * 128:(st + 1) * 128],
                                bc_c[:, sl, 1, :], start=True, stop=True)
            v_cbt = em.op("tensor", _cbt1, inc=s_p)
            em.wait("vector", s_p, v_cbt)
            cbt_mult[c] = em.op("vector", lambda e_, st=st, msl=sl: e_.tensor_mul(
                mcbt[:, msl, st, :], psCBT[:, 0:CH], binmask[:, st, :]), inc=s_v)

    def emit_mms(c, h):
        sl = c % 2
        sem_c = "s_c%d" % sl
        hs = h % 4
        gi = c * NH + h
        em.wait("tensor", s_g, scorem[(c, h)])
        em.wait("tensor", s_g, cexpv[(c, h)])
        if c >= 1:
            em.wait("tensor", s_g, cast_done[(c - 1, h)])
        if gi >= 2:
            pk = gi - 2
            em.wait("tensor", s_v, y2ev[(pk // NH, pk % NH)])
            em.wait("tensor", s_v, scan_done[(pk // NH, pk % NH)])
        if h == 0:
            em.wait("tensor", sem_c, ch_ld[c])
            em.wait("tensor", s_v, v_diagd)
            em.wait("tensor", s_v, v_ddc)
        if h == 1 and c >= 1:
            em.wait("tensor", s_a, v_rstd[c - 1])
        if h == 0 and c == 0:
            em.wait("tensor", s_g, v_prevmem)

        def _hmms(t, sl=sl, h=h, hs=hs, ys=(h % 2) * 512):
            t.matmul(psYS[:, ys:ys + CH], xt_c[:, sl, 0, h * 128:(h + 1) * 128],
                     scoresT[:, hs, 0, :], start=True, stop=False)
            t.matmul(psYS[:, ys:ys + 128], xt_c[:, sl, 0, h * 128:(h + 1) * 128],
                     diagD[:, h, :], start=False, stop=False)
            t.matmul(psYS[:, ys + 256:ys + 384], bd[:, hs, 0, :],
                     xt_c[:, sl, 0, h * 128:(h + 1) * 128], start=False, stop=False)
            t.matmul(psYS[:, ys:ys + CH], xt_c[:, sl, 1, h * 128:(h + 1) * 128],
                     scoresT[:, hs, 1, :], start=False, stop=False)
            t.matmul(psYS[:, ys + 128:ys + CH], xt_c[:, sl, 1, h * 128:(h + 1) * 128],
                     diagD[:, h, :], start=False, stop=False)
            t.matmul(psYS[:, ys + 256:ys + 384], bd[:, hs, 1, :],
                     xt_c[:, sl, 1, h * 128:(h + 1) * 128], start=False, stop=False)
            return t.matmul(psYS[:, ys:ys + CH], prevb[:, h * 128:(h + 1) * 128],
                            cexp[:, hs, :], start=False, stop=True)
        mm_done[(c, h)] = em.op("tensor", _hmms, inc=s_p)

    def chunk_end(c):
        em.wait("tensor", s_a, sqv[(c, NH - 1)])
        em.wait("tensor", s_v, y2ev[(c, NH - 1)])
        em.wait("tensor", s_v, scan_done[(c, NH - 1)])
        for t2 in range(2):
            if c + t2 >= 1:
                em.wait("tensor", s_a, v_rstd.get(c - 1) if t2 == 0 else rstd0_done)

            def _rms(t, t2=t2):
                last = None
                for h in range(NH):
                    last = t.matmul(psYS[:, 896 + t2:897 + t2],
                                    sqb[:, h, t2 * 128:(t2 + 1) * 128],
                                    onescol_bf[:], start=(h == 0), stop=(h == NH - 1))
                return last
            v_rmsmm = em.op("tensor", _rms, inc=s_p)
            em.wait("scalar", s_p, v_rmsmm)
            em.op("scalar", lambda e_, t2=t2: e_.activation(
                rstd_tmp[:, 0:1], psYS[:, 896 + t2:897 + t2], AF.Ln,
                scale=1.0 / DI, bias=EPS))
            vv = em.op("scalar", lambda e_, t2=t2, c=c: e_.activation(
                rstd_all[:, c * 2 + t2:c * 2 + t2 + 1], rstd_tmp[:, 0:1], AF.Exp,
                scale=-0.5), inc=s_a, pre_drain=True)
            if t2 == 0:
                rstd0_done = vv
            else:
                v_rstd[c] = vv
        em.wait("sync", s_g, y3v[(c, NH - 1)])
        em.dma("sync", bass.AP(y3_sp, c * CH, [[T, 128], [128 * T, NH], [1, CH]]),
               y3b[:, c % 2, :, :], "s_y3")
        if "y2" in dbg:
            em.wait("sync", s_v, y2ev[(c, NH - 1)])
            em.dma("sync", bass.AP(dbg["y2"], c * CH, [[T, 128], [128 * T, NH], [1, CH]]),
                   y2b[:, :, :], "s_y2d")
        if c + 2 < NCH:
            em.wait("sync", s_p, mm_done[(c, NH - 1)])
            em.wait("sync", s_v, em.val(s_v))
            em.wait("sync", s_g, em.val(s_g))
            ch_ld[c + 2] = chload(c + 2)
        if c + 2 < 8:
            em.wait("gpsimd", s_p, rro[(c, 31)])
            wdv[c + 2] = wdload(c + 2)

    cbt_mult = {}
    rstd0_done = None
    emit_cbt(0)
    NGI = NCH * NH
    for gi in range(NGI + 2):
        if gi < NGI:
            if gi % NH == 13 and gi // NH + 1 < NCH:
                emit_cbt(gi // NH + 1)
            produce(gi // NH, gi % NH)
        gim = gi - 2
        if gim >= 0:
            c_m, h_m = divmod(gim, NH)
            emit_mms(c_m, h_m)
            if gim >= 1 and (gim - 1) % NH != NH - 1:
                pc, ph = divmod(gim - 1, NH)
                emit_consumers(pc, ph, pc % 2)
            ffn_step(c_m, h_m)
            if h_m == NH - 1:
                emit_consumers(c_m, h_m, c_m % 2)
                chunk_end(c_m)

    em.wait("tensor", s_a, sqv3[(7, 31)])
    em.wait("tensor", s_a, hev3[(7, 3)])
    em.wait("tensor", s_g, v_of)
    v_lmm = em.op("tensor", lambda t: t.matmul(psH2[0:1, 0:128], onescol_f[:],
                                               sqacc[:, :], start=True, stop=True),
                  inc=s_p)
    em.wait("vector", s_p, v_lmm)
    em.op("vector", lambda v: v.tensor_copy(loss_row[:], psH2[0:1, 0:128]))
    em.op("vector", lambda v: v.tensor_reduce(
        losses8[:], loss_row.ap().rearrange("p (a b) -> p a b", a=8), AX.X, ALU.add),
        pre_drain=True)
    em.op("vector", lambda v: v.tensor_reduce(minv[:], losses8[:], AX.X, ALU.min),
          pre_drain=True)
    em.op("vector", lambda v: v.tensor_scalar(maskr[:], losses8[:], minv[0:1, 0:1], None,
                                              ALU.subtract), pre_drain=True)
    em.op("vector", lambda v: v.tensor_scalar(maskr[:], maskr[:], -1.0, 1.0,
                                              ALU.mult, ALU.add), pre_drain=True)
    v_mask = em.op("vector", lambda v: v.tensor_scalar_max(maskr[:], maskr[:], 0.0), inc=s_v,
                   pre_drain=True)
    em.wait("sync", s_v, v_mask)
    v_mask_sp = em.dma("sync", rows_sp[35:36, 0:8], maskr[:], s_h)
    em.wait("sync", s_h, v_mask_sp)
    em.wait("sync", "s_y3")
    em.end("ph2_ssm")
    pclose()

    popen()
    em.begin()
    y3h = sbp("y3h", [128, NJT, 1024], BF16)
    wout3 = sbp("wout3", [128, NJT, DM], BF16)
    wadn = sbp("wadn", [128, NKT, 8, 64], BF16)
    waup = sbp("waup", [64, 8, DM], BF16)
    baup_bf = sbp("baup_bf", [8, DM], BF16)
    mask_bc = sbp("mask_bc", [128, 8], F32)
    mask_col = sbp("mask_col", [8, 1], BF16)
    wadn_sel = sbp("wadn_sel", [128, NKT, 64], BF16)
    waup_sel = sbp("waup_sel", [64, DM], BF16)
    badn_col = sbp("badn_col", [64, 1], F32)
    badn_in = sbp("badn_in", [64, 8], F32)
    bsum = sbp("bsum", [1, DM], BF16)
    bbase_bf = sbp("bbase_bf", [1, DM], BF16)
    a_fm = sbp("a_fm", [64, T], BF16)
    wbase_sb = sbp("wbase_sb", [128, NKT, DM], BF16)
    g_rot = sbp("g_rot", [128, 2, DM], BF16)
    s_rot = sbp("s_rot", [128, 4, DM], BF16)
    u_rot = sbp("u_rot", [128, 2, DM], F32)
    o_rot = sbp("o_rot", [128, 2, DM], F32)
    psG = psp("psG", [128, 2, DM], F32)
    psA2 = psp("psA2", [64, 512], F32)
    psO = psp("psO", [128, 1024], F32)

    em.wait("sync", "s_wc")
    v_wo3 = em.dma("sync", wout3[:, :, :],
                   bass.AP(wout_bf, 0, [[DM, 128], [128 * DM, NJT], [1, DM]]), s_h)
    v_y3l = em.dma("sync", y3h[:, :, :],
                   bass.AP(y3_sp, 0, [[T, 128], [128 * T, NJT], [1, 1024]]), "s_yl")

    em.wait("sync", "s_wc")
    em.wait("scalar", "s_wc")
    v_wb = em.dma("scalar", wbase_sb[:, :, :],
                  bass.AP(wbase_bfd, 0, [[DM, 128], [128 * DM, NKT], [1, DM]]), "s_ts0")
    em.dma("scalar", wadn[:, :, :, :],
           bass.AP(wadn_bfd, 0, [[512, 128], [128 * 512, NKT], [64, 8], [1, 64]]),
           "s_ts0")
    em.dma("scalar", waup[:, :, :],
           bass.AP(waup_bfd, 0, [[8 * DM, 64], [DM, 8], [1, DM]]), "s_ts0")
    v_baup = em.dma("scalar", baup_bf[:, :], baup_bfd.ap(), "s_ts0")
    v_mbc = em.dma("sync", mask_bc[:, :], bass.AP(rows_sp, 35 * T, [[0, 128], [1, 8]]), s_h)
    v_mcol = em.dma("gpsimd", mask_col[:, :], bass.AP(rows_sp, 35 * T, [[1, 8], [1, 1]]), s_d)
    em.dma("sync", badn_in[:, :], ba_down_c[:, :], s_h)
    v_bb = em.dma("gpsimd", bbase_bf[:, :], b_base[:, :], s_d)

    sevb = {}

    def emit_s(tt):
        em.wait("tensor", "s_yl", 16 if tt < 8 else 32)
        em.wait("tensor", s_h)
        if tt >= 1:
            em.wait("tensor", s_v, sevb[tt - 1])
        if tt == 2:
            em.wait("tensor", s_v, v_bsum)

        def _omm(t, tq=tt % 8):
            last = None
            for jt in range(NJT):
                for dc2 in range(2):
                    last = t.matmul(psO[:, dc2 * 512:(dc2 + 1) * 512],
                                    y3h[:, jt, tq * 128:(tq + 1) * 128],
                                    wout3[:, jt, dc2 * 512:(dc2 + 1) * 512],
                                    start=(jt == 0), stop=(jt == NJT - 1))
            return last
        v_om = em.op("tensor", _omm, inc=s_p)
        em.wait("vector", s_p, v_om)
        sevb[tt] = em.op("vector", lambda v, tt=tt: v.tensor_scalar(
            s_rot[:, tt % 4, :], psO[:, :], rstd_all[:, tt:tt + 1], None, ALU.mult),
            inc=s_v)
        if "s" in dbg:
            em.wait("sync", s_v, sevb[tt])
            em.dma("sync", dbg["s"][tt * 128:(tt + 1) * 128, :], s_rot[:, tt % 4, :],
                   "s_y2d")
        if tt == 7:
            em.wait("sync", s_p, v_om)
            em.dma("sync", y3h[:, :, :],
                   bass.AP(y3_sp, 1024, [[T, 128], [128 * T, NJT], [1, 1024]]), "s_yl")

    emit_s(0)
    emit_s(1)
    bumm_emitted = [False]
    em.wait("vector", s_d, em.val(s_d))
    em.wait("vector", "s_ts0")
    em.wait("vector", s_h, em.val(s_h))
    em.op("vector", lambda v: v.tensor_scalar(
        wadn_sel[:, :, :], wadn[:, :, 0, :], mask_bc[:, 0:1], None, ALU.mult))
    for e in range(1, 8):
        v_wsel = em.op("vector", lambda v, e=e: v.scalar_tensor_tensor(
            wadn_sel[:, :, :], wadn[:, :, e, :], mask_bc[:, e:e + 1],
            wadn_sel[:, :, :], ALU.mult, ALU.add),
            inc=s_v if e == 7 else None, pre_drain=True)
    em.op("vector", lambda v: v.tensor_scalar(
        waup_sel[:, :], waup[:, 0, :], mask_bc[0:64, 0:1], None, ALU.mult))
    for e in range(1, 8):
        em.op("vector", lambda v, e=e: v.scalar_tensor_tensor(
            waup_sel[:, :], waup[:, e, :], mask_bc[0:64, e:e + 1],
            waup_sel[:, :], ALU.mult, ALU.add), pre_drain=True)
    em.op("vector", lambda v: v.tensor_scalar(
        badn_col[:, :], badn_in[:, 0:1], mask_bc[0:64, 0:1], None, ALU.mult))
    for e in range(1, 8):
        v_gsel = em.op("vector", lambda v, e=e: v.scalar_tensor_tensor(
            badn_col[:, :], badn_in[:, e:e + 1], mask_bc[0:64, e:e + 1],
            badn_col[:, :], ALU.mult, ALU.add), inc=s_v if e == 7 else None,
            pre_drain=True)
    em.wait("tensor", s_d)
    em.wait("tensor", "s_ts0")
    em.wait("tensor", s_v, sevb[1])
    def _bumm(t):
        t.matmul(psO[0:1, 0:512], mask_col[:, :], baup_bf[:, 0:512], start=True, stop=True)
        return t.matmul(psO[0:1, 512:1024], mask_col[:, :], baup_bf[:, 512:1024],
                        start=True, stop=True)
    v_bu = em.op("tensor", _bumm, inc=s_p)
    em.wait("vector", s_p, v_bu)
    em.wait("vector", s_d, em.val(s_d))
    v_bsum = em.op("vector", lambda v: v.tensor_add(bsum[:], psO[0:1, :], bbase_bf[:]),
                   inc=s_v)

    for nt in range(4):
        em.wait("tensor", s_v, v_wsel)

        def _amm(t, nt=nt):
            last = None
            for kt in range(NKT):
                last = t.matmul(psA2[:, :], wadn_sel[:, kt, :],
                                xn_fm[:, kt, nt * 512:(nt + 1) * 512],
                                start=(kt == 0), stop=(kt == NKT - 1))
            return last
        if nt >= 1:
            em.wait("tensor", s_v, v_aev)
        v_am = em.op("tensor", _amm, inc=s_p)
        em.wait("vector", s_p, v_am)
        em.wait("vector", s_v, v_gsel)
        v_aev = em.op("vector", lambda v, nt=nt: v.tensor_scalar(
            a_fm[:, nt * 512:(nt + 1) * 512], psA2[:, :], badn_col[:, 0:1], 0.0,
            ALU.add, ALU.max), inc=s_v)

    vo_done = [None, None]
    gst = {}
    ldv = {}

    def _fld(tt):
        slt = tt % 2
        su = "s_fu%d" % slt
        if vo_done[slt] is not None:
            em.wait("sync", s_v, vo_done[slt])
        ldv[tt] = em.dma("sync", u_rot[:, slt, :], u[tt * 128:(tt + 1) * 128, :], su)
    _fld(0)
    for tt in range(NTT):
        slt = tt % 2
        su, so = "s_fu%d" % slt, "s_fo%d" % slt
        if tt + 1 < NTT:
            _fld(tt + 1)
        v_s = ldv[tt]
        if tt + 2 < NTT:
            emit_s(tt + 2)
        em.wait("tensor", s_d)
        em.wait("tensor", s_v, v_aev)
        em.wait("tensor", s_v, v_bsum)
        if tt >= 2:
            em.wait("tensor", s_a, gst[tt - 2])

        def _gmm(t, tt=tt, slt=slt):
            last = None
            for oc in range(2):
                for kt in range(NKT):
                    t.matmul(psG[:, slt, oc * 512:(oc + 1) * 512],
                             xn_fm[:, kt, tt * 128:(tt + 1) * 128],
                             wbase_sb[:, kt, oc * 512:(oc + 1) * 512],
                             start=(kt == 0), stop=False)
                t.matmul(psG[:, slt, oc * 512:(oc + 1) * 512],
                         a_fm[:, tt * 128:(tt + 1) * 128],
                         waup_sel[:, oc * 512:(oc + 1) * 512], start=False, stop=False)
                last = t.matmul(psG[:, slt, oc * 512:(oc + 1) * 512],
                                onesrow_bf[:, :], bsum[0:1, oc * 512:(oc + 1) * 512],
                                start=False, stop=True)
            return last
        v_gm = em.op("tensor", _gmm, inc=s_p)
        em.wait("scalar", s_p, v_gm)
        if tt >= 2:
            em.wait("scalar", s_v, vo_done[slt])
        if "g" in dbg and tt >= 2:
            em.wait("scalar", so)
        gst[tt] = em.op("scalar", lambda s, slt=slt: s.activation(
            g_rot[:, slt, :], psG[:, slt, :], AF.Silu), inc=s_a)
        em.wait("vector", s_a, gst[tt])
        em.wait("vector", su, v_s)
        if tt >= 2:
            em.wait("vector", so)
        em.op("vector", lambda v, slt=slt, tt=tt: v.tensor_mul(
            o_rot[:, slt, :], s_rot[:, tt % 4, :], g_rot[:, slt, :]), pre_drain=True)
        vo_done[slt] = em.op("vector", lambda v, slt=slt: v.tensor_add(
            o_rot[:, slt, :], o_rot[:, slt, :], u_rot[:, slt, :]), inc=s_v,
            pre_drain=True)
        em.wait("sync", s_v, vo_done[slt])
        em.dma("sync", out[tt * 128:(tt + 1) * 128, :], o_rot[:, slt, :], so)
        if "g" in dbg:
            em.dma("sync", dbg["g"][tt * 128:(tt + 1) * 128, :], g_rot[:, slt, :], so)
    em.wait("sync", "s_fo0")
    em.wait("sync", "s_fo1")
    em.end("ph3b_adapter")
    pclose()

    ctx.close()
    return nc


def _stage_inputs(inputs):
    asC = np.ascontiguousarray
    W_inT = asC(inputs["L0_attn_W_in"].T)
    W_outT = asC(inputs["L0_attn_W_out"].T)
    conv_w_c = asC(inputs["L0_attn_conv_w"].reshape(18, 128, DCONV).transpose(1, 2, 0))
    conv_b_c = asC(inputs["L0_attn_conv_b"].reshape(18, 128).T)
    dt_bias = asC(inputs["L0_attn_dt_bias"].reshape(NH, 1))
    A_log = asC(inputs["L0_attn_A_log"].reshape(NH, 1))
    Dvec = asC(inputs["L0_attn_D"].reshape(1, NH))
    nw_c = asC(inputs["L0_attn_norm_weight"].reshape(NJT, 128).T)
    b_base = asC(inputs["L0_ffn_b_base"].reshape(1, DM))
    bd1_c = asC(inputs["L0_ffn_bd1"].reshape(8, 2, 128).transpose(2, 1, 0))
    bd2_c = asC(inputs["L0_ffn_bd2"].reshape(8, 8, 128).transpose(2, 1, 0))
    Wa_down_r = asC(inputs["L0_ffn_Wa_down"].transpose(1, 0, 2))
    Wa_up_r = asC(inputs["L0_ffn_Wa_up"].transpose(1, 0, 2))
    ba_down_c = asC(inputs["L0_ffn_ba_down"].T)
    ba_up = asC(inputs["L0_ffn_ba_up"])
    shared = dict(
        W_inT=W_inT, W_outT=W_outT, conv_w_c=conv_w_c, conv_b_c=conv_b_c,
        dt_bias=dt_bias, A_log=A_log, Dvec=Dvec, nw_c=nw_c,
        W_base=asC(inputs["L0_ffn_W_base"]), b_base=b_base,
        Wd1=asC(inputs["L0_ffn_Wd1"]), Wd2=asC(inputs["L0_ffn_Wd2"]),
        bd1_c=bd1_c, bd2_c=bd2_c, Wa_down_r=Wa_down_r, Wa_up_r=Wa_up_r,
        ba_down_c=ba_down_c, ba_up=ba_up,
    )
    in_maps = []
    for i in range(8):
        ui = asC(inputs["L0_attn_u"][i])
        m = dict(shared)
        m["u"] = ui
        m["uT"] = asC(ui.T)
        in_maps.append(m)
    return in_maps


def kernel(**inputs):
    in_maps = _stage_inputs(inputs)
    nc = build_program()
    res = run_bass_kernel_spmd(nc, in_maps, core_ids=list(range(8)))
    return np.stack([r["out"] for r in res.results], axis=0)


if __name__ == "__main__":
    pass


# version 26 (final)
# speedup vs baseline: 1.0578x; 1.0578x over previous
import numpy as np
from contextlib import ExitStack

import concourse.bass as bass
import concourse.mybir as mybir
from concourse.bass_utils import run_bass_kernel_spmd

F32 = mybir.dt.float32
BF16 = mybir.dt.bfloat16
AF = mybir.ActivationFunctionType
ALU = mybir.AluOpType
AX = mybir.AxisListType

T = 2048
DM = 1024
DI = 2048
DS = 128
NH = 16
HD = 128
DCONV = 4
CH = 256
NCH = T // CH
CONVD = DI + 2 * DS
DPROJ = 2 * DI + 2 * DS + NH
NMT = 35
EPS = 1e-5
NKT = DM // 128
NTT = T // 128
NJT = DI // 128


class Emitter:

    ENGINES = ("tensor", "vector", "scalar", "gpsimd", "sync")

    def __init__(self, nc, ctx):
        self.nc = nc
        self.ctx = ctx
        self.sems = {}
        self.q = None
        self._opened = False

    def sem(self, name):
        if name not in self.sems:
            h = self.ctx.enter_context(self.nc.semaphore(name))
            self.sems[name] = [h, 0]
        return self.sems[name]

    def plan_inc(self, name, k=1):
        s = self.sem(name)
        s[1] += k
        return s[1]

    def val(self, name):
        return self.sem(name)[1]

    def begin(self):
        assert self.q is None
        self.q = {e: [] for e in self.ENGINES}
        self._since_drain = {e: 0 for e in self.ENGINES}

    def end(self, name=None):
        q = self.q
        self.q = None
        with self.nc.Block(name) as blk:
            for e in self.ENGINES:
                if q[e]:
                    getattr(blk, e)(lambda eng, ops=tuple(q[e]): [op(eng) for op in ops])

    def add(self, engine, fn):
        self.q[engine].append(fn)

    def wait(self, engine, sem_name, v=None):
        v = self.val(sem_name) if v is None else v
        h = self.sem(sem_name)[0]
        self.q[engine].append(lambda eng: eng.wait_ge(h, v))

    def drain(self, engine):
        self.q[engine].append(lambda eng: eng.drain())
        self._since_drain[engine] = 0

    def op(self, engine, fn, inc=None, inck=1, autodrain=False, pre_drain=False):
        if pre_drain:
            self.drain(engine)
        if inc is None:
            self.q[engine].append(lambda eng: fn(eng))
            return None
        v = self.plan_inc(inc, inck)
        h = self.sem(inc)[0]
        self.q[engine].append(lambda eng: fn(eng).then_inc(h, inck))
        return v

    def dma(self, engine, out_ap, in_ap, inc, **kw):
        v = self.plan_inc(inc, 16)
        h = self.sem(inc)[0]
        self.q[engine].append(
            lambda eng: eng.dma_start(out=out_ap, in_=in_ap, **kw).then_inc(h, 16))
        return v


def bcast_read(shape_free_ap_dims, src_tensor, src_offset_elems, free_dims):
    return bass.AP(src_tensor, src_offset_elems, [[0, 128]] + list(free_dims))


def build_program(dbg_taps=()):
    nc = bass.Bass("TRN2", target_bir_lowering=False)
    dt = nc.dram_tensor

    u = dt("u", [T, DM], F32, kind="ExternalInput")
    uT = dt("uT", [DM, T], F32, kind="ExternalInput")
    W_inT = dt("W_inT", [DM, DPROJ], F32, kind="ExternalInput")
    conv_w_c = dt("conv_w_c", [128, DCONV, 18], F32, kind="ExternalInput")
    conv_b_c = dt("conv_b_c", [128, 18], F32, kind="ExternalInput")
    dt_bias = dt("dt_bias", [NH, 1], F32, kind="ExternalInput")
    A_log = dt("A_log", [NH, 1], F32, kind="ExternalInput")
    Dvec = dt("Dvec", [1, NH], F32, kind="ExternalInput")
    nw_c = dt("nw_c", [128, NJT], F32, kind="ExternalInput")
    W_outT = dt("W_outT", [DI, DM], F32, kind="ExternalInput")
    W_base = dt("W_base", [DM, DM], F32, kind="ExternalInput")
    b_base = dt("b_base", [1, DM], F32, kind="ExternalInput")
    Wd1 = dt("Wd1", [8, DM, 256], F32, kind="ExternalInput")
    Wd2 = dt("Wd2", [8, 256, DM], F32, kind="ExternalInput")
    bd1_c = dt("bd1_c", [128, 2, 8], F32, kind="ExternalInput")
    bd2_c = dt("bd2_c", [128, 8, 8], F32, kind="ExternalInput")
    Wa_down_r = dt("Wa_down_r", [DM, 8, 64], F32, kind="ExternalInput")
    Wa_up_r = dt("Wa_up_r", [64, 8, DM], F32, kind="ExternalInput")
    ba_down_c = dt("ba_down_c", [64, 8], F32, kind="ExternalInput")
    ba_up = dt("ba_up", [8, DM], F32, kind="ExternalInput")

    out = dt("out", [T, DM], F32, kind="ExternalOutput")

    z_sp = dt("z_sp", [DI, T], BF16)
    xc_sp = dt("xc_sp", [CONVD, T], BF16)
    y3_sp = dt("y3_sp", [DI, T], BF16)
    wout_bf = dt("wout_bf", [DI, DM], BF16)
    wbase_bfd = dt("wbase_bfd", [DM, DM], BF16)
    wadn_bfd = dt("wadn_bfd", [DM, 8, 64], BF16)
    waup_bfd = dt("waup_bfd", [64, 8, DM], BF16)
    baup_bfd = dt("baup_bfd", [8, DM], BF16)
    xt_sp = dt("xt_sp", [T, DI], BF16)
    bt_sp = dt("bt_sp", [T, DS], BF16)
    rows_sp = dt("rows_sp", [40, T], F32)

    dbg = {}
    for name, shape, dtp in (
        ("xn", [DM, T], BF16), ("dtv", [NH, T], F32), ("acs", [NH, T], F32),
        ("xconv", [CONVD, T], BF16), ("z", [DI, T], BF16), ("s", [T, DM], BF16),
        ("g", [T, DM], BF16), ("y2", [DI, T], BF16),
    ):
        if name in dbg_taps:
            dbg[name] = dt("dbg_" + name, shape, dtp, kind="ExternalOutput")

    ctx = ExitStack()
    em = Emitter(nc, ctx)
    sb = lambda n_, s_, dtp=F32: ctx.enter_context(nc.sbuf_tensor(n_, s_, dtp))
    ps = lambda n_, s_, dtp=F32: ctx.enter_context(nc.psum_tensor(n_, s_, dtp))
    pctx = [None]

    def popen():
        pctx[0] = ExitStack()

    def pclose():
        pctx[0].close()
        pctx[0] = None

    sbp = lambda n_, s_, dtp=F32: pctx[0].enter_context(nc.sbuf_tensor(n_, s_, dtp))
    psp = lambda n_, s_, dtp=F32: pctx[0].enter_context(nc.psum_tensor(n_, s_, dtp))

    const_tiles = []

    def need_const(v):
        try:
            nc.const_aps.tensor(v, [128, 1], F32)
        except AssertionError:
            h = ctx.enter_context(nc.sbuf_tensor("cst_%d" % len(const_tiles), [128, 1], F32))
            nc.const_aps.register(F32, v, h[:])
            const_tiles.append((h, v))
    for v in (EPS, -0.5, -1.0, 1.0, 0.0):
        need_const(v)

    onescol_bf = sb("onescol_bf", [128, 1], BF16)
    onesrow_bf = sb("onesrow_bf", [1, 128], BF16)
    id128_bf = sb("id128_bf", [128, 128], BF16)
    id16_f = sb("id16_f", [16, 16], F32)
    id128_f = sb("id128_f", [128, 128], F32)
    convw = sb("convw", [128, DCONV, 18], F32)
    convb = sb("convb", [128, 18], F32)
    dtb_col = sb("dtb_col", [NH, 1], F32)
    alog_col = sb("alog_col", [NH, 1], F32)
    negA = sb("negA", [NH, 1], F32)
    nwcol = sb("nwcol", [128, NJT], F32)
    Dbc = sb("Dbc", [128, NH], F32)
    diagD = sb("diagD", [128, NH, 128], BF16)
    cd_fm = sb("cd_fm", [NH, NCH], F32)
    bd1_cs = sb("bd1_cs", [128, 2, 8], F32)
    rstd_all = sb("rstd_all", [128, NTT], F32)
    bd2_cs = sb("bd2_cs", [128, 8, 8], F32)
    negid = sb("negid", [128, 128], BF16)
    dt_tm = sb("dt_tm", [128, NTT, NH], F32)
    acs_tm = sb("acs_tm", [128, NTT, NH], F32)
    dec_tm = sb("dec_tm", [128, NTT, NH], F32)
    ddc_tm = sb("ddc_tm", [128, NTT, NH], F32)
    ldt_tm = sb("ldt_tm", [128, NTT, NH], F32)
    cd_bc = sb("cd_bc", [128, NH, NCH], F32)
    binmask = sb("binmask", [128, 2, CH], BF16)

    s_d = "s_d"
    s_h = "s_h"
    s_p = "s_p"
    s_v = "s_v"
    s_a = "s_a"
    s_g = "s_g"

    xn_fm = ctx.enter_context(nc.sbuf_tensor("xn_fm", [128, NKT, T], BF16, side="right"))
    popen()
    em.begin()
    g_setup_done = None

    def _setup(g):
        for h, v in const_tiles:
            g.memset(h[:], v)
        g.memset(onescol_bf[:], 1.0)
        g.memset(onesrow_bf[:], 1.0)
        g.memset(id128_bf[:], 1.0)
        g.drain()
        g.affine_select(id128_bf[:], id128_bf[:], [[-1, 128]], ALU.is_equal, 0.0,
                        base=0, channel_multiplier=1)
        g.memset(id16_f[:], 1.0)
        g.drain()
        g.affine_select(id16_f[:], id16_f[:], [[-1, 16]], ALU.is_equal, 0.0,
                        base=0, channel_multiplier=1)
        g.memset(id128_f[:], 1.0)
        g.drain()
        g.affine_select(id128_f[:], id128_f[:], [[-1, 128]], ALU.is_equal, 0.0,
                        base=0, channel_multiplier=1)
        g.memset(binmask[:], 1.0)
        g.drain()
        g.affine_select(binmask[:, 0, :], binmask[:, 0, :], [[1, CH]], ALU.is_ge, 0.0,
                        base=0, channel_multiplier=-1)
        g.drain()
        return g.affine_select(binmask[:, 1, :], binmask[:, 1, :], [[1, CH]], ALU.is_ge, 0.0,
                               base=-128, channel_multiplier=-1)
    g_setup_done = em.op("gpsimd", _setup, inc=s_g)

    em.dma("sync", convw[:], conv_w_c[:, :, :], s_h)
    em.dma("sync", convb[:], conv_b_c[:, :], s_h)
    em.dma("sync", bd1_cs[:, :, :], bd1_c[:, :, :], s_h)
    em.dma("sync", bd2_cs[:, :, :], bd2_c[:, :, :], s_h)
    em.dma("sync", dtb_col[:], dt_bias[:, :], s_h)
    em.dma("sync", alog_col[:], A_log[:, :], s_h)
    em.dma("sync", nwcol[:], nw_c[:, :], s_h)
    v_dbc = em.dma("sync", Dbc[:], bass.AP(Dvec, 0, [[0, 128], [1, NH]]), s_h)

    u_f32 = sbp("u_f32", [128, NKT, T], F32)
    usq = sbp("usq", [128, 2, T], F32)
    onescol_f0 = sbp("onescol_f0", [128, 1], F32)
    v_o1 = em.op("gpsimd", lambda e: e.memset(onescol_f0[:], 1.0), inc=s_g)

    kv = {}
    for k in range(NKT):
        qe = "sync" if k % 2 == 0 else "scalar"
        sm = "s_u%d" % k
        kv[k] = (sm, em.dma(qe, u_f32[:, k, :],
                            bass.AP(uT, k * 128 * T, [[T, 128], [1, T]]), sm))

    psStat = psp("psStat", [1, 2048], F32)
    psStat2 = psp("psStat2", [1, 2048], F32)
    em.wait("tensor", s_g, g_setup_done)
    em.wait("tensor", s_g, v_o1)
    sq_mm_v = {}
    for k in range(NKT):
        em.wait("tensor", kv[k][0], kv[k][1])

        def _mean_k(t, k=k):
            last = None
            for nt in range(4):
                last = t.matmul(psStat[0:1, nt * 512:(nt + 1) * 512], onescol_f0[:],
                                u_f32[:, k, nt * 512:(nt + 1) * 512],
                                start=(k == 0), stop=(k == NKT - 1))
            return last
        em.op("tensor", _mean_k, inc=s_p if k == NKT - 1 else None)
        em.wait("scalar", kv[k][0], kv[k][1])
        if k >= 2:
            em.wait("scalar", s_p, sq_mm_v[k - 2])
        vsq = em.op("scalar", lambda e, k=k: e.activation(
            usq[:, k % 2, :], u_f32[:, k, :], AF.Square), inc=s_a)
        em.wait("tensor", s_a, vsq)

        def _sq_k(t, k=k):
            last = None
            for nt in range(4):
                last = t.matmul(psStat2[0:1, nt * 512:(nt + 1) * 512], onescol_f0[:],
                                usq[:, k % 2, nt * 512:(nt + 1) * 512],
                                start=(k == 0), stop=(k == NKT - 1))
            return last
        sq_mm_v[k] = em.op("tensor", _sq_k, inc=s_p)
    v_stat_mms = em.val(s_p)
    mean_row = sbp("mean_row", [1, T], F32)
    em.wait("vector", s_p, v_stat_mms)
    v_meanrow = em.op("vector", lambda e: e.tensor_scalar_mul(mean_row[:], psStat[0:1, :], 1.0 / DM),
                      inc=s_v)
    rstd_row = sbp("rstd_row", [1, T], F32)
    em.op("vector", lambda e: e.scalar_tensor_tensor(
        rstd_row[:], mean_row[:], -1.0, mean_row[:], ALU.mult, ALU.mult),
        pre_drain=True)
    v_var = em.op("vector", lambda e: e.scalar_tensor_tensor(
        rstd_row[:], psStat2[0:1, :], 1.0 / DM, rstd_row[:], ALU.mult, ALU.add), inc=s_v,
        pre_drain=True)
    em.wait("scalar", s_v, v_var)
    em.wait("scalar", s_g, g_setup_done)
    em.op("scalar", lambda e: e.activation(rstd_row[:], rstd_row[:], AF.Ln, bias=EPS))
    v_rstd = em.op("scalar", lambda e: e.activation(rstd_row[:], rstd_row[:], AF.Exp, scale=-0.5),
                   inc=s_a, pre_drain=True)
    em.wait("sync", s_v, v_meanrow)
    em.wait("sync", s_a, v_rstd)
    em.dma("sync", rows_sp[0:1, :], mean_row[:], s_h)
    v_rows = em.dma("sync", rows_sp[1:2, :], rstd_row[:], s_h)
    mean_bc = sbp("mean_bc", [128, T], F32)
    rstd_bc = sbp("rstd_bc", [128, T], F32)
    em.wait("sync", s_h, v_rows)
    em.dma("sync", mean_bc[:], bass.AP(rows_sp, 0, [[0, 128], [1, T]]), s_h)
    v_bc = em.dma("sync", rstd_bc[:], bass.AP(rows_sp, T, [[0, 128], [1, T]]), s_h)

    em.wait("vector", s_h, v_bc)
    em.wait("gpsimd", s_h, v_bc)
    vsubk = {}
    for k in range(NKT):
        vsubk[k] = em.op("vector", lambda e, k=k: e.tensor_sub(
            u_f32[:, k, :], u_f32[:, k, :], mean_bc[:]), inc=s_v)
    for k in range(NKT):
        em.wait("gpsimd", s_v, vsubk[k])
        em.op("gpsimd", lambda e, k=k: e.tensor_mul(
            xn_fm[:, k, :], u_f32[:, k, :], rstd_bc[:]), inc=s_g)
    v_xn_v = em.val(s_v)
    v_xn_g = em.val(s_g)
    em.wait("sync", s_v, v_xn_v)
    em.wait("sync", s_g, v_xn_g)
    if "xn" in dbg:
        for k in range(NKT):
            em.dma("sync", dbg["xn"][k * 128:(k + 1) * 128, :], xn_fm[:, k, :], s_h)
    em.end("ph0_ln")
    pclose()

    mctx = ExitStack()
    sbm = lambda n_, s_, dtp=F32: mctx.enter_context(nc.sbuf_tensor(n_, s_, dtp))
    dt_fm = sbm("dt_fm", [NH, T], F32)
    acs_fm = sbm("acs_fm", [NH, T], F32)
    row16a = sbm("row16a", [NH, T], F32)
    row16b = sbm("row16b", [NH, T], F32)
    chunkmask = sbm("chunkmask", [NH, T], F32)

    popen()
    em.begin()
    GW = 4
    NG = 9
    wbuf = sbp("wbuf", [128, 3, NKT, GW * 128], BF16)
    wstg = sbp("wstg", [128, 2, NKT, GW * 128], F32)
    ztile = sbp("ztile", [128, 2, T], BF16)
    xbc_sb = sbp("xbc_sb", [128, 2, 4 + T], BF16)
    tapp = sbp("tapp", [128, 2, 4, T], BF16)
    xctile = sbp("xctile", [128, 2, T], BF16)
    psA = psp("psA", [128, T], F32)
    psB = psp("psB", [128, T], F32)
    psum = (psA, psB)

    def _cmask(g):
        g.memset(chunkmask[:], 1.0)
        g.drain()
        last = None
        for c in range(NCH):
            last = g.memset(chunkmask[:, c * CH:c * CH + 1], 0.0)
        return last
    v_cmask = em.op("gpsimd", _cmask, inc=s_g)

    em.wait("vector", s_h)
    def _diagd(v):
        last = None
        for h in range(NH):
            last = v.tensor_scalar(diagD[:, h, :], id128_bf[:], Dbc[:, h:h + 1], None, ALU.mult)
        return last
    v_diagd = em.op("vector", _diagd, inc=s_v)

    def wload(gi):
        sl = gi % 2
        j0 = gi * GW * 128
        w = min(GW * 128, DPROJ - j0)
        em.dma("sync", wstg[:, sl, 0:4, 0:w],
               bass.AP(W_inT, j0, [[DPROJ, 128], [DPROJ * 128, 4], [1, w]]),
               "s_q%d" % sl)
        return em.dma("scalar", wstg[:, sl, 4:8, 0:w],
                      bass.AP(W_inT, 4 * 128 * DPROJ + j0,
                              [[DPROJ, 128], [DPROJ * 128, 4], [1, w]]),
                      "s_q%d" % sl)

    wv = {}
    castv = {}
    em.dma("gpsimd", wout_bf.ap(), W_outT.ap(), "s_wc")
    em.dma("gpsimd", wbase_bfd.ap(), W_base.ap(), "s_wc")
    em.dma("gpsimd", wadn_bfd.ap(), Wa_down_r.ap(), "s_wc")
    em.dma("gpsimd", waup_bfd.ap(), Wa_up_r.ap(), "s_wc")
    em.dma("gpsimd", baup_bfd.ap(), ba_up.ap(), "s_wc")

    def wcast(gi):
        sl = gi % 2
        bsl = gi % 3
        w = min(GW * 128, DPROJ - gi * GW * 128)
        em.wait("vector", "s_q%d" % sl, wv[gi])
        if gi >= 3:
            em.wait("vector", s_p, pe_done[(gi - 2) * GW - 1])
        castv[gi] = em.op("vector", lambda e, bsl=bsl, sl=sl, w=w: e.tensor_copy(
            wbuf[:, bsl, :, 0:w], wstg[:, sl, :, 0:w]), inc=s_v)

    wv[0] = wload(0)
    wv[1] = wload(1)
    wcast(0)
    wcast(1)
    em.wait("sync", s_v, castv[0])
    em.wait("scalar", s_v, castv[0])
    wv[2] = wload(2)
    wcast(2)

    pe_done = {}
    evac_done = {}
    silu_done = {}
    tap_done = {}
    v_dtevac_a = v_dtevac_v = None

    def emit_silu(mt):
        esl = mt % 2
        cm = mt - 16
        em.wait("scalar", tap_done[mt][0], tap_done[mt][1])
        if cm >= 2:
            em.wait("scalar", "s_x%d" % esl)
        silu_done[mt] = em.op(
            "scalar", lambda e, esl=esl, cm=cm: e.activation(
                xctile[:, esl, :], tapp[:, esl, 1, :], AF.Silu,
                bias=convb[:, cm:cm + 1]), inc=s_a)
        em.wait("sync", s_a, silu_done[mt])
        em.dma("sync", xc_sp[cm * 128:(cm + 1) * 128, :], xctile[:, esl, :], "s_x%d" % esl)
        if "xconv" in dbg:
            em.dma("sync", dbg["xconv"][cm * 128:(cm + 1) * 128, :], xctile[:, esl, :],
                   "s_x%d" % esl)

    for mt in range(NMT):
        gi = mt // GW
        sl = gi % 2
        moff = (mt % GW) * 128
        mw = 128 if mt < NMT - 1 else 16
        pslot = psum[mt % 2]
        em.wait("tensor", s_v, castv[gi])
        if mt == 0:
            em.wait("tensor", s_v, v_xn_v)
            em.wait("tensor", s_g, v_xn_g)
        if mt >= 2 and evac_done.get(mt - 2):
            sname, val = evac_done[mt - 2]
            em.wait("tensor", sname, val)

        def _mms(t, bsl=gi % 3, moff=moff, mw=mw, pslot=pslot):
            last = None
            for kt in range(NKT):
                for nt in range(4):
                    last = t.matmul(pslot[0:mw, nt * 512:(nt + 1) * 512],
                                    wbuf[:, bsl, kt, moff:moff + mw],
                                    xn_fm[:, kt, nt * 512:(nt + 1) * 512],
                                    start=(kt == 0), stop=(kt == NKT - 1))
            return last
        pe_done[mt] = em.op("tensor", _mms, inc=s_p)
        if mt % GW == 0 and gi >= 1 and gi + 2 < NG:
            em.wait("sync", s_v, castv[gi])
            em.wait("scalar", s_v, castv[gi])
            wv[gi + 2] = wload(gi + 2)
            wcast(gi + 2)

        esl = mt % 2
        if mt < 16:
            em.wait("scalar", s_p, pe_done[mt])
            if mt >= 2:
                em.wait("scalar", "s_z%d" % esl)
            silu_done[mt] = em.op(
                "scalar", lambda e, esl=esl, pslot=pslot: e.activation(
                    ztile[:, esl, :], pslot[:, :], AF.Silu), inc=s_a)
            evac_done[mt] = (s_a, silu_done[mt])
            em.wait("sync", s_a, silu_done[mt])
            em.dma("sync", z_sp[mt * 128:(mt + 1) * 128, :], ztile[:, esl, :], "s_z%d" % esl)
            if "z" in dbg:
                em.dma("sync", dbg["z"][mt * 128:(mt + 1) * 128, :], ztile[:, esl, :],
                       "s_z%d" % esl)
        elif mt < 34:
            cm = mt - 16
            if cm >= 2 and tap_done.get(mt - 2):
                sname, val = tap_done[mt - 2]
                em.wait("vector", sname, val)
            if cm == 0:
                em.op("gpsimd", lambda g: g.memset(xbc_sb[:, 0, 0:4], 0.0))
                em.op("gpsimd", lambda g: g.memset(xbc_sb[:, 1, 0:4], 0.0), inc=s_g)
            em.wait("scalar", s_p, pe_done[mt])
            ev = em.op("scalar", lambda e, esl=esl, pslot=pslot: e.activation(
                xbc_sb[:, esl, 4:4 + T], pslot[0:128, :], AF.Copy), inc=s_a)
            evac_done[mt] = (s_a, ev)
            if cm == 0:
                em.wait("vector", s_g)
            em.wait("vector", s_a, ev)
            for k in range(4):
                em.op("vector", lambda e, esl=esl, cm=cm, k=k: e.tensor_scalar(
                    tapp[:, esl, k, :], xbc_sb[:, esl, 1 + k:1 + k + T],
                    convw[:, k, cm:cm + 1], None, ALU.mult),
                    inc=s_v if k == 3 else None)
            v_pp = em.val(s_v)
            em.wait("gpsimd", s_v, v_pp)
            if cm >= 2:
                em.wait("gpsimd", s_a)
            v_ts = em.op("gpsimd", lambda g, esl=esl: g.tensor_add(
                tapp[:, esl, 0, :], tapp[:, esl, 0, :], tapp[:, esl, 1, :]), inc=s_g)
            em.op("vector", lambda e, esl=esl: e.tensor_add(
                tapp[:, esl, 2, :], tapp[:, esl, 2, :], tapp[:, esl, 3, :]),
                pre_drain=True)
            em.wait("vector", s_g, v_ts)
            tv = em.op("vector", lambda e, esl=esl: e.tensor_add(
                tapp[:, esl, 1, :], tapp[:, esl, 0, :], tapp[:, esl, 2, :]), inc=s_v,
                pre_drain=True)
            tap_done[mt] = (s_v, tv)
            if mt > 16:
                emit_silu(mt - 1)
            if mt == 33:
                emit_silu(33)
        else:
            em.wait("scalar", s_p, pe_done[mt])
            em.wait("scalar", s_h, em.val(s_h))
            v_dtevac_a = em.op("scalar", lambda e, pslot=pslot: e.activation(
                row16a[:, :], pslot[0:16, :], AF.Abs, bias=dtb_col[:, 0:1]), inc=s_a)
            em.wait("vector", s_p, pe_done[mt])
            v_dtevac_v = em.op("vector", lambda e, pslot=pslot: e.tensor_scalar(
                dt_fm[:, :], pslot[0:16, :], dtb_col[:, 0:1], 0.0, ALU.add, ALU.max), inc=s_v)
    for sem_name in ("s_z0", "s_z1", "s_x0", "s_x1"):
        em.wait("sync", sem_name)
    em.end("ph1_inproj")
    pclose()

    popen()
    em.begin()
    psTx = psp("psTx", [128, 16], F32)
    psTy = psp("psTy", [128, 16], F32)
    psTd = (psTx, psTy)
    em.op("scalar", lambda e: e.activation(row16b[:, :], row16a[:, :], AF.Exp, scale=-1.0),
          pre_drain=True)
    va3 = em.op("scalar", lambda e: e.activation(row16b[:, :], row16b[:, :], AF.Ln, bias=1.0),
                inc=s_a, pre_drain=True)
    em.wait("vector", s_a, va3)
    vdt = em.op("vector", lambda e: e.tensor_add(dt_fm[:, :], dt_fm[:, :], row16b[:, :]),
                inc=s_v, pre_drain=True)
    vna = em.op("scalar", lambda e: e.activation(negA[:, :], alog_col[:, :], AF.Exp), inc=s_a)
    em.wait("vector", s_a, vna)
    em.op("vector", lambda e: e.tensor_scalar(
        row16a[:, :], dt_fm[:, :], negA[:, 0:1], -1.0, ALU.mult, ALU.mult), pre_drain=True)
    vacs = em.op("vector", lambda e: e.tensor_tensor_scan(
        acs_fm[:, :], chunkmask[:, :], row16a[:, :], 0.0, ALU.mult, ALU.add), inc=s_v,
        pre_drain=True)
    em.wait("sync", s_v, vacs)
    em.dma("sync", bass.AP(rows_sp, 2 * T, [[CH, NH], [NH * CH, NCH], [1, CH]]),
           acs_fm[:, :].rearrange("p (c q) -> p c q", c=NCH), s_h)
    em.wait("scalar", s_v, vacs)
    for c in range(NCH):
        em.op("scalar", lambda e, c=c: e.activation(
            row16a[:, c * CH:(c + 1) * CH], acs_fm[:, c * CH:(c + 1) * CH],
            AF.Exp, scale=-1.0, bias=acs_fm[:, c * CH + CH - 1:c * CH + CH]))
    vcd = em.op("scalar", lambda e: e.activation(
        cd_fm[:, :], bass.AP(acs_fm, CH - 1, [[T, NH], [CH, NCH]]), AF.Exp), inc=s_a)
    em.wait("sync", s_a, vcd)
    v_cd_sp = em.dma("sync", bass.AP(rows_sp, 34 * T, [[8, NH], [1, NCH]]), cd_fm[:, :], s_h)
    em.wait("sync", s_h, v_cd_sp)
    em.dma("sync", cd_bc[:, :, :], bass.AP(rows_sp, 34 * T, [[0, 128], [8, NH], [1, NCH]]), s_h)
    tsrc = (dt_fm, acs_fm, row16a)
    tdst = (dt_tm, acs_tm, dec_tm)
    em.wait("tensor", s_v, vdt)
    em.wait("tensor", s_v, vacs)
    em.wait("tensor", s_a, vcd)
    tp_done = {}
    for idx in range(NTT * 3):
        tt, wch = idx // 3, idx % 3
        bk = idx % 2
        if idx >= 2:
            em.wait("tensor", s_v, tp_done[idx - 2])
        pv = em.op("tensor", lambda t, tt=tt, wch=wch, bk=bk: t.transpose(
            psTd[bk][:, :], tsrc[wch][:, tt * 128:(tt + 1) * 128],
            id16_f[:]), inc=s_p)
        em.wait("vector", s_p, pv)
        tp_done[idx] = em.op("vector", lambda e, tt=tt, wch=wch, bk=bk: e.tensor_copy(
            tdst[wch][:, tt, :], psTd[bk][:, :]), inc=s_v)
    em.wait("vector", s_v, tp_done[NTT * 3 - 1])
    v_ddc = em.op("vector", lambda v: v.tensor_mul(
        ddc_tm.ap().rearrange("p a b -> p (a b)"),
        dec_tm.ap().rearrange("p a b -> p (a b)"),
        dt_tm.ap().rearrange("p a b -> p (a b)")), inc=s_v, pre_drain=True)
    em.wait("scalar", s_v, v_ddc)
    v_ldt = em.op("scalar", lambda s: s.activation(
        ldt_tm.ap().rearrange("p a b -> p (a b)"),
        dt_tm.ap().rearrange("p a b -> p (a b)"), AF.Ln), inc=s_a)
    if "dtv" in dbg:
        em.wait("sync", s_v, vdt)
        em.dma("sync", dbg["dtv"][:, :], dt_fm[:, :], s_h)
    if "acs" in dbg:
        em.wait("sync", s_v, vacs)
        em.dma("sync", dbg["acs"][:, :], acs_fm[:, :], s_h)
    em.wait("sync", s_h, em.val(s_h))
    em.end("ph1b_prep")
    pclose()
    mctx.close()

    popen()
    em.begin()
    xcm = sbp("xcm", [128, 2, T], BF16)
    xstage = sbp("xstage", [128, 2, NTT, 128], BF16)
    psTq = [psp("psTq%d" % i, [128, 128], BF16) for i in range(4)]
    ld_done = {}
    tr_ev = {}
    st_done = {}
    for m in range(17):
        sl = m % 2
        sem_l = "s_tc%d" % sl
        if m >= 2:
            em.wait("sync", s_p, tr_ev[(m - 2, NTT - 1)][0])
        ld_done[m] = em.dma("sync", xcm[:, sl, :],
                            xc_sp[m * 128:(m + 1) * 128, :], sem_l)
        em.wait("tensor", sem_l, ld_done[m])
        for tt in range(NTT):
            gi = (m * NTT + tt)
            bk = gi % 4
            if gi >= 4:
                pk = ((gi - 4) // NTT, (gi - 4) % NTT)
                em.wait("tensor", tr_ev[pk][2], tr_ev[pk][1])
            pv = em.op("tensor", lambda t, sl=sl, tt=tt, bk=bk: t.transpose(
                psTq[bk][:, :], xcm[:, sl, tt * 128:(tt + 1) * 128], id128_bf[:]),
                inc=s_p)
            if gi % 2 == 0:
                em.wait("scalar", s_p, pv)
                if m >= 2 and tt <= 1:
                    em.wait("scalar", "s_ts%d" % sl)
                ev = em.op("scalar", lambda s, sl=sl, tt=tt, bk=bk: s.activation(
                    xstage[:, sl, tt, :], psTq[bk][:, :], AF.Copy), inc=s_a)
                tr_ev[(m, tt)] = (pv, ev, s_a)
            else:
                em.wait("vector", s_p, pv)
                if m >= 2 and tt <= 1:
                    em.wait("vector", "s_ts%d" % sl)
                ev = em.op("vector", lambda v, sl=sl, tt=tt, bk=bk: v.tensor_copy(
                    xstage[:, sl, tt, :], psTq[bk][:, :]), inc=s_v)
                tr_ev[(m, tt)] = (pv, ev, s_v)
        em.wait("scalar", tr_ev[(m, NTT - 2)][2], tr_ev[(m, NTT - 2)][1])
        em.wait("scalar", tr_ev[(m, NTT - 1)][2], tr_ev[(m, NTT - 1)][1])
        if m < 16:
            st_done[m] = em.dma("scalar",
                bass.AP(xt_sp, m * 128, [[DI, 128], [128 * DI, NTT], [1, 128]]),
                xstage[:, sl, :, :], "s_ts%d" % sl)
        else:
            st_done[m] = em.dma("scalar",
                bass.AP(bt_sp, 0, [[DS, 128], [128 * DS, NTT], [1, 128]]),
                xstage[:, sl, :, :], "s_ts%d" % sl)
    em.wait("sync", "s_ts0")
    em.wait("sync", "s_ts1")
    em.end("ph1c_transpose")
    pclose()

    popen()
    em.begin()
    xt_c = sbp("xt_c", [128, 2, 2, DI], BF16)
    bc_c = sbp("bc_c", [128, 2, 2, CH], BF16)
    bt_c = sbp("bt_c", [128, 2, 2, DS], BF16)
    sz_c = sbp("sz_c", [128, 2, NH, CH], BF16)
    csb_c = sbp("csb_c", [128, 2, NH, CH], F32)
    bd = sbp("bd", [128, 4, 2, 128], BF16)
    mcbt = sbp("mcbt", [128, 2, 2, CH], BF16)
    sc_f32 = sbp("sc_f32", [128, 2, 2, CH], F32)
    exps = sbp("exps", [128, 2, 2, CH], BF16)
    eabx = sbp("eabx", [128, 2, CH], BF16)
    scoresT = sbp("scoresT", [128, 4, 2, CH], BF16)
    cexp = sbp("cexp", [128, 4, CH], BF16)
    prevf = sbp("prevf", [128, NH * 128], F32)
    prevb = sbp("prevb", [128, NH * 128], BF16)
    NY2 = NH if "y2" in dbg else 4
    y2b = sbp("y2b", [128, NY2, CH], BF16)
    y3b = sbp("y3b", [128, 2, NH, CH], BF16)
    sqb = sbp("sqb", [128, NH, CH], BF16)
    rstd_tmp = sbp("rstd_tmp", [128, 1], F32)
    wd1buf = sbp("wd1buf", [128, 2, NKT, 256], BF16)
    wd2buf = sbp("wd2buf", [128, 2, 2, DM], BF16)
    h_sb = sbp("h_sb", [128, 2, T], BF16)
    sqscr = sbp("sqscr", [128, 512], BF16)
    rsub_sb = sbp("rsub_sb", [128, 2, 512], BF16)
    sqacc = sbp("sqacc", [128, 256], F32)
    onescol_f = sbp("onescol_f", [128, 1], F32)
    loss_row = sbp("loss_row", [1, 256], F32)
    losses8 = sbp("losses8", [1, 8], F32)
    minv = sbp("minv", [1, 1], F32)
    maskr = sbp("maskr", [1, 8], F32)
    psYS = psp("psYS", [128, 1024], F32)
    psCBT = psp("psCBT", [128, 512], F32)
    psH2 = psp("psH2", [128, 2, 512], F32)
    psR2 = psp("psR2", [128, 3, 512], F32)

    em.op("gpsimd", lambda e: e.memset(prevf[:], 0.0))
    v_prevmem = em.op("gpsimd", lambda e: e.memset(prevb[:], 0.0), inc=s_g)
    v_of = em.op("gpsimd", lambda e: e.memset(onescol_f[:], 1.0), inc=s_g)
    v_negid = em.op("gpsimd", lambda g: g.tensor_scalar(
        negid[:, :], id128_bf[:, :], -1.0, None, ALU.mult), inc=s_g)

    def chload(c):
        sl = c % 2
        sem = "s_c%d" % sl
        em.dma("sync", xt_c[:, sl, :, :],
               bass.AP(xt_sp, c * CH * DI, [[DI, 128], [128 * DI, 2], [1, DI]]), sem)
        em.dma("sync", bt_c[:, sl, :, :],
               bass.AP(bt_sp, c * CH * DS, [[DS, 128], [128 * DS, 2], [1, DS]]), sem)
        em.dma("sync", bc_c[:, sl, :, :],
               bass.AP(xc_sp, 16 * 128 * T + c * CH, [[T, 128], [128 * T, 2], [1, CH]]),
               sem)
        em.dma("sync", sz_c[:, sl, :, :],
               bass.AP(z_sp, c * CH, [[T, 128], [128 * T, NH], [1, CH]]), sem)
        return em.dma("sync", csb_c[:, sl, :, :],
                      bass.AP(rows_sp, 2 * T + c * NH * CH, [[0, 128], [1, NH * CH]]),
                      sem)

    def wdload(e):
        sl = e % 2
        em.dma("gpsimd", wd1buf[:, sl, :, :],
               bass.AP(Wd1, e * DM * 256, [[256, 128], [128 * 256, NKT], [1, 256]]),
               "s_d1%d" % sl)
        return em.dma("gpsimd", wd2buf[:, sl, :, :],
                      bass.AP(Wd2, e * 256 * DM, [[DM, 128], [128 * DM, 2], [1, DM]]),
                      "s_d2%d" % sl)

    ch_ld = {}
    ch_ld[0] = chload(0)
    ch_ld[1] = chload(1)
    wdv = {0: wdload(0), 1: wdload(1)}

    y2ev = {}
    scan_done = {}
    cast_done = {}
    mm_done = {}
    sqv = {}
    y3v = {}
    scorem = {}
    cexpv = {}
    bdv = {}
    lastexp = []
    lastscorem = []
    v_rstd = {}
    hev3 = {}
    hro = {}
    rro = {}
    rro_sub = {}
    sqv3 = {}

    def ffn_h_round(e, r):
        mtH, th = r % 2, r // 2
        sl = e % 2
        em.wait("tensor", "s_d1%d" % sl, wdv[e])
        for nth in range(2):
            if (e, r) != (0, 0):
                pk = (e, r - 1, nth) if r else (e - 1, 3, nth)
                em.wait("tensor", s_a, hev3[pk])
            if e >= 1 and r == 0 and nth == 0:
                em.wait("tensor", s_p, rro[(e - 1, 31)])

            def _hmm(t, sl=sl, mtH=mtH, th=th, nth=nth):
                last = None
                c0 = th * 1024 + nth * 512
                for kt in range(NKT):
                    last = t.matmul(psH2[:, nth, :],
                                    wd1buf[:, sl, kt, mtH * 128:(mtH + 1) * 128],
                                    xn_fm[:, kt, c0:c0 + 512],
                                    start=(kt == 0), stop=(kt == NKT - 1))
                return last
            hro[(e, r, nth)] = em.op("tensor", _hmm, inc=s_p)
            em.wait("scalar", s_p, hro[(e, r, nth)])
            hev3[(e, r, nth)] = em.op("scalar", lambda s_, e=e, mtH=mtH, th=th,
                                      nth=nth: s_.activation(
                h_sb[:, mtH, th * 1024 + nth * 512:th * 1024 + (nth + 1) * 512],
                psH2[:, nth, :], AF.Relu, bias=bd1_cs[:, mtH, e:e + 1]), inc=s_a)

    def ffn_r_round(e, ri):
        mtD, q = ri // 4, ri % 4
        th = q // 2
        gk = e * 32 + ri
        rsl = gk % 3
        sl = e % 2
        em.wait("tensor", "s_d2%d" % sl, wdv[e])
        em.wait("tensor", s_a, hev3[(e, 2 * th + 1, q % 2)])
        if gk == 0:
            em.wait("tensor", s_g, v_negid)
        if gk >= 3:
            em.wait("tensor", s_a, sqv3[((gk - 3) // 32, (gk - 3) % 32)])

        def _rmm(t, sl=sl, mtD=mtD, q=q, rsl=rsl):
            t.matmul(psR2[:, rsl, :], wd2buf[:, sl, 0, mtD * 128:(mtD + 1) * 128],
                     h_sb[:, 0, q * 512:(q + 1) * 512], start=True, stop=False)
            t.matmul(psR2[:, rsl, :], wd2buf[:, sl, 1, mtD * 128:(mtD + 1) * 128],
                     h_sb[:, 1, q * 512:(q + 1) * 512], start=False, stop=False)
            return t.matmul(psR2[:, rsl, :], negid[:, :],
                            xn_fm[:, mtD, q * 512:(q + 1) * 512], start=False, stop=True)
        rro[(e, ri)] = em.op("tensor", _rmm, inc=s_p)
        rro_sub[(e, ri)] = rro[(e, ri)]
        em.wait("scalar", s_p, rro[(e, ri)])
        if gk >= 1:
            pgk = gk - 1
            em.wait("scalar", s_a, sqv3[(pgk // 32, pgk % 32)])
        col = e * 32 + mtD * 4 + q
        sqv3[(e, ri)] = em.op("scalar", lambda s, e=e, mtD=mtD, rsl=rsl, col=col:
            s.activation(sqscr[:, :], psR2[:, rsl, :], AF.Square,
                         bias=bd2_cs[:, mtD, e:e + 1],
                         accum_out=sqacc[:, col:col + 1]), inc=s_a)

    def ffn_step(e, h):
        if h < 4:
            ffn_h_round(e, h)
        else:
            i0 = 3 * (h - 4) if h < 12 else 24 + 2 * (h - 12)
            n = 3 if h < 12 else 2
            for ri in range(i0, i0 + n):
                ffn_r_round(e, ri)

    def emit_consumers(c, h, sl):
        gi = c * NH + h
        em.wait("vector", s_p, mm_done[(c, h)])
        scan_done[(c, h)] = em.op("vector", lambda e, h=h, c=c, ys=(h % 2) * 512:
            e.scalar_tensor_tensor(prevf[:, h * 128:(h + 1) * 128],
                                   prevf[:, h * 128:(h + 1) * 128],
                                   cd_bc[:, h, c:c + 1], psYS[:, ys + 256:ys + 384],
                                   ALU.mult, ALU.add), inc=s_v)
        em.wait("gpsimd", s_v, scan_done[(c, h)])
        cast_done[(c, h)] = em.op("gpsimd", lambda e, h=h: e.tensor_copy(
            prevb[:, h * 128:(h + 1) * 128], prevf[:, h * 128:(h + 1) * 128]), inc=s_g)
        if gi >= NY2:
            pk4 = (gi - NY2)
            em.wait("vector", s_a, sqv[(pk4 // NH, pk4 % NH)])
            em.wait("vector", s_g, y3v[(pk4 // NH, pk4 % NH)])
        y2ev[(c, h)] = em.op("vector", lambda e, sl=sl, h=h, hy=h % NY2, ys=(h % 2) * 512:
            e.tensor_mul(y2b[:, hy, :], psYS[:, ys:ys + CH], sz_c[:, sl, h, :]), inc=s_v)
        em.wait("scalar", s_v, y2ev[(c, h)])
        sqv[(c, h)] = em.op("scalar", lambda e, h=h, hy=h % NY2: e.activation(
            sqb[:, h, :], y2b[:, hy, :], AF.Square), inc=s_a)
        em.wait("gpsimd", s_v, y2ev[(c, h)])
        if h == 0 and c >= 2:
            em.wait("gpsimd", "s_y3")
        y3v[(c, h)] = em.op("gpsimd", lambda e, h=h, hy=h % NY2, ysl=c % 2:
            e.tensor_tensor(y3b[:, ysl, h, :], y2b[:, hy, :],
                            nwcol[:, h:h + 1].broadcast_to([128, CH]), ALU.mult),
            inc=s_g)

    def produce(c, h):
        sl = c % 2
        sem_c = "s_c%d" % sl
        hs = h % 4
        hs2 = h % 2
        gi = c * NH + h
        if h == 0:
            em.wait("vector", sem_c, ch_ld[c])
            em.wait("gpsimd", sem_c, ch_ld[c])
            em.wait("scalar", sem_c, ch_ld[c])
        if gi >= 4:
            pk = gi - 4
            em.wait("vector", s_p, mm_done[(pk // NH, pk % NH)])
            em.wait("gpsimd", s_p, mm_done[(pk // NH, pk % NH)])

        def _bd(v, sl=sl, h=h, hs=hs, c=c):
            v.tensor_scalar(bd[:, hs, 0, :], bt_c[:, sl, 0, :],
                            ddc_tm[:, 2 * c, h:h + 1], None, ALU.mult)
            return v.tensor_scalar(bd[:, hs, 1, :], bt_c[:, sl, 1, :],
                                   ddc_tm[:, 2 * c + 1, h:h + 1], None, ALU.mult)
        bdv[(c, h)] = em.op("vector", _bd, inc=s_v)
        if gi >= 2:
            pk2 = gi - 2
            em.wait("scalar", s_g, cexpv[(pk2 // NH, pk2 % NH)])
        v_eabx = em.op("scalar", lambda s_, sl=sl, h=h, hs2=hs2: s_.activation(
            eabx[:, hs2, :], csb_c[:, sl, h, :], AF.Exp), inc=s_a)
        em.wait("gpsimd", s_a, v_eabx)
        cexpv[(c, h)] = em.op("gpsimd", lambda e_, sl=sl, h=h, hs=hs, hs2=hs2:
            e_.tensor_mul(cexp[:, hs, :], eabx[:, hs2, :], bc_c[:, sl, 1, :]),
            inc=s_g)
        if len(lastexp) >= 2:
            em.wait("vector", s_a, lastexp[-2])
        em.op("vector", lambda e_, sl=sl, h=h, c=c, hs2=hs2: e_.tensor_scalar(
            sc_f32[:, hs2, 0, :], csb_c[:, sl, h, :], acs_tm[:, 2 * c, h:h + 1], 0.0,
            ALU.subtract, ALU.min))
        v_diff = em.op("vector", lambda e_, sl=sl, h=h, c=c, hs2=hs2: e_.tensor_scalar(
            sc_f32[:, hs2, 1, :], csb_c[:, sl, h, :], acs_tm[:, 2 * c + 1, h:h + 1],
            0.0, ALU.subtract, ALU.min), inc=s_v)
        em.wait("scalar", s_v, v_diff)
        if len(lastscorem) >= 2:
            em.wait("scalar", s_g, lastscorem[-2])
        if c == 0 and h == 0:
            em.wait("scalar", s_a, v_ldt)
        em.op("scalar", lambda e_, hs2=hs2, h=h, c=c: e_.activation(
            exps[:, hs2, 0, :], sc_f32[:, hs2, 0, :], AF.Exp,
            bias=ldt_tm[:, 2 * c, h:h + 1]))
        v_exp = em.op("scalar", lambda e_, hs2=hs2, h=h, c=c: e_.activation(
            exps[:, hs2, 1, :], sc_f32[:, hs2, 1, :], AF.Exp,
            bias=ldt_tm[:, 2 * c + 1, h:h + 1]), inc=s_a)
        lastexp.append(v_exp)
        em.wait("gpsimd", s_a, v_exp)
        if h == 0:
            em.wait("gpsimd", s_v, cbt_mult[c])
        scorem[(c, h)] = em.op("gpsimd", lambda e_, hs=hs, hs2=hs2, msl=c % 2:
            e_.tensor_mul(scoresT[:, hs, :, :], exps[:, hs2, :, :], mcbt[:, msl, :, :]),
            inc=s_g)
        lastscorem.append(scorem[(c, h)])

    def emit_cbt(c):
        sl = c % 2
        sem_c = "s_c%d" % sl
        em.wait("tensor", sem_c, ch_ld[c])
        if c >= 2:
            em.wait("vector", s_g, scorem[(c - 2, NH - 1)])
        for st in range(2):
            if c >= 1 or st == 1:
                em.wait("tensor", s_v, cbt_mult[c] if st == 1 else cbt_mult[c - 1])

            def _cbt1(t, sl=sl, st=st):
                return t.matmul(psCBT[:, 0:CH], bc_c[:, sl, 0, st * 128:(st + 1) * 128],
                                bc_c[:, sl, 1, :], start=True, stop=True)
            v_cbt = em.op("tensor", _cbt1, inc=s_p)
            em.wait("vector", s_p, v_cbt)
            cbt_mult[c] = em.op("vector", lambda e_, st=st, msl=sl: e_.tensor_mul(
                mcbt[:, msl, st, :], psCBT[:, 0:CH], binmask[:, st, :]), inc=s_v)

    def emit_mms(c, h):
        sl = c % 2
        sem_c = "s_c%d" % sl
        hs = h % 4
        gi = c * NH + h
        em.wait("tensor", s_g, scorem[(c, h)])
        em.wait("tensor", s_g, cexpv[(c, h)])
        if c >= 1:
            em.wait("tensor", s_g, cast_done[(c - 1, h)])
        if gi >= 2:
            pk = gi - 2
            em.wait("tensor", s_v, y2ev[(pk // NH, pk % NH)])
            em.wait("tensor", s_v, scan_done[(pk // NH, pk % NH)])
        if h == 0:
            em.wait("tensor", sem_c, ch_ld[c])
            em.wait("tensor", s_v, v_diagd)
            em.wait("tensor", s_v, v_ddc)
        if h == 1 and c >= 1:
            em.wait("tensor", s_a, v_rstd[c - 1])
        if h == 0 and c == 0:
            em.wait("tensor", s_g, v_prevmem)

        def _hmms(t, sl=sl, h=h, hs=hs, ys=(h % 2) * 512):
            t.matmul(psYS[:, ys:ys + CH], xt_c[:, sl, 0, h * 128:(h + 1) * 128],
                     scoresT[:, hs, 0, :], start=True, stop=False)
            t.matmul(psYS[:, ys:ys + 128], xt_c[:, sl, 0, h * 128:(h + 1) * 128],
                     diagD[:, h, :], start=False, stop=False)
            t.matmul(psYS[:, ys + 256:ys + 384], bd[:, hs, 0, :],
                     xt_c[:, sl, 0, h * 128:(h + 1) * 128], start=False, stop=False)
            t.matmul(psYS[:, ys:ys + CH], xt_c[:, sl, 1, h * 128:(h + 1) * 128],
                     scoresT[:, hs, 1, :], start=False, stop=False)
            t.matmul(psYS[:, ys + 128:ys + CH], xt_c[:, sl, 1, h * 128:(h + 1) * 128],
                     diagD[:, h, :], start=False, stop=False)
            t.matmul(psYS[:, ys + 256:ys + 384], bd[:, hs, 1, :],
                     xt_c[:, sl, 1, h * 128:(h + 1) * 128], start=False, stop=False)
            return t.matmul(psYS[:, ys:ys + CH], prevb[:, h * 128:(h + 1) * 128],
                            cexp[:, hs, :], start=False, stop=True)
        mm_done[(c, h)] = em.op("tensor", _hmms, inc=s_p)

    def chunk_end(c):
        em.wait("tensor", s_a, sqv[(c, NH - 1)])
        em.wait("tensor", s_v, y2ev[(c, NH - 1)])
        em.wait("tensor", s_v, scan_done[(c, NH - 1)])
        for t2 in range(2):
            if c + t2 >= 1:
                em.wait("tensor", s_a, v_rstd.get(c - 1) if t2 == 0 else rstd0_done)

            def _rms(t, t2=t2):
                last = None
                for h in range(NH):
                    last = t.matmul(psYS[:, 896 + t2:897 + t2],
                                    sqb[:, h, t2 * 128:(t2 + 1) * 128],
                                    onescol_bf[:], start=(h == 0), stop=(h == NH - 1))
                return last
            v_rmsmm = em.op("tensor", _rms, inc=s_p)
            em.wait("scalar", s_p, v_rmsmm)
            em.op("scalar", lambda e_, t2=t2: e_.activation(
                rstd_tmp[:, 0:1], psYS[:, 896 + t2:897 + t2], AF.Ln,
                scale=1.0 / DI, bias=EPS))
            vv = em.op("scalar", lambda e_, t2=t2, c=c: e_.activation(
                rstd_all[:, c * 2 + t2:c * 2 + t2 + 1], rstd_tmp[:, 0:1], AF.Exp,
                scale=-0.5), inc=s_a, pre_drain=True)
            if t2 == 0:
                rstd0_done = vv
            else:
                v_rstd[c] = vv
        em.wait("sync", s_g, y3v[(c, NH - 1)])
        em.dma("sync", bass.AP(y3_sp, c * CH, [[T, 128], [128 * T, NH], [1, CH]]),
               y3b[:, c % 2, :, :], "s_y3")
        if "y2" in dbg:
            em.wait("sync", s_v, y2ev[(c, NH - 1)])
            em.dma("sync", bass.AP(dbg["y2"], c * CH, [[T, 128], [128 * T, NH], [1, CH]]),
                   y2b[:, :, :], "s_y2d")
        if c + 2 < NCH:
            em.wait("sync", s_p, mm_done[(c, NH - 1)])
            em.wait("sync", s_v, em.val(s_v))
            em.wait("sync", s_g, em.val(s_g))
            ch_ld[c + 2] = chload(c + 2)
        if c + 2 < 8:
            em.wait("gpsimd", s_p, rro[(c, 31)])
            wdv[c + 2] = wdload(c + 2)

    cbt_mult = {}
    rstd0_done = None
    emit_cbt(0)
    NGI = NCH * NH
    for gi in range(NGI + 2):
        if gi < NGI:
            if gi % NH == 13 and gi // NH + 1 < NCH:
                emit_cbt(gi // NH + 1)
            produce(gi // NH, gi % NH)
        gim = gi - 2
        if gim >= 0:
            c_m, h_m = divmod(gim, NH)
            emit_mms(c_m, h_m)
            if gim >= 1 and (gim - 1) % NH != NH - 1:
                pc, ph = divmod(gim - 1, NH)
                emit_consumers(pc, ph, pc % 2)
            ffn_step(c_m, h_m)
            if h_m == NH - 1:
                emit_consumers(c_m, h_m, c_m % 2)
                chunk_end(c_m)

    em.wait("tensor", s_a, sqv3[(7, 31)])
    em.wait("tensor", s_a, hev3[(7, 3, 1)])
    em.wait("tensor", s_g, v_of)
    v_lmm = em.op("tensor", lambda t: t.matmul(psH2[0:1, 0, 0:256], onescol_f[:],
                                               sqacc[:, :], start=True, stop=True),
                  inc=s_p)
    em.wait("vector", s_p, v_lmm)
    em.op("vector", lambda v: v.tensor_copy(loss_row[:], psH2[0:1, 0, 0:256]))
    em.op("vector", lambda v: v.tensor_reduce(
        losses8[:], loss_row.ap().rearrange("p (a b) -> p a b", a=8), AX.X, ALU.add),
        pre_drain=True)
    em.op("vector", lambda v: v.tensor_reduce(minv[:], losses8[:], AX.X, ALU.min),
          pre_drain=True)
    em.op("vector", lambda v: v.tensor_scalar(maskr[:], losses8[:], minv[0:1, 0:1], None,
                                              ALU.subtract), pre_drain=True)
    em.op("vector", lambda v: v.tensor_scalar(maskr[:], maskr[:], -1.0, 1.0,
                                              ALU.mult, ALU.add), pre_drain=True)
    v_mask = em.op("vector", lambda v: v.tensor_scalar_max(maskr[:], maskr[:], 0.0), inc=s_v,
                   pre_drain=True)
    em.wait("sync", s_v, v_mask)
    v_mask_sp = em.dma("sync", rows_sp[35:36, 0:8], maskr[:], s_h)
    em.wait("sync", s_h, v_mask_sp)
    em.wait("sync", "s_y3")
    em.end("ph2_ssm")
    pclose()

    popen()
    em.begin()
    y3h = sbp("y3h", [128, NJT, 1024], BF16)
    wout3 = sbp("wout3", [128, NJT, DM], BF16)
    wadn = sbp("wadn", [128, NKT, 8, 64], BF16)
    waup = sbp("waup", [64, 8, DM], BF16)
    baup_bf = sbp("baup_bf", [8, DM], BF16)
    mask_bc = sbp("mask_bc", [128, 8], F32)
    mask_col = sbp("mask_col", [8, 1], BF16)
    wadn_sel = sbp("wadn_sel", [128, NKT, 64], BF16)
    waup_sel = sbp("waup_sel", [64, DM], BF16)
    badn_col = sbp("badn_col", [64, 1], F32)
    badn_in = sbp("badn_in", [64, 8], F32)
    bsum = sbp("bsum", [1, DM], BF16)
    bbase_bf = sbp("bbase_bf", [1, DM], BF16)
    a_fm = sbp("a_fm", [64, T], BF16)
    wbase_sb = sbp("wbase_sb", [128, NKT, DM], BF16)
    g_rot = sbp("g_rot", [128, 2, DM], BF16)
    s_rot = sbp("s_rot", [128, 4, DM], BF16)
    u_rot = sbp("u_rot", [128, 2, DM], F32)
    o_rot = sbp("o_rot", [128, 2, DM], F32)
    psG = psp("psG", [128, 2, DM], F32)
    psA2 = psp("psA2", [64, 512], F32)
    psO = psp("psO", [128, 1024], F32)

    em.wait("sync", "s_wc")
    v_wo3 = em.dma("sync", wout3[:, :, :],
                   bass.AP(wout_bf, 0, [[DM, 128], [128 * DM, NJT], [1, DM]]), s_h)
    v_y3l = em.dma("sync", y3h[:, :, :],
                   bass.AP(y3_sp, 0, [[T, 128], [128 * T, NJT], [1, 1024]]), "s_yl")

    em.wait("sync", "s_wc")
    em.wait("scalar", "s_wc")
    v_wb = em.dma("scalar", wbase_sb[:, :, :],
                  bass.AP(wbase_bfd, 0, [[DM, 128], [128 * DM, NKT], [1, DM]]), "s_ts0")
    em.dma("scalar", wadn[:, :, :, :],
           bass.AP(wadn_bfd, 0, [[512, 128], [128 * 512, NKT], [64, 8], [1, 64]]),
           "s_ts0")
    em.dma("scalar", waup[:, :, :],
           bass.AP(waup_bfd, 0, [[8 * DM, 64], [DM, 8], [1, DM]]), "s_ts0")
    v_baup = em.dma("scalar", baup_bf[:, :], baup_bfd.ap(), "s_ts0")
    v_mbc = em.dma("sync", mask_bc[:, :], bass.AP(rows_sp, 35 * T, [[0, 128], [1, 8]]), s_h)
    v_mcol = em.dma("gpsimd", mask_col[:, :], bass.AP(rows_sp, 35 * T, [[1, 8], [1, 1]]), s_d)
    em.dma("sync", badn_in[:, :], ba_down_c[:, :], s_h)
    v_bb = em.dma("gpsimd", bbase_bf[:, :], b_base[:, :], s_d)

    sevb = {}

    def emit_s(tt):
        em.wait("tensor", "s_yl", 16 if tt < 8 else 32)
        em.wait("tensor", s_h)
        if tt >= 1:
            em.wait("tensor", s_v, sevb[tt - 1])
        if tt == 2:
            em.wait("tensor", s_v, v_bsum)

        def _omm(t, tq=tt % 8):
            last = None
            for jt in range(NJT):
                for dc2 in range(2):
                    last = t.matmul(psO[:, dc2 * 512:(dc2 + 1) * 512],
                                    y3h[:, jt, tq * 128:(tq + 1) * 128],
                                    wout3[:, jt, dc2 * 512:(dc2 + 1) * 512],
                                    start=(jt == 0), stop=(jt == NJT - 1))
            return last
        v_om = em.op("tensor", _omm, inc=s_p)
        em.wait("vector", s_p, v_om)
        sevb[tt] = em.op("vector", lambda v, tt=tt: v.tensor_scalar(
            s_rot[:, tt % 4, :], psO[:, :], rstd_all[:, tt:tt + 1], None, ALU.mult),
            inc=s_v)
        if "s" in dbg:
            em.wait("sync", s_v, sevb[tt])
            em.dma("sync", dbg["s"][tt * 128:(tt + 1) * 128, :], s_rot[:, tt % 4, :],
                   "s_y2d")
        if tt == 7:
            em.wait("sync", s_p, v_om)
            em.dma("sync", y3h[:, :, :],
                   bass.AP(y3_sp, 1024, [[T, 128], [128 * T, NJT], [1, 1024]]), "s_yl")

    emit_s(0)
    emit_s(1)
    bumm_emitted = [False]
    em.wait("vector", s_d, em.val(s_d))
    em.wait("vector", "s_ts0")
    em.wait("vector", s_h, em.val(s_h))
    em.op("vector", lambda v: v.tensor_scalar(
        wadn_sel[:, :, :], wadn[:, :, 0, :], mask_bc[:, 0:1], None, ALU.mult))
    for e in range(1, 8):
        v_wsel = em.op("vector", lambda v, e=e: v.scalar_tensor_tensor(
            wadn_sel[:, :, :], wadn[:, :, e, :], mask_bc[:, e:e + 1],
            wadn_sel[:, :, :], ALU.mult, ALU.add),
            inc=s_v if e == 7 else None, pre_drain=True)
    em.op("vector", lambda v: v.tensor_scalar(
        waup_sel[:, :], waup[:, 0, :], mask_bc[0:64, 0:1], None, ALU.mult))
    for e in range(1, 8):
        em.op("vector", lambda v, e=e: v.scalar_tensor_tensor(
            waup_sel[:, :], waup[:, e, :], mask_bc[0:64, e:e + 1],
            waup_sel[:, :], ALU.mult, ALU.add), pre_drain=True)
    em.op("vector", lambda v: v.tensor_scalar(
        badn_col[:, :], badn_in[:, 0:1], mask_bc[0:64, 0:1], None, ALU.mult))
    for e in range(1, 8):
        v_gsel = em.op("vector", lambda v, e=e: v.scalar_tensor_tensor(
            badn_col[:, :], badn_in[:, e:e + 1], mask_bc[0:64, e:e + 1],
            badn_col[:, :], ALU.mult, ALU.add), inc=s_v if e == 7 else None,
            pre_drain=True)
    em.wait("tensor", s_d)
    em.wait("tensor", "s_ts0")
    em.wait("tensor", s_v, sevb[1])
    def _bumm(t):
        t.matmul(psO[0:1, 0:512], mask_col[:, :], baup_bf[:, 0:512], start=True, stop=True)
        return t.matmul(psO[0:1, 512:1024], mask_col[:, :], baup_bf[:, 512:1024],
                        start=True, stop=True)
    v_bu = em.op("tensor", _bumm, inc=s_p)
    em.wait("vector", s_p, v_bu)
    em.wait("vector", s_d, em.val(s_d))
    v_bsum = em.op("vector", lambda v: v.tensor_add(bsum[:], psO[0:1, :], bbase_bf[:]),
                   inc=s_v)

    for nt in range(4):
        em.wait("tensor", s_v, v_wsel)

        def _amm(t, nt=nt):
            last = None
            for kt in range(NKT):
                last = t.matmul(psA2[:, :], wadn_sel[:, kt, :],
                                xn_fm[:, kt, nt * 512:(nt + 1) * 512],
                                start=(kt == 0), stop=(kt == NKT - 1))
            return last
        if nt >= 1:
            em.wait("tensor", s_v, v_aev)
        v_am = em.op("tensor", _amm, inc=s_p)
        em.wait("vector", s_p, v_am)
        em.wait("vector", s_v, v_gsel)
        v_aev = em.op("vector", lambda v, nt=nt: v.tensor_scalar(
            a_fm[:, nt * 512:(nt + 1) * 512], psA2[:, :], badn_col[:, 0:1], 0.0,
            ALU.add, ALU.max), inc=s_v)

    vo_done = [None, None]
    gst = {}
    ldv = {}

    def _fld(tt):
        slt = tt % 2
        su = "s_fu%d" % slt
        if vo_done[slt] is not None:
            em.wait("sync", s_v, vo_done[slt])
        ldv[tt] = em.dma("sync", u_rot[:, slt, :], u[tt * 128:(tt + 1) * 128, :], su)
    _fld(0)
    for tt in range(NTT):
        slt = tt % 2
        su, so = "s_fu%d" % slt, "s_fo%d" % slt
        if tt + 1 < NTT:
            _fld(tt + 1)
        v_s = ldv[tt]
        if tt + 2 < NTT:
            emit_s(tt + 2)
        em.wait("tensor", s_d)
        em.wait("tensor", s_v, v_aev)
        em.wait("tensor", s_v, v_bsum)
        if tt >= 2:
            em.wait("tensor", s_a, gst[tt - 2])

        def _gmm(t, tt=tt, slt=slt):
            last = None
            for oc in range(2):
                for kt in range(NKT):
                    t.matmul(psG[:, slt, oc * 512:(oc + 1) * 512],
                             xn_fm[:, kt, tt * 128:(tt + 1) * 128],
                             wbase_sb[:, kt, oc * 512:(oc + 1) * 512],
                             start=(kt == 0), stop=False)
                t.matmul(psG[:, slt, oc * 512:(oc + 1) * 512],
                         a_fm[:, tt * 128:(tt + 1) * 128],
                         waup_sel[:, oc * 512:(oc + 1) * 512], start=False, stop=False)
                last = t.matmul(psG[:, slt, oc * 512:(oc + 1) * 512],
                                onesrow_bf[:, :], bsum[0:1, oc * 512:(oc + 1) * 512],
                                start=False, stop=True)
            return last
        v_gm = em.op("tensor", _gmm, inc=s_p)
        em.wait("scalar", s_p, v_gm)
        if tt >= 2:
            em.wait("scalar", s_v, vo_done[slt])
        if "g" in dbg and tt >= 2:
            em.wait("scalar", so)
        gst[tt] = em.op("scalar", lambda s, slt=slt: s.activation(
            g_rot[:, slt, :], psG[:, slt, :], AF.Silu), inc=s_a)
        em.wait("vector", s_a, gst[tt])
        em.wait("vector", su, v_s)
        if tt >= 2:
            em.wait("vector", so)
        em.op("vector", lambda v, slt=slt, tt=tt: v.tensor_mul(
            o_rot[:, slt, :], s_rot[:, tt % 4, :], g_rot[:, slt, :]), pre_drain=True)
        vo_done[slt] = em.op("vector", lambda v, slt=slt: v.tensor_add(
            o_rot[:, slt, :], o_rot[:, slt, :], u_rot[:, slt, :]), inc=s_v,
            pre_drain=True)
        em.wait("sync", s_v, vo_done[slt])
        em.dma("sync", out[tt * 128:(tt + 1) * 128, :], o_rot[:, slt, :], so)
        if "g" in dbg:
            em.dma("sync", dbg["g"][tt * 128:(tt + 1) * 128, :], g_rot[:, slt, :], so)
    em.wait("sync", "s_fo0")
    em.wait("sync", "s_fo1")
    em.end("ph3b_adapter")
    pclose()

    ctx.close()
    return nc


def _stage_inputs(inputs):
    asC = np.ascontiguousarray
    W_inT = asC(inputs["L0_attn_W_in"].T)
    W_outT = asC(inputs["L0_attn_W_out"].T)
    conv_w_c = asC(inputs["L0_attn_conv_w"].reshape(18, 128, DCONV).transpose(1, 2, 0))
    conv_b_c = asC(inputs["L0_attn_conv_b"].reshape(18, 128).T)
    dt_bias = asC(inputs["L0_attn_dt_bias"].reshape(NH, 1))
    A_log = asC(inputs["L0_attn_A_log"].reshape(NH, 1))
    Dvec = asC(inputs["L0_attn_D"].reshape(1, NH))
    nw_c = asC(inputs["L0_attn_norm_weight"].reshape(NJT, 128).T)
    b_base = asC(inputs["L0_ffn_b_base"].reshape(1, DM))
    bd1_c = asC(inputs["L0_ffn_bd1"].reshape(8, 2, 128).transpose(2, 1, 0))
    bd2_c = asC(inputs["L0_ffn_bd2"].reshape(8, 8, 128).transpose(2, 1, 0))
    Wa_down_r = asC(inputs["L0_ffn_Wa_down"].transpose(1, 0, 2))
    Wa_up_r = asC(inputs["L0_ffn_Wa_up"].transpose(1, 0, 2))
    ba_down_c = asC(inputs["L0_ffn_ba_down"].T)
    ba_up = asC(inputs["L0_ffn_ba_up"])
    shared = dict(
        W_inT=W_inT, W_outT=W_outT, conv_w_c=conv_w_c, conv_b_c=conv_b_c,
        dt_bias=dt_bias, A_log=A_log, Dvec=Dvec, nw_c=nw_c,
        W_base=asC(inputs["L0_ffn_W_base"]), b_base=b_base,
        Wd1=asC(inputs["L0_ffn_Wd1"]), Wd2=asC(inputs["L0_ffn_Wd2"]),
        bd1_c=bd1_c, bd2_c=bd2_c, Wa_down_r=Wa_down_r, Wa_up_r=Wa_up_r,
        ba_down_c=ba_down_c, ba_up=ba_up,
    )
    in_maps = []
    for i in range(8):
        ui = asC(inputs["L0_attn_u"][i])
        m = dict(shared)
        m["u"] = ui
        m["uT"] = asC(ui.T)
        in_maps.append(m)
    return in_maps


def kernel(**inputs):
    in_maps = _stage_inputs(inputs)
    nc = build_program()
    res = run_bass_kernel_spmd(nc, in_maps, core_ids=list(range(8)))
    return np.stack([r["out"] for r in res.results], axis=0)


if __name__ == "__main__":
    pass


# version 27
# speedup vs baseline: 1.0632x; 1.0051x over previous
import numpy as np
from contextlib import ExitStack

import concourse.bass as bass
import concourse.mybir as mybir
from concourse.bass_utils import run_bass_kernel_spmd

F32 = mybir.dt.float32
BF16 = mybir.dt.bfloat16
AF = mybir.ActivationFunctionType
ALU = mybir.AluOpType
AX = mybir.AxisListType

T = 2048
DM = 1024
DI = 2048
DS = 128
NH = 16
HD = 128
DCONV = 4
CH = 256
NCH = T // CH
CONVD = DI + 2 * DS
DPROJ = 2 * DI + 2 * DS + NH
NMT = 35
EPS = 1e-5
NKT = DM // 128
NTT = T // 128
NJT = DI // 128


class Emitter:

    ENGINES = ("tensor", "vector", "scalar", "gpsimd", "sync")

    def __init__(self, nc, ctx):
        self.nc = nc
        self.ctx = ctx
        self.sems = {}
        self.q = None
        self._opened = False

    def sem(self, name):
        if name not in self.sems:
            h = self.ctx.enter_context(self.nc.semaphore(name))
            self.sems[name] = [h, 0]
        return self.sems[name]

    def plan_inc(self, name, k=1):
        s = self.sem(name)
        s[1] += k
        return s[1]

    def val(self, name):
        return self.sem(name)[1]

    def begin(self):
        assert self.q is None
        self.q = {e: [] for e in self.ENGINES}
        self._since_drain = {e: 0 for e in self.ENGINES}

    def end(self, name=None):
        q = self.q
        self.q = None
        with self.nc.Block(name) as blk:
            for e in self.ENGINES:
                if q[e]:
                    getattr(blk, e)(lambda eng, ops=tuple(q[e]): [op(eng) for op in ops])

    def add(self, engine, fn):
        self.q[engine].append(fn)

    def wait(self, engine, sem_name, v=None):
        v = self.val(sem_name) if v is None else v
        h = self.sem(sem_name)[0]
        self.q[engine].append(lambda eng: eng.wait_ge(h, v))

    def drain(self, engine):
        self.q[engine].append(lambda eng: eng.drain())
        self._since_drain[engine] = 0

    def op(self, engine, fn, inc=None, inck=1, autodrain=False, pre_drain=False):
        if pre_drain:
            self.drain(engine)
        if inc is None:
            self.q[engine].append(lambda eng: fn(eng))
            return None
        v = self.plan_inc(inc, inck)
        h = self.sem(inc)[0]
        self.q[engine].append(lambda eng: fn(eng).then_inc(h, inck))
        return v

    def dma(self, engine, out_ap, in_ap, inc, **kw):
        v = self.plan_inc(inc, 16)
        h = self.sem(inc)[0]
        self.q[engine].append(
            lambda eng: eng.dma_start(out=out_ap, in_=in_ap, **kw).then_inc(h, 16))
        return v


def bcast_read(shape_free_ap_dims, src_tensor, src_offset_elems, free_dims):
    return bass.AP(src_tensor, src_offset_elems, [[0, 128]] + list(free_dims))


def build_program(dbg_taps=()):
    nc = bass.Bass("TRN2", target_bir_lowering=False)
    dt = nc.dram_tensor

    u = dt("u", [T, DM], F32, kind="ExternalInput")
    uT = dt("uT", [DM, T], F32, kind="ExternalInput")
    W_inT = dt("W_inT", [DM, DPROJ], F32, kind="ExternalInput")
    conv_w_c = dt("conv_w_c", [128, DCONV, 18], F32, kind="ExternalInput")
    conv_b_c = dt("conv_b_c", [128, 18], F32, kind="ExternalInput")
    dt_bias = dt("dt_bias", [NH, 1], F32, kind="ExternalInput")
    A_log = dt("A_log", [NH, 1], F32, kind="ExternalInput")
    Dvec = dt("Dvec", [1, NH], F32, kind="ExternalInput")
    nw_c = dt("nw_c", [128, NJT], F32, kind="ExternalInput")
    W_outT = dt("W_outT", [DI, DM], F32, kind="ExternalInput")
    W_base = dt("W_base", [DM, DM], F32, kind="ExternalInput")
    b_base = dt("b_base", [1, DM], F32, kind="ExternalInput")
    Wd1 = dt("Wd1", [8, DM, 256], F32, kind="ExternalInput")
    Wd2 = dt("Wd2", [8, 256, DM], F32, kind="ExternalInput")
    bd1_c = dt("bd1_c", [128, 2, 8], F32, kind="ExternalInput")
    bd2_c = dt("bd2_c", [128, 8, 8], F32, kind="ExternalInput")
    Wa_down_r = dt("Wa_down_r", [DM, 8, 64], F32, kind="ExternalInput")
    Wa_up_r = dt("Wa_up_r", [64, 8, DM], F32, kind="ExternalInput")
    ba_down_c = dt("ba_down_c", [64, 8], F32, kind="ExternalInput")
    ba_up = dt("ba_up", [8, DM], F32, kind="ExternalInput")

    out = dt("out", [T, DM], F32, kind="ExternalOutput")

    z_sp = dt("z_sp", [DI, T], BF16)
    xc_sp = dt("xc_sp", [CONVD, T], BF16)
    y3_sp = dt("y3_sp", [DI, T], BF16)
    wout_bf = dt("wout_bf", [DI, DM], BF16)
    wbase_bfd = dt("wbase_bfd", [DM, DM], BF16)
    wadn_bfd = dt("wadn_bfd", [DM, 8, 64], BF16)
    waup_bfd = dt("waup_bfd", [64, 8, DM], BF16)
    baup_bfd = dt("baup_bfd", [8, DM], BF16)
    xt_sp = dt("xt_sp", [T, DI], BF16)
    bt_sp = dt("bt_sp", [T, DS], BF16)
    rows_sp = dt("rows_sp", [40, T], F32)

    dbg = {}
    for name, shape, dtp in (
        ("xn", [DM, T], BF16), ("dtv", [NH, T], F32), ("acs", [NH, T], F32),
        ("xconv", [CONVD, T], BF16), ("z", [DI, T], BF16), ("s", [T, DM], BF16),
        ("g", [T, DM], BF16), ("y2", [DI, T], BF16),
    ):
        if name in dbg_taps:
            dbg[name] = dt("dbg_" + name, shape, dtp, kind="ExternalOutput")

    ctx = ExitStack()
    em = Emitter(nc, ctx)
    sb = lambda n_, s_, dtp=F32: ctx.enter_context(nc.sbuf_tensor(n_, s_, dtp))
    ps = lambda n_, s_, dtp=F32: ctx.enter_context(nc.psum_tensor(n_, s_, dtp))
    pctx = [None]

    def popen():
        pctx[0] = ExitStack()

    def pclose():
        pctx[0].close()
        pctx[0] = None

    sbp = lambda n_, s_, dtp=F32: pctx[0].enter_context(nc.sbuf_tensor(n_, s_, dtp))
    psp = lambda n_, s_, dtp=F32: pctx[0].enter_context(nc.psum_tensor(n_, s_, dtp))

    const_tiles = []

    def need_const(v):
        try:
            nc.const_aps.tensor(v, [128, 1], F32)
        except AssertionError:
            h = ctx.enter_context(nc.sbuf_tensor("cst_%d" % len(const_tiles), [128, 1], F32))
            nc.const_aps.register(F32, v, h[:])
            const_tiles.append((h, v))
    for v in (EPS, -0.5, -1.0, 1.0, 0.0):
        need_const(v)

    onescol_bf = sb("onescol_bf", [128, 1], BF16)
    onesrow_bf = sb("onesrow_bf", [1, 128], BF16)
    id128_bf = sb("id128_bf", [128, 128], BF16)
    id16_f = sb("id16_f", [16, 16], F32)
    id128_f = sb("id128_f", [128, 128], F32)
    convw = sb("convw", [128, DCONV, 18], F32)
    convb = sb("convb", [128, 18], F32)
    dtb_col = sb("dtb_col", [NH, 1], F32)
    alog_col = sb("alog_col", [NH, 1], F32)
    negA = sb("negA", [NH, 1], F32)
    nwcol = sb("nwcol", [128, NJT], F32)
    Dbc = sb("Dbc", [128, NH], F32)
    diagD = sb("diagD", [128, NH, 128], BF16)
    cd_fm = sb("cd_fm", [NH, NCH], F32)
    bd1_cs = sb("bd1_cs", [128, 2, 8], F32)
    rstd_all = sb("rstd_all", [128, NTT], F32)
    bd2_cs = sb("bd2_cs", [128, 8, 8], F32)
    negid = sb("negid", [128, 128], BF16)
    dt_tm = sb("dt_tm", [128, NTT, NH], F32)
    acs_tm = sb("acs_tm", [128, NTT, NH], F32)
    dec_tm = sb("dec_tm", [128, NTT, NH], F32)
    ddc_tm = sb("ddc_tm", [128, NTT, NH], F32)
    ldt_tm = sb("ldt_tm", [128, NTT, NH], F32)
    cd_bc = sb("cd_bc", [128, NH, NCH], F32)
    binmask = sb("binmask", [128, 2, CH], BF16)

    s_d = "s_d"
    s_h = "s_h"
    s_p = "s_p"
    s_v = "s_v"
    s_a = "s_a"
    s_g = "s_g"

    xn_fm = ctx.enter_context(nc.sbuf_tensor("xn_fm", [128, NKT, T], BF16, side="right"))
    popen()
    em.begin()
    g_setup_done = None

    def _setup(g):
        for h, v in const_tiles:
            g.memset(h[:], v)
        g.memset(onescol_bf[:], 1.0)
        g.memset(onesrow_bf[:], 1.0)
        g.memset(id128_bf[:], 1.0)
        g.drain()
        g.affine_select(id128_bf[:], id128_bf[:], [[-1, 128]], ALU.is_equal, 0.0,
                        base=0, channel_multiplier=1)
        g.memset(id16_f[:], 1.0)
        g.drain()
        g.affine_select(id16_f[:], id16_f[:], [[-1, 16]], ALU.is_equal, 0.0,
                        base=0, channel_multiplier=1)
        g.memset(id128_f[:], 1.0)
        g.drain()
        g.affine_select(id128_f[:], id128_f[:], [[-1, 128]], ALU.is_equal, 0.0,
                        base=0, channel_multiplier=1)
        g.memset(binmask[:], 1.0)
        g.drain()
        g.affine_select(binmask[:, 0, :], binmask[:, 0, :], [[1, CH]], ALU.is_ge, 0.0,
                        base=0, channel_multiplier=-1)
        g.drain()
        return g.affine_select(binmask[:, 1, :], binmask[:, 1, :], [[1, CH]], ALU.is_ge, 0.0,
                               base=-128, channel_multiplier=-1)
    g_setup_done = em.op("gpsimd", _setup, inc=s_g)

    em.dma("sync", convw[:], conv_w_c[:, :, :], s_h)
    em.dma("sync", convb[:], conv_b_c[:, :], s_h)
    em.dma("sync", bd1_cs[:, :, :], bd1_c[:, :, :], s_h)
    em.dma("sync", bd2_cs[:, :, :], bd2_c[:, :, :], s_h)
    em.dma("sync", dtb_col[:], dt_bias[:, :], s_h)
    em.dma("sync", alog_col[:], A_log[:, :], s_h)
    em.dma("sync", nwcol[:], nw_c[:, :], s_h)
    v_dbc = em.dma("sync", Dbc[:], bass.AP(Dvec, 0, [[0, 128], [1, NH]]), s_h)

    u_f32 = sbp("u_f32", [128, NKT, T], F32)
    usq = sbp("usq", [128, 2, T], F32)
    onescol_f0 = sbp("onescol_f0", [128, 1], F32)
    v_o1 = em.op("gpsimd", lambda e: e.memset(onescol_f0[:], 1.0), inc=s_g)

    kv = {}
    for k in range(NKT):
        qe = "sync" if k % 2 == 0 else "scalar"
        sm = "s_u%d" % k
        kv[k] = (sm, em.dma(qe, u_f32[:, k, :],
                            bass.AP(uT, k * 128 * T, [[T, 128], [1, T]]), sm))

    psStat = psp("psStat", [1, 2048], F32)
    psStat2 = psp("psStat2", [1, 2048], F32)
    em.wait("tensor", s_g, g_setup_done)
    em.wait("tensor", s_g, v_o1)
    sq_mm_v = {}
    for k in range(NKT):
        em.wait("tensor", kv[k][0], kv[k][1])

        def _mean_k(t, k=k):
            last = None
            for nt in range(4):
                last = t.matmul(psStat[0:1, nt * 512:(nt + 1) * 512], onescol_f0[:],
                                u_f32[:, k, nt * 512:(nt + 1) * 512],
                                start=(k == 0), stop=(k == NKT - 1))
            return last
        em.op("tensor", _mean_k, inc=s_p if k == NKT - 1 else None)
        em.wait("scalar", kv[k][0], kv[k][1])
        if k >= 2:
            em.wait("scalar", s_p, sq_mm_v[k - 2])
        vsq = em.op("scalar", lambda e, k=k: e.activation(
            usq[:, k % 2, :], u_f32[:, k, :], AF.Square), inc=s_a)
        em.wait("tensor", s_a, vsq)

        def _sq_k(t, k=k):
            last = None
            for nt in range(4):
                last = t.matmul(psStat2[0:1, nt * 512:(nt + 1) * 512], onescol_f0[:],
                                usq[:, k % 2, nt * 512:(nt + 1) * 512],
                                start=(k == 0), stop=(k == NKT - 1))
            return last
        sq_mm_v[k] = em.op("tensor", _sq_k, inc=s_p)
    v_stat_mms = em.val(s_p)
    mean_row = sbp("mean_row", [1, T], F32)
    em.wait("vector", s_p, v_stat_mms)
    v_meanrow = em.op("vector", lambda e: e.tensor_scalar_mul(mean_row[:], psStat[0:1, :], 1.0 / DM),
                      inc=s_v)
    rstd_row = sbp("rstd_row", [1, T], F32)
    em.op("vector", lambda e: e.scalar_tensor_tensor(
        rstd_row[:], mean_row[:], -1.0, mean_row[:], ALU.mult, ALU.mult),
        pre_drain=True)
    v_var = em.op("vector", lambda e: e.scalar_tensor_tensor(
        rstd_row[:], psStat2[0:1, :], 1.0 / DM, rstd_row[:], ALU.mult, ALU.add), inc=s_v,
        pre_drain=True)
    em.wait("scalar", s_v, v_var)
    em.wait("scalar", s_g, g_setup_done)
    em.op("scalar", lambda e: e.activation(rstd_row[:], rstd_row[:], AF.Ln, bias=EPS))
    v_rstd = em.op("scalar", lambda e: e.activation(rstd_row[:], rstd_row[:], AF.Exp, scale=-0.5),
                   inc=s_a, pre_drain=True)
    em.wait("sync", s_v, v_meanrow)
    em.wait("sync", s_a, v_rstd)
    em.dma("sync", rows_sp[0:1, :], mean_row[:], s_h)
    v_rows = em.dma("sync", rows_sp[1:2, :], rstd_row[:], s_h)
    mean_bc = sbp("mean_bc", [128, T], F32)
    rstd_bc = sbp("rstd_bc", [128, T], F32)
    em.wait("sync", s_h, v_rows)
    em.dma("sync", mean_bc[:], bass.AP(rows_sp, 0, [[0, 128], [1, T]]), s_h)
    v_bc = em.dma("sync", rstd_bc[:], bass.AP(rows_sp, T, [[0, 128], [1, T]]), s_h)

    em.wait("vector", s_h, v_bc)
    em.wait("gpsimd", s_h, v_bc)
    vsubk = {}
    for k in range(NKT):
        vsubk[k] = em.op("vector", lambda e, k=k: e.tensor_sub(
            u_f32[:, k, :], u_f32[:, k, :], mean_bc[:]), inc=s_v)
    for k in range(NKT):
        em.wait("gpsimd", s_v, vsubk[k])
        em.op("gpsimd", lambda e, k=k: e.tensor_mul(
            xn_fm[:, k, :], u_f32[:, k, :], rstd_bc[:]), inc=s_g)
    v_xn_v = em.val(s_v)
    v_xn_g = em.val(s_g)
    em.wait("sync", s_v, v_xn_v)
    em.wait("sync", s_g, v_xn_g)
    if "xn" in dbg:
        for k in range(NKT):
            em.dma("sync", dbg["xn"][k * 128:(k + 1) * 128, :], xn_fm[:, k, :], s_h)
    em.end("ph0_ln")
    pclose()

    mctx = ExitStack()
    sbm = lambda n_, s_, dtp=F32: mctx.enter_context(nc.sbuf_tensor(n_, s_, dtp))
    dt_fm = sbm("dt_fm", [NH, T], F32)
    acs_fm = sbm("acs_fm", [NH, T], F32)
    row16a = sbm("row16a", [NH, T], F32)
    row16b = sbm("row16b", [NH, T], F32)
    chunkmask = sbm("chunkmask", [NH, T], F32)

    popen()
    em.begin()
    GW = 4
    NG = 9
    wbuf = sbp("wbuf", [128, 3, NKT, GW * 128], BF16)
    wstg = sbp("wstg", [128, 2, NKT, GW * 128], F32)
    ztile = sbp("ztile", [128, 2, T], BF16)
    xbc_sb = sbp("xbc_sb", [128, 2, 4 + T], BF16)
    tapp = sbp("tapp", [128, 2, 4, T], BF16)
    xctile = sbp("xctile", [128, 2, T], BF16)
    psA = psp("psA", [128, T], F32)
    psB = psp("psB", [128, T], F32)
    psum = (psA, psB)

    def _cmask(g):
        g.memset(chunkmask[:], 1.0)
        g.drain()
        last = None
        for c in range(NCH):
            last = g.memset(chunkmask[:, c * CH:c * CH + 1], 0.0)
        return last
    v_cmask = em.op("gpsimd", _cmask, inc=s_g)

    em.wait("vector", s_h)
    def _diagd(v):
        last = None
        for h in range(NH):
            last = v.tensor_scalar(diagD[:, h, :], id128_bf[:], Dbc[:, h:h + 1], None, ALU.mult)
        return last
    v_diagd = em.op("vector", _diagd, inc=s_v)

    def wload(gi):
        sl = gi % 2
        j0 = gi * GW * 128
        w = min(GW * 128, DPROJ - j0)
        em.dma("sync", wstg[:, sl, 0:4, 0:w],
               bass.AP(W_inT, j0, [[DPROJ, 128], [DPROJ * 128, 4], [1, w]]),
               "s_q%d" % sl)
        return em.dma("scalar", wstg[:, sl, 4:8, 0:w],
                      bass.AP(W_inT, 4 * 128 * DPROJ + j0,
                              [[DPROJ, 128], [DPROJ * 128, 4], [1, w]]),
                      "s_q%d" % sl)

    wv = {}
    castv = {}
    em.dma("gpsimd", wout_bf.ap(), W_outT.ap(), "s_wc")
    em.dma("gpsimd", wbase_bfd.ap(), W_base.ap(), "s_wc")
    em.dma("gpsimd", wadn_bfd.ap(), Wa_down_r.ap(), "s_wc")
    em.dma("gpsimd", waup_bfd.ap(), Wa_up_r.ap(), "s_wc")
    em.dma("gpsimd", baup_bfd.ap(), ba_up.ap(), "s_wc")

    def wcast(gi):
        sl = gi % 2
        bsl = gi % 3
        w = min(GW * 128, DPROJ - gi * GW * 128)
        em.wait("vector", "s_q%d" % sl, wv[gi])
        if gi >= 3:
            em.wait("vector", s_p, pe_done[(gi - 2) * GW - 1])
        castv[gi] = em.op("vector", lambda e, bsl=bsl, sl=sl, w=w: e.tensor_copy(
            wbuf[:, bsl, :, 0:w], wstg[:, sl, :, 0:w]), inc=s_v)

    wv[0] = wload(0)
    wv[1] = wload(1)
    wcast(0)
    wcast(1)
    em.wait("sync", s_v, castv[0])
    em.wait("scalar", s_v, castv[0])
    wv[2] = wload(2)
    wcast(2)

    pe_done = {}
    evac_done = {}
    silu_done = {}
    tap_done = {}
    v_dtevac_a = v_dtevac_v = None

    def emit_silu(mt):
        esl = mt % 2
        cm = mt - 16
        em.wait("scalar", tap_done[mt][0], tap_done[mt][1])
        if cm >= 2:
            em.wait("scalar", "s_x%d" % esl)
        silu_done[mt] = em.op(
            "scalar", lambda e, esl=esl, cm=cm: e.activation(
                xctile[:, esl, :], tapp[:, esl, 1, :], AF.Silu,
                bias=convb[:, cm:cm + 1]), inc=s_a)
        em.wait("sync", s_a, silu_done[mt])
        em.dma("sync", xc_sp[cm * 128:(cm + 1) * 128, :], xctile[:, esl, :], "s_x%d" % esl)
        if "xconv" in dbg:
            em.dma("sync", dbg["xconv"][cm * 128:(cm + 1) * 128, :], xctile[:, esl, :],
                   "s_x%d" % esl)

    for mt in range(NMT):
        gi = mt // GW
        sl = gi % 2
        moff = (mt % GW) * 128
        mw = 128 if mt < NMT - 1 else 16
        pslot = psum[mt % 2]
        em.wait("tensor", s_v, castv[gi])
        if mt == 0:
            em.wait("tensor", s_v, v_xn_v)
            em.wait("tensor", s_g, v_xn_g)
        if mt >= 2 and evac_done.get(mt - 2):
            sname, val = evac_done[mt - 2]
            em.wait("tensor", sname, val)

        def _mms(t, bsl=gi % 3, moff=moff, mw=mw, pslot=pslot):
            last = None
            for kt in range(NKT):
                for nt in range(4):
                    last = t.matmul(pslot[0:mw, nt * 512:(nt + 1) * 512],
                                    wbuf[:, bsl, kt, moff:moff + mw],
                                    xn_fm[:, kt, nt * 512:(nt + 1) * 512],
                                    start=(kt == 0), stop=(kt == NKT - 1))
            return last
        pe_done[mt] = em.op("tensor", _mms, inc=s_p)
        if mt % GW == 0 and gi >= 1 and gi + 2 < NG:
            em.wait("sync", s_v, castv[gi])
            em.wait("scalar", s_v, castv[gi])
            wv[gi + 2] = wload(gi + 2)
            wcast(gi + 2)

        esl = mt % 2
        if mt < 16:
            em.wait("scalar", s_p, pe_done[mt])
            if mt >= 2:
                em.wait("scalar", "s_z%d" % esl)
            silu_done[mt] = em.op(
                "scalar", lambda e, esl=esl, pslot=pslot: e.activation(
                    ztile[:, esl, :], pslot[:, :], AF.Silu), inc=s_a)
            evac_done[mt] = (s_a, silu_done[mt])
            em.wait("sync", s_a, silu_done[mt])
            em.dma("sync", z_sp[mt * 128:(mt + 1) * 128, :], ztile[:, esl, :], "s_z%d" % esl)
            if "z" in dbg:
                em.dma("sync", dbg["z"][mt * 128:(mt + 1) * 128, :], ztile[:, esl, :],
                       "s_z%d" % esl)
        elif mt < 34:
            cm = mt - 16
            if cm >= 2 and tap_done.get(mt - 2):
                sname, val = tap_done[mt - 2]
                em.wait("vector", sname, val)
            if cm == 0:
                em.op("gpsimd", lambda g: g.memset(xbc_sb[:, 0, 0:4], 0.0))
                em.op("gpsimd", lambda g: g.memset(xbc_sb[:, 1, 0:4], 0.0), inc=s_g)
            em.wait("scalar", s_p, pe_done[mt])
            ev = em.op("scalar", lambda e, esl=esl, pslot=pslot: e.activation(
                xbc_sb[:, esl, 4:4 + T], pslot[0:128, :], AF.Copy), inc=s_a)
            evac_done[mt] = (s_a, ev)
            if cm == 0:
                em.wait("vector", s_g)
            em.wait("vector", s_a, ev)
            for k in range(4):
                em.op("vector", lambda e, esl=esl, cm=cm, k=k: e.tensor_scalar(
                    tapp[:, esl, k, :], xbc_sb[:, esl, 1 + k:1 + k + T],
                    convw[:, k, cm:cm + 1], None, ALU.mult),
                    inc=s_v if k == 3 else None)
            v_pp = em.val(s_v)
            em.wait("gpsimd", s_v, v_pp)
            if cm >= 2:
                em.wait("gpsimd", s_a)
            v_ts = em.op("gpsimd", lambda g, esl=esl: g.tensor_add(
                tapp[:, esl, 0, :], tapp[:, esl, 0, :], tapp[:, esl, 1, :]), inc=s_g)
            em.op("vector", lambda e, esl=esl: e.tensor_add(
                tapp[:, esl, 2, :], tapp[:, esl, 2, :], tapp[:, esl, 3, :]),
                pre_drain=True)
            em.wait("vector", s_g, v_ts)
            tv = em.op("vector", lambda e, esl=esl: e.tensor_add(
                tapp[:, esl, 1, :], tapp[:, esl, 0, :], tapp[:, esl, 2, :]), inc=s_v,
                pre_drain=True)
            tap_done[mt] = (s_v, tv)
            if mt > 16:
                emit_silu(mt - 1)
            if mt == 33:
                emit_silu(33)
        else:
            em.wait("scalar", s_p, pe_done[mt])
            em.wait("scalar", s_h, em.val(s_h))
            v_dtevac_a = em.op("scalar", lambda e, pslot=pslot: e.activation(
                row16a[:, :], pslot[0:16, :], AF.Abs, bias=dtb_col[:, 0:1]), inc=s_a)
            em.wait("vector", s_p, pe_done[mt])
            v_dtevac_v = em.op("vector", lambda e, pslot=pslot: e.tensor_scalar(
                dt_fm[:, :], pslot[0:16, :], dtb_col[:, 0:1], 0.0, ALU.add, ALU.max), inc=s_v)
    for sem_name in ("s_z0", "s_z1", "s_x0", "s_x1"):
        em.wait("sync", sem_name)
    em.end("ph1_inproj")
    pclose()

    popen()
    em.begin()
    psTx = psp("psTx", [128, 16], F32)
    psTy = psp("psTy", [128, 16], F32)
    psTd = (psTx, psTy)
    em.op("scalar", lambda e: e.activation(row16b[:, :], row16a[:, :], AF.Exp, scale=-1.0),
          pre_drain=True)
    va3 = em.op("scalar", lambda e: e.activation(row16b[:, :], row16b[:, :], AF.Ln, bias=1.0),
                inc=s_a, pre_drain=True)
    em.wait("vector", s_a, va3)
    vdt = em.op("vector", lambda e: e.tensor_add(dt_fm[:, :], dt_fm[:, :], row16b[:, :]),
                inc=s_v, pre_drain=True)
    vna = em.op("scalar", lambda e: e.activation(negA[:, :], alog_col[:, :], AF.Exp), inc=s_a)
    em.wait("vector", s_a, vna)
    em.op("vector", lambda e: e.tensor_scalar(
        row16a[:, :], dt_fm[:, :], negA[:, 0:1], -1.0, ALU.mult, ALU.mult), pre_drain=True)
    vacs = em.op("vector", lambda e: e.tensor_tensor_scan(
        acs_fm[:, :], chunkmask[:, :], row16a[:, :], 0.0, ALU.mult, ALU.add), inc=s_v,
        pre_drain=True)
    em.wait("sync", s_v, vacs)
    em.dma("sync", bass.AP(rows_sp, 2 * T, [[CH, NH], [NH * CH, NCH], [1, CH]]),
           acs_fm[:, :].rearrange("p (c q) -> p c q", c=NCH), s_h)
    em.wait("scalar", s_v, vacs)
    for c in range(NCH):
        em.op("scalar", lambda e, c=c: e.activation(
            row16a[:, c * CH:(c + 1) * CH], acs_fm[:, c * CH:(c + 1) * CH],
            AF.Exp, scale=-1.0, bias=acs_fm[:, c * CH + CH - 1:c * CH + CH]))
    vcd = em.op("scalar", lambda e: e.activation(
        cd_fm[:, :], bass.AP(acs_fm, CH - 1, [[T, NH], [CH, NCH]]), AF.Exp), inc=s_a)
    em.wait("sync", s_a, vcd)
    v_cd_sp = em.dma("sync", bass.AP(rows_sp, 34 * T, [[8, NH], [1, NCH]]), cd_fm[:, :], s_h)
    em.wait("sync", s_h, v_cd_sp)
    em.dma("sync", cd_bc[:, :, :], bass.AP(rows_sp, 34 * T, [[0, 128], [8, NH], [1, NCH]]), s_h)
    tsrc = (dt_fm, acs_fm, row16a)
    tdst = (dt_tm, acs_tm, dec_tm)
    em.wait("tensor", s_v, vdt)
    em.wait("tensor", s_v, vacs)
    em.wait("tensor", s_a, vcd)
    tp_done = {}
    for idx in range(NTT * 3):
        tt, wch = idx // 3, idx % 3
        bk = idx % 2
        if idx >= 2:
            em.wait("tensor", s_v, tp_done[idx - 2])
        pv = em.op("tensor", lambda t, tt=tt, wch=wch, bk=bk: t.transpose(
            psTd[bk][:, :], tsrc[wch][:, tt * 128:(tt + 1) * 128],
            id16_f[:]), inc=s_p)
        em.wait("vector", s_p, pv)
        tp_done[idx] = em.op("vector", lambda e, tt=tt, wch=wch, bk=bk: e.tensor_copy(
            tdst[wch][:, tt, :], psTd[bk][:, :]), inc=s_v)
    em.wait("vector", s_v, tp_done[NTT * 3 - 1])
    v_ddc = em.op("vector", lambda v: v.tensor_mul(
        ddc_tm.ap().rearrange("p a b -> p (a b)"),
        dec_tm.ap().rearrange("p a b -> p (a b)"),
        dt_tm.ap().rearrange("p a b -> p (a b)")), inc=s_v, pre_drain=True)
    em.wait("scalar", s_v, v_ddc)
    v_ldt = em.op("scalar", lambda s: s.activation(
        ldt_tm.ap().rearrange("p a b -> p (a b)"),
        dt_tm.ap().rearrange("p a b -> p (a b)"), AF.Ln), inc=s_a)
    if "dtv" in dbg:
        em.wait("sync", s_v, vdt)
        em.dma("sync", dbg["dtv"][:, :], dt_fm[:, :], s_h)
    if "acs" in dbg:
        em.wait("sync", s_v, vacs)
        em.dma("sync", dbg["acs"][:, :], acs_fm[:, :], s_h)
    em.wait("sync", s_h, em.val(s_h))
    em.end("ph1b_prep")
    pclose()
    mctx.close()

    popen()
    em.begin()
    xcm = sbp("xcm", [128, 2, T], BF16)
    xstage = sbp("xstage", [128, 2, NTT, 128], BF16)
    psTq = [psp("psTq%d" % i, [128, 128], BF16) for i in range(4)]
    ld_done = {}
    tr_ev = {}
    st_done = {}
    for m in range(17):
        sl = m % 2
        sem_l = "s_tc%d" % sl
        if m >= 2:
            em.wait("sync", s_p, tr_ev[(m - 2, NTT - 1)][0])
        ld_done[m] = em.dma("sync", xcm[:, sl, :],
                            xc_sp[m * 128:(m + 1) * 128, :], sem_l)
        em.wait("tensor", sem_l, ld_done[m])
        for tt in range(NTT):
            gi = (m * NTT + tt)
            bk = gi % 4
            if gi >= 4:
                pk = ((gi - 4) // NTT, (gi - 4) % NTT)
                em.wait("tensor", tr_ev[pk][2], tr_ev[pk][1])
            pv = em.op("tensor", lambda t, sl=sl, tt=tt, bk=bk: t.transpose(
                psTq[bk][:, :], xcm[:, sl, tt * 128:(tt + 1) * 128], id128_bf[:]),
                inc=s_p)
            if gi % 2 == 0:
                em.wait("scalar", s_p, pv)
                if m >= 2 and tt <= 1:
                    em.wait("scalar", "s_ts%d" % sl)
                ev = em.op("scalar", lambda s, sl=sl, tt=tt, bk=bk: s.activation(
                    xstage[:, sl, tt, :], psTq[bk][:, :], AF.Copy), inc=s_a)
                tr_ev[(m, tt)] = (pv, ev, s_a)
            else:
                em.wait("vector", s_p, pv)
                if m >= 2 and tt <= 1:
                    em.wait("vector", "s_ts%d" % sl)
                ev = em.op("vector", lambda v, sl=sl, tt=tt, bk=bk: v.tensor_copy(
                    xstage[:, sl, tt, :], psTq[bk][:, :]), inc=s_v)
                tr_ev[(m, tt)] = (pv, ev, s_v)
        em.wait("scalar", tr_ev[(m, NTT - 2)][2], tr_ev[(m, NTT - 2)][1])
        em.wait("scalar", tr_ev[(m, NTT - 1)][2], tr_ev[(m, NTT - 1)][1])
        if m < 16:
            st_done[m] = em.dma("scalar",
                bass.AP(xt_sp, m * 128, [[DI, 128], [128 * DI, NTT], [1, 128]]),
                xstage[:, sl, :, :], "s_ts%d" % sl)
        else:
            st_done[m] = em.dma("scalar",
                bass.AP(bt_sp, 0, [[DS, 128], [128 * DS, NTT], [1, 128]]),
                xstage[:, sl, :, :], "s_ts%d" % sl)
    em.wait("sync", "s_ts0")
    em.wait("sync", "s_ts1")
    em.end("ph1c_transpose")
    pclose()

    popen()
    em.begin()
    xt_c = sbp("xt_c", [128, 2, 2, DI], BF16)
    bc_c = sbp("bc_c", [128, 2, 2, CH], BF16)
    bt_c = sbp("bt_c", [128, 2, 2, DS], BF16)
    sz_c = sbp("sz_c", [128, 2, NH, CH], BF16)
    csb_c = sbp("csb_c", [128, 2, NH, CH], F32)
    bd = sbp("bd", [128, 4, 2, 128], BF16)
    mcbt = sbp("mcbt", [128, 2, 2, CH], BF16)
    sc_f32 = sbp("sc_f32", [128, 2, 2, CH], F32)
    exps = sbp("exps", [128, 2, 2, CH], BF16)
    eabx = sbp("eabx", [128, 2, CH], BF16)
    scoresT = sbp("scoresT", [128, 4, 2, CH], BF16)
    cexp = sbp("cexp", [128, 4, CH], BF16)
    prevf = sbp("prevf", [128, NH * 128], F32)
    prevb = sbp("prevb", [128, NH * 128], BF16)
    NY2 = NH if "y2" in dbg else 4
    y2b = sbp("y2b", [128, NY2, CH], BF16)
    y3b = sbp("y3b", [128, 2, NH, CH], BF16)
    sqb = sbp("sqb", [128, NH, CH], BF16)
    rstd_tmp = sbp("rstd_tmp", [128, 1], F32)
    wd1buf = sbp("wd1buf", [128, 2, NKT, 256], BF16)
    wd2buf = sbp("wd2buf", [128, 2, 2, DM], BF16)
    h_sb = sbp("h_sb", [128, 2, T], BF16)
    sqscr = sbp("sqscr", [128, 512], BF16)
    rsub_sb = sbp("rsub_sb", [128, 2, 512], BF16)
    sqacc = sbp("sqacc", [128, 256], F32)
    onescol_f = sbp("onescol_f", [128, 1], F32)
    loss_row = sbp("loss_row", [1, 256], F32)
    losses8 = sbp("losses8", [1, 8], F32)
    minv = sbp("minv", [1, 1], F32)
    maskr = sbp("maskr", [1, 8], F32)
    psYS = psp("psYS", [128, 1024], F32)
    psCBT = psp("psCBT", [128, 512], F32)
    psH2 = psp("psH2", [128, 2, 512], F32)
    psR2 = psp("psR2", [128, 3, 512], F32)

    em.op("gpsimd", lambda e: e.memset(prevf[:], 0.0))
    v_prevmem = em.op("gpsimd", lambda e: e.memset(prevb[:], 0.0), inc=s_g)
    v_of = em.op("gpsimd", lambda e: e.memset(onescol_f[:], 1.0), inc=s_g)
    v_negid = em.op("gpsimd", lambda g: g.tensor_scalar(
        negid[:, :], id128_bf[:, :], -1.0, None, ALU.mult), inc=s_g)

    def chload(c):
        sl = c % 2
        sem = "s_c%d" % sl
        em.dma("sync", xt_c[:, sl, :, :],
               bass.AP(xt_sp, c * CH * DI, [[DI, 128], [128 * DI, 2], [1, DI]]), sem)
        em.dma("sync", bt_c[:, sl, :, :],
               bass.AP(bt_sp, c * CH * DS, [[DS, 128], [128 * DS, 2], [1, DS]]), sem)
        em.dma("sync", bc_c[:, sl, :, :],
               bass.AP(xc_sp, 16 * 128 * T + c * CH, [[T, 128], [128 * T, 2], [1, CH]]),
               sem)
        em.dma("sync", sz_c[:, sl, :, :],
               bass.AP(z_sp, c * CH, [[T, 128], [128 * T, NH], [1, CH]]), sem)
        return em.dma("sync", csb_c[:, sl, :, :],
                      bass.AP(rows_sp, 2 * T + c * NH * CH, [[0, 128], [1, NH * CH]]),
                      sem)

    def wdload(e):
        sl = e % 2
        em.dma("gpsimd", wd1buf[:, sl, :, :],
               bass.AP(Wd1, e * DM * 256, [[256, 128], [128 * 256, NKT], [1, 256]]),
               "s_d1%d" % sl)
        return em.dma("gpsimd", wd2buf[:, sl, :, :],
                      bass.AP(Wd2, e * 256 * DM, [[DM, 128], [128 * DM, 2], [1, DM]]),
                      "s_d2%d" % sl)

    ch_ld = {}
    ch_ld[0] = chload(0)
    ch_ld[1] = chload(1)
    wdv = {0: wdload(0), 1: wdload(1)}

    y2ev = {}
    scan_done = {}
    cast_done = {}
    mm_done = {}
    sqv = {}
    y3v = {}
    scorem = {}
    cexpv = {}
    bdv = {}
    lastexp = []
    lastscorem = []
    v_rstd = {}
    hev3 = {}
    hro = {}
    rro = {}
    rro_sub = {}
    sqv3 = {}

    def ffn_h_round(e, r):
        mtH, th = r % 2, r // 2
        sl = e % 2
        em.wait("tensor", "s_d1%d" % sl, wdv[e])
        for nth in range(2):
            if (e, r) != (0, 0):
                pk = (e, r - 1, nth) if r else (e - 1, 3, nth)
                em.wait("tensor", s_a, hev3[pk])
            if e >= 1 and r == 0 and nth == 0:
                em.wait("tensor", s_p, rro[(e - 1, 31)])

            def _hmm(t, sl=sl, mtH=mtH, th=th, nth=nth):
                last = None
                c0 = th * 1024 + nth * 512
                for kt in range(NKT):
                    last = t.matmul(psH2[:, nth, :],
                                    wd1buf[:, sl, kt, mtH * 128:(mtH + 1) * 128],
                                    xn_fm[:, kt, c0:c0 + 512],
                                    start=(kt == 0), stop=(kt == NKT - 1))
                return last
            hro[(e, r, nth)] = em.op("tensor", _hmm, inc=s_p)
            em.wait("scalar", s_p, hro[(e, r, nth)])
            hev3[(e, r, nth)] = em.op("scalar", lambda s_, e=e, mtH=mtH, th=th,
                                      nth=nth: s_.activation(
                h_sb[:, mtH, th * 1024 + nth * 512:th * 1024 + (nth + 1) * 512],
                psH2[:, nth, :], AF.Relu, bias=bd1_cs[:, mtH, e:e + 1]), inc=s_a)

    def ffn_r_round(e, ri):
        mtD, q = ri // 4, ri % 4
        th = q // 2
        gk = e * 32 + ri
        rsl = gk % 3
        sl = e % 2
        em.wait("tensor", "s_d2%d" % sl, wdv[e])
        em.wait("tensor", s_a, hev3[(e, 2 * th + 1, q % 2)])
        if gk == 0:
            em.wait("tensor", s_g, v_negid)
        if gk >= 3:
            em.wait("tensor", s_a, sqv3[((gk - 3) // 32, (gk - 3) % 32)])

        def _rmm(t, sl=sl, mtD=mtD, q=q, rsl=rsl):
            t.matmul(psR2[:, rsl, :], wd2buf[:, sl, 0, mtD * 128:(mtD + 1) * 128],
                     h_sb[:, 0, q * 512:(q + 1) * 512], start=True, stop=False)
            t.matmul(psR2[:, rsl, :], wd2buf[:, sl, 1, mtD * 128:(mtD + 1) * 128],
                     h_sb[:, 1, q * 512:(q + 1) * 512], start=False, stop=False)
            return t.matmul(psR2[:, rsl, :], negid[:, :],
                            xn_fm[:, mtD, q * 512:(q + 1) * 512], start=False, stop=True)
        rro[(e, ri)] = em.op("tensor", _rmm, inc=s_p)
        rro_sub[(e, ri)] = rro[(e, ri)]
        em.wait("scalar", s_p, rro[(e, ri)])
        if gk >= 1:
            pgk = gk - 1
            em.wait("scalar", s_a, sqv3[(pgk // 32, pgk % 32)])
        col = e * 32 + mtD * 4 + q
        sqv3[(e, ri)] = em.op("scalar", lambda s, e=e, mtD=mtD, rsl=rsl, col=col:
            s.activation(sqscr[:, :], psR2[:, rsl, :], AF.Square,
                         bias=bd2_cs[:, mtD, e:e + 1],
                         accum_out=sqacc[:, col:col + 1]), inc=s_a)

    def ffn_step(e, h):
        if h < 4:
            ffn_h_round(e, h)
        else:
            i0 = 3 * (h - 4) if h < 12 else 24 + 2 * (h - 12)
            n = 3 if h < 12 else 2
            for ri in range(i0, i0 + n):
                ffn_r_round(e, ri)

    def emit_consumers(c, h, sl):
        gi = c * NH + h
        em.wait("vector", s_p, mm_done[(c, h)])
        scan_done[(c, h)] = em.op("vector", lambda e, h=h, c=c, ys=(h % 2) * 512:
            e.scalar_tensor_tensor(prevf[:, h * 128:(h + 1) * 128],
                                   prevf[:, h * 128:(h + 1) * 128],
                                   cd_bc[:, h, c:c + 1], psYS[:, ys + 256:ys + 384],
                                   ALU.mult, ALU.add), inc=s_v)
        em.wait("gpsimd", s_v, scan_done[(c, h)])
        cast_done[(c, h)] = em.op("gpsimd", lambda e, h=h: e.tensor_copy(
            prevb[:, h * 128:(h + 1) * 128], prevf[:, h * 128:(h + 1) * 128]), inc=s_g)
        if gi >= NY2:
            pk4 = (gi - NY2)
            em.wait("vector", s_a, sqv[(pk4 // NH, pk4 % NH)])
            em.wait("vector", s_g, y3v[(pk4 // NH, pk4 % NH)])
        y2ev[(c, h)] = em.op("vector", lambda e, sl=sl, h=h, hy=h % NY2, ys=(h % 2) * 512:
            e.tensor_mul(y2b[:, hy, :], psYS[:, ys:ys + CH], sz_c[:, sl, h, :]), inc=s_v)
        em.wait("scalar", s_v, y2ev[(c, h)])
        sqv[(c, h)] = em.op("scalar", lambda e, h=h, hy=h % NY2: e.activation(
            sqb[:, h, :], y2b[:, hy, :], AF.Square), inc=s_a)
        em.wait("gpsimd", s_v, y2ev[(c, h)])
        if h == 0 and c >= 2:
            em.wait("gpsimd", "s_y3")
        y3v[(c, h)] = em.op("gpsimd", lambda e, h=h, hy=h % NY2, ysl=c % 2:
            e.tensor_tensor(y3b[:, ysl, h, :], y2b[:, hy, :],
                            nwcol[:, h:h + 1].broadcast_to([128, CH]), ALU.mult),
            inc=s_g)

    def produce(c, h):
        sl = c % 2
        sem_c = "s_c%d" % sl
        hs = h % 4
        hs2 = h % 2
        gi = c * NH + h
        if h == 0:
            em.wait("vector", sem_c, ch_ld[c])
            em.wait("gpsimd", sem_c, ch_ld[c])
            em.wait("scalar", sem_c, ch_ld[c])
        if gi >= 4:
            pk = gi - 4
            em.wait("vector", s_p, mm_done[(pk // NH, pk % NH)])
            em.wait("gpsimd", s_p, mm_done[(pk // NH, pk % NH)])

        def _bd(v, sl=sl, h=h, hs=hs, c=c):
            v.tensor_scalar(bd[:, hs, 0, :], bt_c[:, sl, 0, :],
                            ddc_tm[:, 2 * c, h:h + 1], None, ALU.mult)
            return v.tensor_scalar(bd[:, hs, 1, :], bt_c[:, sl, 1, :],
                                   ddc_tm[:, 2 * c + 1, h:h + 1], None, ALU.mult)
        bdv[(c, h)] = em.op("vector", _bd, inc=s_v)
        if gi >= 2:
            pk2 = gi - 2
            em.wait("scalar", s_g, cexpv[(pk2 // NH, pk2 % NH)])
        v_eabx = em.op("scalar", lambda s_, sl=sl, h=h, hs2=hs2: s_.activation(
            eabx[:, hs2, :], csb_c[:, sl, h, :], AF.Exp), inc=s_a)
        em.wait("gpsimd", s_a, v_eabx)
        cexpv[(c, h)] = em.op("gpsimd", lambda e_, sl=sl, h=h, hs=hs, hs2=hs2:
            e_.tensor_mul(cexp[:, hs, :], eabx[:, hs2, :], bc_c[:, sl, 1, :]),
            inc=s_g)
        if len(lastexp) >= 2:
            em.wait("vector", s_a, lastexp[-2])
        em.op("vector", lambda e_, sl=sl, h=h, c=c, hs2=hs2: e_.tensor_scalar(
            sc_f32[:, hs2, 0, :], csb_c[:, sl, h, :], acs_tm[:, 2 * c, h:h + 1], 0.0,
            ALU.subtract, ALU.min))
        v_diff = em.op("vector", lambda e_, sl=sl, h=h, c=c, hs2=hs2: e_.tensor_scalar(
            sc_f32[:, hs2, 1, :], csb_c[:, sl, h, :], acs_tm[:, 2 * c + 1, h:h + 1],
            0.0, ALU.subtract, ALU.min), inc=s_v)
        em.wait("scalar", s_v, v_diff)
        if len(lastscorem) >= 2:
            em.wait("scalar", s_g, lastscorem[-2])
        if c == 0 and h == 0:
            em.wait("scalar", s_a, v_ldt)
        em.op("scalar", lambda e_, hs2=hs2, h=h, c=c: e_.activation(
            exps[:, hs2, 0, :], sc_f32[:, hs2, 0, :], AF.Exp,
            bias=ldt_tm[:, 2 * c, h:h + 1]))
        v_exp = em.op("scalar", lambda e_, hs2=hs2, h=h, c=c: e_.activation(
            exps[:, hs2, 1, :], sc_f32[:, hs2, 1, :], AF.Exp,
            bias=ldt_tm[:, 2 * c + 1, h:h + 1]), inc=s_a)
        lastexp.append(v_exp)
        em.wait("gpsimd", s_a, v_exp)
        if h == 0:
            em.wait("gpsimd", s_v, cbt_mult[c])
        scorem[(c, h)] = em.op("gpsimd", lambda e_, hs=hs, hs2=hs2, msl=c % 2:
            e_.tensor_mul(scoresT[:, hs, :, :], exps[:, hs2, :, :], mcbt[:, msl, :, :]),
            inc=s_g)
        lastscorem.append(scorem[(c, h)])

    def emit_cbt(c):
        sl = c % 2
        sem_c = "s_c%d" % sl
        em.wait("tensor", sem_c, ch_ld[c])
        if c >= 2:
            em.wait("vector", s_g, scorem[(c - 2, NH - 1)])
        for st in range(2):
            if c >= 1 or st == 1:
                em.wait("tensor", s_v, cbt_mult[c] if st == 1 else cbt_mult[c - 1])

            def _cbt1(t, sl=sl, st=st):
                return t.matmul(psCBT[:, 0:CH], bc_c[:, sl, 0, st * 128:(st + 1) * 128],
                                bc_c[:, sl, 1, :], start=True, stop=True)
            v_cbt = em.op("tensor", _cbt1, inc=s_p)
            em.wait("vector", s_p, v_cbt)
            cbt_mult[c] = em.op("vector", lambda e_, st=st, msl=sl: e_.tensor_mul(
                mcbt[:, msl, st, :], psCBT[:, 0:CH], binmask[:, st, :]), inc=s_v)

    def emit_mms(c, h):
        sl = c % 2
        sem_c = "s_c%d" % sl
        hs = h % 4
        gi = c * NH + h
        em.wait("tensor", s_g, scorem[(c, h)])
        em.wait("tensor", s_g, cexpv[(c, h)])
        if c >= 1:
            em.wait("tensor", s_g, cast_done[(c - 1, h)])
        if gi >= 2:
            pk = gi - 2
            em.wait("tensor", s_v, y2ev[(pk // NH, pk % NH)])
            em.wait("tensor", s_v, scan_done[(pk // NH, pk % NH)])
        if h == 0:
            em.wait("tensor", sem_c, ch_ld[c])
            em.wait("tensor", s_v, v_diagd)
            em.wait("tensor", s_v, v_ddc)
        if h == 1 and c >= 1:
            em.wait("tensor", s_a, v_rstd[c - 1])
        if h == 0 and c == 0:
            em.wait("tensor", s_g, v_prevmem)

        def _hmms(t, sl=sl, h=h, hs=hs, ys=(h % 2) * 512):
            t.matmul(psYS[:, ys:ys + CH], xt_c[:, sl, 0, h * 128:(h + 1) * 128],
                     scoresT[:, hs, 0, :], start=True, stop=False)
            t.matmul(psYS[:, ys:ys + 128], xt_c[:, sl, 0, h * 128:(h + 1) * 128],
                     diagD[:, h, :], start=False, stop=False)
            t.matmul(psYS[:, ys + 256:ys + 384], bd[:, hs, 0, :],
                     xt_c[:, sl, 0, h * 128:(h + 1) * 128], start=False, stop=False)
            t.matmul(psYS[:, ys:ys + CH], xt_c[:, sl, 1, h * 128:(h + 1) * 128],
                     scoresT[:, hs, 1, :], start=False, stop=False)
            t.matmul(psYS[:, ys + 128:ys + CH], xt_c[:, sl, 1, h * 128:(h + 1) * 128],
                     diagD[:, h, :], start=False, stop=False)
            t.matmul(psYS[:, ys + 256:ys + 384], bd[:, hs, 1, :],
                     xt_c[:, sl, 1, h * 128:(h + 1) * 128], start=False, stop=False)
            return t.matmul(psYS[:, ys:ys + CH], prevb[:, h * 128:(h + 1) * 128],
                            cexp[:, hs, :], start=False, stop=True)
        mm_done[(c, h)] = em.op("tensor", _hmms, inc=s_p)

    def chunk_end(c):
        em.wait("tensor", s_a, sqv[(c, NH - 1)])
        em.wait("tensor", s_v, y2ev[(c, NH - 1)])
        em.wait("tensor", s_v, scan_done[(c, NH - 1)])
        for t2 in range(2):
            if c + t2 >= 1:
                em.wait("tensor", s_a, v_rstd.get(c - 1) if t2 == 0 else rstd0_done)

            def _rms(t, t2=t2):
                last = None
                for h in range(NH):
                    last = t.matmul(psYS[:, 896 + t2:897 + t2],
                                    sqb[:, h, t2 * 128:(t2 + 1) * 128],
                                    onescol_bf[:], start=(h == 0), stop=(h == NH - 1))
                return last
            v_rmsmm = em.op("tensor", _rms, inc=s_p)
            em.wait("scalar", s_p, v_rmsmm)
            em.op("scalar", lambda e_, t2=t2: e_.activation(
                rstd_tmp[:, 0:1], psYS[:, 896 + t2:897 + t2], AF.Ln,
                scale=1.0 / DI, bias=EPS))
            vv = em.op("scalar", lambda e_, t2=t2, c=c: e_.activation(
                rstd_all[:, c * 2 + t2:c * 2 + t2 + 1], rstd_tmp[:, 0:1], AF.Exp,
                scale=-0.5), inc=s_a, pre_drain=True)
            if t2 == 0:
                rstd0_done = vv
            else:
                v_rstd[c] = vv
        em.wait("sync", s_g, y3v[(c, NH - 1)])
        em.dma("sync", bass.AP(y3_sp, c * CH, [[T, 128], [128 * T, NH], [1, CH]]),
               y3b[:, c % 2, :, :], "s_y3")
        if "y2" in dbg:
            em.wait("sync", s_v, y2ev[(c, NH - 1)])
            em.dma("sync", bass.AP(dbg["y2"], c * CH, [[T, 128], [128 * T, NH], [1, CH]]),
                   y2b[:, :, :], "s_y2d")
        if c + 2 < NCH:
            em.wait("sync", s_p, mm_done[(c, NH - 1)])
            em.wait("sync", s_v, em.val(s_v))
            em.wait("sync", s_g, em.val(s_g))
            ch_ld[c + 2] = chload(c + 2)
        if c + 2 < 8:
            em.wait("gpsimd", s_p, rro[(c, 31)])
            wdv[c + 2] = wdload(c + 2)

    cbt_mult = {}
    rstd0_done = None
    emit_cbt(0)
    NGI = NCH * NH
    for gi in range(NGI + 3):
        if gi < NGI:
            if gi % NH == 13 and gi // NH + 1 < NCH:
                emit_cbt(gi // NH + 1)
            produce(gi // NH, gi % NH)
        gim = gi - 3
        if gim >= 0:
            c_m, h_m = divmod(gim, NH)
            emit_mms(c_m, h_m)
            if gim >= 1 and (gim - 1) % NH != NH - 1:
                pc, ph = divmod(gim - 1, NH)
                emit_consumers(pc, ph, pc % 2)
            ffn_step(c_m, h_m)
            if h_m == NH - 1:
                emit_consumers(c_m, h_m, c_m % 2)
                chunk_end(c_m)

    em.wait("tensor", s_a, sqv3[(7, 31)])
    em.wait("tensor", s_a, hev3[(7, 3, 1)])
    em.wait("tensor", s_g, v_of)
    v_lmm = em.op("tensor", lambda t: t.matmul(psH2[0:1, 0, 0:256], onescol_f[:],
                                               sqacc[:, :], start=True, stop=True),
                  inc=s_p)
    em.wait("vector", s_p, v_lmm)
    em.op("vector", lambda v: v.tensor_copy(loss_row[:], psH2[0:1, 0, 0:256]))
    em.op("vector", lambda v: v.tensor_reduce(
        losses8[:], loss_row.ap().rearrange("p (a b) -> p a b", a=8), AX.X, ALU.add),
        pre_drain=True)
    em.op("vector", lambda v: v.tensor_reduce(minv[:], losses8[:], AX.X, ALU.min),
          pre_drain=True)
    em.op("vector", lambda v: v.tensor_scalar(maskr[:], losses8[:], minv[0:1, 0:1], None,
                                              ALU.subtract), pre_drain=True)
    em.op("vector", lambda v: v.tensor_scalar(maskr[:], maskr[:], -1.0, 1.0,
                                              ALU.mult, ALU.add), pre_drain=True)
    v_mask = em.op("vector", lambda v: v.tensor_scalar_max(maskr[:], maskr[:], 0.0), inc=s_v,
                   pre_drain=True)
    em.wait("sync", s_v, v_mask)
    v_mask_sp = em.dma("sync", rows_sp[35:36, 0:8], maskr[:], s_h)
    em.wait("sync", s_h, v_mask_sp)
    em.wait("sync", "s_y3")
    em.end("ph2_ssm")
    pclose()

    popen()
    em.begin()
    y3h = sbp("y3h", [128, NJT, 1024], BF16)
    wout3 = sbp("wout3", [128, NJT, DM], BF16)
    wadn = sbp("wadn", [128, NKT, 8, 64], BF16)
    waup = sbp("waup", [64, 8, DM], BF16)
    baup_bf = sbp("baup_bf", [8, DM], BF16)
    mask_bc = sbp("mask_bc", [128, 8], F32)
    mask_col = sbp("mask_col", [8, 1], BF16)
    wadn_sel = sbp("wadn_sel", [128, NKT, 64], BF16)
    waup_sel = sbp("waup_sel", [64, DM], BF16)
    badn_col = sbp("badn_col", [64, 1], F32)
    badn_in = sbp("badn_in", [64, 8], F32)
    bsum = sbp("bsum", [1, DM], BF16)
    bbase_bf = sbp("bbase_bf", [1, DM], BF16)
    a_fm = sbp("a_fm", [64, T], BF16)
    wbase_sb = sbp("wbase_sb", [128, NKT, DM], BF16)
    g_rot = sbp("g_rot", [128, 2, DM], BF16)
    s_rot = sbp("s_rot", [128, 4, DM], BF16)
    u_rot = sbp("u_rot", [128, 2, DM], F32)
    o_rot = sbp("o_rot", [128, 2, DM], F32)
    psG = psp("psG", [128, 2, DM], F32)
    psA2 = psp("psA2", [64, 512], F32)
    psO = psp("psO", [128, 1024], F32)

    em.wait("sync", "s_wc")
    v_wo3 = em.dma("sync", wout3[:, :, :],
                   bass.AP(wout_bf, 0, [[DM, 128], [128 * DM, NJT], [1, DM]]), s_h)
    v_y3l = em.dma("sync", y3h[:, :, :],
                   bass.AP(y3_sp, 0, [[T, 128], [128 * T, NJT], [1, 1024]]), "s_yl")

    em.wait("sync", "s_wc")
    em.wait("scalar", "s_wc")
    v_wb = em.dma("scalar", wbase_sb[:, :, :],
                  bass.AP(wbase_bfd, 0, [[DM, 128], [128 * DM, NKT], [1, DM]]), "s_ts0")
    em.dma("scalar", wadn[:, :, :, :],
           bass.AP(wadn_bfd, 0, [[512, 128], [128 * 512, NKT], [64, 8], [1, 64]]),
           "s_ts0")
    em.dma("scalar", waup[:, :, :],
           bass.AP(waup_bfd, 0, [[8 * DM, 64], [DM, 8], [1, DM]]), "s_ts0")
    v_baup = em.dma("scalar", baup_bf[:, :], baup_bfd.ap(), "s_ts0")
    v_mbc = em.dma("sync", mask_bc[:, :], bass.AP(rows_sp, 35 * T, [[0, 128], [1, 8]]), s_h)
    v_mcol = em.dma("gpsimd", mask_col[:, :], bass.AP(rows_sp, 35 * T, [[1, 8], [1, 1]]), s_d)
    em.dma("sync", badn_in[:, :], ba_down_c[:, :], s_h)
    v_bb = em.dma("gpsimd", bbase_bf[:, :], b_base[:, :], s_d)

    sevb = {}

    def emit_s(tt):
        em.wait("tensor", "s_yl", 16 if tt < 8 else 32)
        em.wait("tensor", s_h)
        if tt >= 1:
            em.wait("tensor", s_v, sevb[tt - 1])
        if tt == 2:
            em.wait("tensor", s_v, v_bsum)

        def _omm(t, tq=tt % 8):
            last = None
            for jt in range(NJT):
                for dc2 in range(2):
                    last = t.matmul(psO[:, dc2 * 512:(dc2 + 1) * 512],
                                    y3h[:, jt, tq * 128:(tq + 1) * 128],
                                    wout3[:, jt, dc2 * 512:(dc2 + 1) * 512],
                                    start=(jt == 0), stop=(jt == NJT - 1))
            return last
        v_om = em.op("tensor", _omm, inc=s_p)
        em.wait("vector", s_p, v_om)
        sevb[tt] = em.op("vector", lambda v, tt=tt: v.tensor_scalar(
            s_rot[:, tt % 4, :], psO[:, :], rstd_all[:, tt:tt + 1], None, ALU.mult),
            inc=s_v)
        if "s" in dbg:
            em.wait("sync", s_v, sevb[tt])
            em.dma("sync", dbg["s"][tt * 128:(tt + 1) * 128, :], s_rot[:, tt % 4, :],
                   "s_y2d")
        if tt == 7:
            em.wait("sync", s_p, v_om)
            em.dma("sync", y3h[:, :, :],
                   bass.AP(y3_sp, 1024, [[T, 128], [128 * T, NJT], [1, 1024]]), "s_yl")

    emit_s(0)
    emit_s(1)
    bumm_emitted = [False]
    em.wait("vector", s_d, em.val(s_d))
    em.wait("vector", "s_ts0")
    em.wait("vector", s_h, em.val(s_h))
    em.op("vector", lambda v: v.tensor_scalar(
        wadn_sel[:, :, :], wadn[:, :, 0, :], mask_bc[:, 0:1], None, ALU.mult))
    for e in range(1, 8):
        v_wsel = em.op("vector", lambda v, e=e: v.scalar_tensor_tensor(
            wadn_sel[:, :, :], wadn[:, :, e, :], mask_bc[:, e:e + 1],
            wadn_sel[:, :, :], ALU.mult, ALU.add),
            inc=s_v if e == 7 else None, pre_drain=True)
    em.op("vector", lambda v: v.tensor_scalar(
        waup_sel[:, :], waup[:, 0, :], mask_bc[0:64, 0:1], None, ALU.mult))
    for e in range(1, 8):
        em.op("vector", lambda v, e=e: v.scalar_tensor_tensor(
            waup_sel[:, :], waup[:, e, :], mask_bc[0:64, e:e + 1],
            waup_sel[:, :], ALU.mult, ALU.add), pre_drain=True)
    em.op("vector", lambda v: v.tensor_scalar(
        badn_col[:, :], badn_in[:, 0:1], mask_bc[0:64, 0:1], None, ALU.mult))
    for e in range(1, 8):
        v_gsel = em.op("vector", lambda v, e=e: v.scalar_tensor_tensor(
            badn_col[:, :], badn_in[:, e:e + 1], mask_bc[0:64, e:e + 1],
            badn_col[:, :], ALU.mult, ALU.add), inc=s_v if e == 7 else None,
            pre_drain=True)
    em.wait("tensor", s_d)
    em.wait("tensor", "s_ts0")
    em.wait("tensor", s_v, sevb[1])
    def _bumm(t):
        t.matmul(psO[0:1, 0:512], mask_col[:, :], baup_bf[:, 0:512], start=True, stop=True)
        return t.matmul(psO[0:1, 512:1024], mask_col[:, :], baup_bf[:, 512:1024],
                        start=True, stop=True)
    v_bu = em.op("tensor", _bumm, inc=s_p)
    em.wait("vector", s_p, v_bu)
    em.wait("vector", s_d, em.val(s_d))
    v_bsum = em.op("vector", lambda v: v.tensor_add(bsum[:], psO[0:1, :], bbase_bf[:]),
                   inc=s_v)

    for nt in range(4):
        em.wait("tensor", s_v, v_wsel)

        def _amm(t, nt=nt):
            last = None
            for kt in range(NKT):
                last = t.matmul(psA2[:, :], wadn_sel[:, kt, :],
                                xn_fm[:, kt, nt * 512:(nt + 1) * 512],
                                start=(kt == 0), stop=(kt == NKT - 1))
            return last
        if nt >= 1:
            em.wait("tensor", s_v, v_aev)
        v_am = em.op("tensor", _amm, inc=s_p)
        em.wait("vector", s_p, v_am)
        em.wait("vector", s_v, v_gsel)
        v_aev = em.op("vector", lambda v, nt=nt: v.tensor_scalar(
            a_fm[:, nt * 512:(nt + 1) * 512], psA2[:, :], badn_col[:, 0:1], 0.0,
            ALU.add, ALU.max), inc=s_v)

    vo_done = [None, None]
    gst = {}
    ldv = {}

    def _fld(tt):
        slt = tt % 2
        su = "s_fu%d" % slt
        if vo_done[slt] is not None:
            em.wait("sync", s_v, vo_done[slt])
        ldv[tt] = em.dma("sync", u_rot[:, slt, :], u[tt * 128:(tt + 1) * 128, :], su)
    _fld(0)
    for tt in range(NTT):
        slt = tt % 2
        su, so = "s_fu%d" % slt, "s_fo%d" % slt
        if tt + 1 < NTT:
            _fld(tt + 1)
        v_s = ldv[tt]
        if tt + 2 < NTT:
            emit_s(tt + 2)
        em.wait("tensor", s_d)
        em.wait("tensor", s_v, v_aev)
        em.wait("tensor", s_v, v_bsum)
        if tt >= 2:
            em.wait("tensor", s_a, gst[tt - 2])

        def _gmm(t, tt=tt, slt=slt):
            last = None
            for oc in range(2):
                for kt in range(NKT):
                    t.matmul(psG[:, slt, oc * 512:(oc + 1) * 512],
                             xn_fm[:, kt, tt * 128:(tt + 1) * 128],
                             wbase_sb[:, kt, oc * 512:(oc + 1) * 512],
                             start=(kt == 0), stop=False)
                t.matmul(psG[:, slt, oc * 512:(oc + 1) * 512],
                         a_fm[:, tt * 128:(tt + 1) * 128],
                         waup_sel[:, oc * 512:(oc + 1) * 512], start=False, stop=False)
                last = t.matmul(psG[:, slt, oc * 512:(oc + 1) * 512],
                                onesrow_bf[:, :], bsum[0:1, oc * 512:(oc + 1) * 512],
                                start=False, stop=True)
            return last
        v_gm = em.op("tensor", _gmm, inc=s_p)
        em.wait("scalar", s_p, v_gm)
        if tt >= 2:
            em.wait("scalar", s_v, vo_done[slt])
        if "g" in dbg and tt >= 2:
            em.wait("scalar", so)
        gst[tt] = em.op("scalar", lambda s, slt=slt: s.activation(
            g_rot[:, slt, :], psG[:, slt, :], AF.Silu), inc=s_a)
        em.wait("vector", s_a, gst[tt])
        em.wait("vector", su, v_s)
        if tt >= 2:
            em.wait("vector", so)
        em.op("vector", lambda v, slt=slt, tt=tt: v.tensor_mul(
            o_rot[:, slt, :], s_rot[:, tt % 4, :], g_rot[:, slt, :]), pre_drain=True)
        vo_done[slt] = em.op("vector", lambda v, slt=slt: v.tensor_add(
            o_rot[:, slt, :], o_rot[:, slt, :], u_rot[:, slt, :]), inc=s_v,
            pre_drain=True)
        em.wait("sync", s_v, vo_done[slt])
        em.dma("sync", out[tt * 128:(tt + 1) * 128, :], o_rot[:, slt, :], so)
        if "g" in dbg:
            em.dma("sync", dbg["g"][tt * 128:(tt + 1) * 128, :], g_rot[:, slt, :], so)
    em.wait("sync", "s_fo0")
    em.wait("sync", "s_fo1")
    em.end("ph3b_adapter")
    pclose()

    ctx.close()
    return nc


def _stage_inputs(inputs):
    asC = np.ascontiguousarray
    W_inT = asC(inputs["L0_attn_W_in"].T)
    W_outT = asC(inputs["L0_attn_W_out"].T)
    conv_w_c = asC(inputs["L0_attn_conv_w"].reshape(18, 128, DCONV).transpose(1, 2, 0))
    conv_b_c = asC(inputs["L0_attn_conv_b"].reshape(18, 128).T)
    dt_bias = asC(inputs["L0_attn_dt_bias"].reshape(NH, 1))
    A_log = asC(inputs["L0_attn_A_log"].reshape(NH, 1))
    Dvec = asC(inputs["L0_attn_D"].reshape(1, NH))
    nw_c = asC(inputs["L0_attn_norm_weight"].reshape(NJT, 128).T)
    b_base = asC(inputs["L0_ffn_b_base"].reshape(1, DM))
    bd1_c = asC(inputs["L0_ffn_bd1"].reshape(8, 2, 128).transpose(2, 1, 0))
    bd2_c = asC(inputs["L0_ffn_bd2"].reshape(8, 8, 128).transpose(2, 1, 0))
    Wa_down_r = asC(inputs["L0_ffn_Wa_down"].transpose(1, 0, 2))
    Wa_up_r = asC(inputs["L0_ffn_Wa_up"].transpose(1, 0, 2))
    ba_down_c = asC(inputs["L0_ffn_ba_down"].T)
    ba_up = asC(inputs["L0_ffn_ba_up"])
    shared = dict(
        W_inT=W_inT, W_outT=W_outT, conv_w_c=conv_w_c, conv_b_c=conv_b_c,
        dt_bias=dt_bias, A_log=A_log, Dvec=Dvec, nw_c=nw_c,
        W_base=asC(inputs["L0_ffn_W_base"]), b_base=b_base,
        Wd1=asC(inputs["L0_ffn_Wd1"]), Wd2=asC(inputs["L0_ffn_Wd2"]),
        bd1_c=bd1_c, bd2_c=bd2_c, Wa_down_r=Wa_down_r, Wa_up_r=Wa_up_r,
        ba_down_c=ba_down_c, ba_up=ba_up,
    )
    in_maps = []
    for i in range(8):
        ui = asC(inputs["L0_attn_u"][i])
        m = dict(shared)
        m["u"] = ui
        m["uT"] = asC(ui.T)
        in_maps.append(m)
    return in_maps


def kernel(**inputs):
    in_maps = _stage_inputs(inputs)
    nc = build_program()
    res = run_bass_kernel_spmd(nc, in_maps, core_ids=list(range(8)))
    return np.stack([r["out"] for r in res.results], axis=0)


if __name__ == "__main__":
    pass


# version 28
# speedup vs baseline: 1.0782x; 1.0141x over previous
import numpy as np
from contextlib import ExitStack

import concourse.bass as bass
import concourse.mybir as mybir
from concourse.bass_utils import run_bass_kernel_spmd

F32 = mybir.dt.float32
BF16 = mybir.dt.bfloat16
AF = mybir.ActivationFunctionType
ALU = mybir.AluOpType
AX = mybir.AxisListType

T = 2048
DM = 1024
DI = 2048
DS = 128
NH = 16
HD = 128
DCONV = 4
CH = 256
NCH = T // CH
CONVD = DI + 2 * DS
DPROJ = 2 * DI + 2 * DS + NH
NMT = 35
EPS = 1e-5
NKT = DM // 128
NTT = T // 128
NJT = DI // 128


class Emitter:

    ENGINES = ("tensor", "vector", "scalar", "gpsimd", "sync")

    def __init__(self, nc, ctx):
        self.nc = nc
        self.ctx = ctx
        self.sems = {}
        self.q = None
        self._opened = False

    def sem(self, name):
        if name not in self.sems:
            h = self.ctx.enter_context(self.nc.semaphore(name))
            self.sems[name] = [h, 0]
        return self.sems[name]

    def plan_inc(self, name, k=1):
        s = self.sem(name)
        s[1] += k
        return s[1]

    def val(self, name):
        return self.sem(name)[1]

    def begin(self):
        assert self.q is None
        self.q = {e: [] for e in self.ENGINES}
        self._since_drain = {e: 0 for e in self.ENGINES}

    def end(self, name=None):
        q = self.q
        self.q = None
        with self.nc.Block(name) as blk:
            for e in self.ENGINES:
                if q[e]:
                    getattr(blk, e)(lambda eng, ops=tuple(q[e]): [op(eng) for op in ops])

    def add(self, engine, fn):
        self.q[engine].append(fn)

    def wait(self, engine, sem_name, v=None):
        v = self.val(sem_name) if v is None else v
        h = self.sem(sem_name)[0]
        self.q[engine].append(lambda eng: eng.wait_ge(h, v))

    def drain(self, engine):
        self.q[engine].append(lambda eng: eng.drain())
        self._since_drain[engine] = 0

    def op(self, engine, fn, inc=None, inck=1, autodrain=False, pre_drain=False):
        if pre_drain:
            self.drain(engine)
        if inc is None:
            self.q[engine].append(lambda eng: fn(eng))
            return None
        v = self.plan_inc(inc, inck)
        h = self.sem(inc)[0]
        self.q[engine].append(lambda eng: fn(eng).then_inc(h, inck))
        return v

    def dma(self, engine, out_ap, in_ap, inc, **kw):
        v = self.plan_inc(inc, 16)
        h = self.sem(inc)[0]
        self.q[engine].append(
            lambda eng: eng.dma_start(out=out_ap, in_=in_ap, **kw).then_inc(h, 16))
        return v


def bcast_read(shape_free_ap_dims, src_tensor, src_offset_elems, free_dims):
    return bass.AP(src_tensor, src_offset_elems, [[0, 128]] + list(free_dims))


def build_program(dbg_taps=()):
    nc = bass.Bass("TRN2", target_bir_lowering=False)
    dt = nc.dram_tensor

    u = dt("u", [T, DM], F32, kind="ExternalInput")
    uT = dt("uT", [DM, T], F32, kind="ExternalInput")
    W_inT = dt("W_inT", [DM, DPROJ], F32, kind="ExternalInput")
    conv_w_c = dt("conv_w_c", [128, DCONV, 18], F32, kind="ExternalInput")
    conv_b_c = dt("conv_b_c", [128, 18], F32, kind="ExternalInput")
    dt_bias = dt("dt_bias", [NH, 1], F32, kind="ExternalInput")
    A_log = dt("A_log", [NH, 1], F32, kind="ExternalInput")
    Dvec = dt("Dvec", [1, NH], F32, kind="ExternalInput")
    nw_c = dt("nw_c", [128, NJT], F32, kind="ExternalInput")
    W_outT = dt("W_outT", [DI, DM], F32, kind="ExternalInput")
    W_base = dt("W_base", [DM, DM], F32, kind="ExternalInput")
    b_base = dt("b_base", [1, DM], F32, kind="ExternalInput")
    Wd1 = dt("Wd1", [8, DM, 256], F32, kind="ExternalInput")
    Wd2 = dt("Wd2", [8, 256, DM], F32, kind="ExternalInput")
    bd1_c = dt("bd1_c", [128, 2, 8], F32, kind="ExternalInput")
    bd2_c = dt("bd2_c", [128, 8, 8], F32, kind="ExternalInput")
    Wa_down_r = dt("Wa_down_r", [DM, 8, 64], F32, kind="ExternalInput")
    Wa_up_r = dt("Wa_up_r", [64, 8, DM], F32, kind="ExternalInput")
    ba_down_c = dt("ba_down_c", [64, 8], F32, kind="ExternalInput")
    ba_up = dt("ba_up", [8, DM], F32, kind="ExternalInput")

    out = dt("out", [T, DM], F32, kind="ExternalOutput")

    z_sp = dt("z_sp", [DI, T], BF16)
    xc_sp = dt("xc_sp", [CONVD, T], BF16)
    y3_sp = dt("y3_sp", [DI, T], BF16)
    wout_bf = dt("wout_bf", [DI, DM], BF16)
    wbase_bfd = dt("wbase_bfd", [DM, DM], BF16)
    wadn_bfd = dt("wadn_bfd", [DM, 8, 64], BF16)
    waup_bfd = dt("waup_bfd", [64, 8, DM], BF16)
    baup_bfd = dt("baup_bfd", [8, DM], BF16)
    xt_sp = dt("xt_sp", [T, DI], BF16)
    bt_sp = dt("bt_sp", [T, DS], BF16)
    rows_sp = dt("rows_sp", [40, T], F32)

    dbg = {}
    for name, shape, dtp in (
        ("xn", [DM, T], BF16), ("dtv", [NH, T], F32), ("acs", [NH, T], F32),
        ("xconv", [CONVD, T], BF16), ("z", [DI, T], BF16), ("s", [T, DM], BF16),
        ("g", [T, DM], BF16), ("y2", [DI, T], BF16),
    ):
        if name in dbg_taps:
            dbg[name] = dt("dbg_" + name, shape, dtp, kind="ExternalOutput")

    ctx = ExitStack()
    em = Emitter(nc, ctx)
    sb = lambda n_, s_, dtp=F32: ctx.enter_context(nc.sbuf_tensor(n_, s_, dtp))
    ps = lambda n_, s_, dtp=F32: ctx.enter_context(nc.psum_tensor(n_, s_, dtp))
    pctx = [None]

    def popen():
        pctx[0] = ExitStack()

    def pclose():
        pctx[0].close()
        pctx[0] = None

    sbp = lambda n_, s_, dtp=F32: pctx[0].enter_context(nc.sbuf_tensor(n_, s_, dtp))
    psp = lambda n_, s_, dtp=F32: pctx[0].enter_context(nc.psum_tensor(n_, s_, dtp))

    const_tiles = []

    def need_const(v):
        try:
            nc.const_aps.tensor(v, [128, 1], F32)
        except AssertionError:
            h = ctx.enter_context(nc.sbuf_tensor("cst_%d" % len(const_tiles), [128, 1], F32))
            nc.const_aps.register(F32, v, h[:])
            const_tiles.append((h, v))
    for v in (EPS, -0.5, -1.0, 1.0, 0.0):
        need_const(v)

    onescol_bf = sb("onescol_bf", [128, 1], BF16)
    onesrow_bf = sb("onesrow_bf", [1, 128], BF16)
    id128_bf = sb("id128_bf", [128, 128], BF16)
    id16_f = sb("id16_f", [16, 16], F32)
    id128_f = sb("id128_f", [128, 128], F32)
    convw = sb("convw", [128, DCONV, 18], F32)
    convb = sb("convb", [128, 18], F32)
    dtb_col = sb("dtb_col", [NH, 1], F32)
    alog_col = sb("alog_col", [NH, 1], F32)
    negA = sb("negA", [NH, 1], F32)
    nwcol = sb("nwcol", [128, NJT], F32)
    Dbc = sb("Dbc", [128, NH], F32)
    diagD = sb("diagD", [128, NH, 128], BF16)
    cd_fm = sb("cd_fm", [NH, NCH], F32)
    bd1_cs = sb("bd1_cs", [128, 2, 8], F32)
    rstd_all = sb("rstd_all", [128, NTT], F32)
    bd2_cs = sb("bd2_cs", [128, 8, 8], F32)
    negid = sb("negid", [128, 128], BF16)
    dt_tm = sb("dt_tm", [128, NTT, NH], F32)
    acs_tm = sb("acs_tm", [128, NTT, NH], F32)
    dec_tm = sb("dec_tm", [128, NTT, NH], F32)
    ddc_tm = sb("ddc_tm", [128, NTT, NH], F32)
    ldt_tm = sb("ldt_tm", [128, NTT, NH], F32)
    cd_bc = sb("cd_bc", [128, NH, NCH], F32)
    binmask = sb("binmask", [128, 2, CH], BF16)

    s_d = "s_d"
    s_h = "s_h"
    s_p = "s_p"
    s_v = "s_v"
    s_a = "s_a"
    s_g = "s_g"

    xn_fm = ctx.enter_context(nc.sbuf_tensor("xn_fm", [128, NKT, T], BF16, side="right"))
    popen()
    em.begin()
    g_setup_done = None

    def _setup(g):
        for h, v in const_tiles:
            g.memset(h[:], v)
        g.memset(onescol_bf[:], 1.0)
        g.memset(onesrow_bf[:], 1.0)
        g.memset(id128_bf[:], 1.0)
        g.drain()
        g.affine_select(id128_bf[:], id128_bf[:], [[-1, 128]], ALU.is_equal, 0.0,
                        base=0, channel_multiplier=1)
        g.memset(id16_f[:], 1.0)
        g.drain()
        g.affine_select(id16_f[:], id16_f[:], [[-1, 16]], ALU.is_equal, 0.0,
                        base=0, channel_multiplier=1)
        g.memset(id128_f[:], 1.0)
        g.drain()
        g.affine_select(id128_f[:], id128_f[:], [[-1, 128]], ALU.is_equal, 0.0,
                        base=0, channel_multiplier=1)
        g.memset(binmask[:], 1.0)
        g.drain()
        g.affine_select(binmask[:, 0, :], binmask[:, 0, :], [[1, CH]], ALU.is_ge, 0.0,
                        base=0, channel_multiplier=-1)
        g.drain()
        return g.affine_select(binmask[:, 1, :], binmask[:, 1, :], [[1, CH]], ALU.is_ge, 0.0,
                               base=-128, channel_multiplier=-1)
    g_setup_done = em.op("gpsimd", _setup, inc=s_g)

    em.dma("sync", convw[:], conv_w_c[:, :, :], s_h)
    em.dma("sync", convb[:], conv_b_c[:, :], s_h)
    em.dma("sync", bd1_cs[:, :, :], bd1_c[:, :, :], s_h)
    em.dma("sync", bd2_cs[:, :, :], bd2_c[:, :, :], s_h)
    em.dma("sync", dtb_col[:], dt_bias[:, :], s_h)
    em.dma("sync", alog_col[:], A_log[:, :], s_h)
    em.dma("sync", nwcol[:], nw_c[:, :], s_h)
    v_dbc = em.dma("sync", Dbc[:], bass.AP(Dvec, 0, [[0, 128], [1, NH]]), s_h)

    u_f32 = sbp("u_f32", [128, NKT, T], F32)
    usq = sbp("usq", [128, 2, T], F32)
    onescol_f0 = sbp("onescol_f0", [128, 1], F32)
    v_o1 = em.op("gpsimd", lambda e: e.memset(onescol_f0[:], 1.0), inc=s_g)

    kv = {}
    for k in range(NKT):
        qe = "sync" if k % 2 == 0 else "scalar"
        sm = "s_u%d" % k
        kv[k] = (sm, em.dma(qe, u_f32[:, k, :],
                            bass.AP(uT, k * 128 * T, [[T, 128], [1, T]]), sm))

    psStat = psp("psStat", [1, 2048], F32)
    psStat2 = psp("psStat2", [1, 2048], F32)
    em.wait("tensor", s_g, g_setup_done)
    em.wait("tensor", s_g, v_o1)
    sq_mm_v = {}
    for k in range(NKT):
        em.wait("tensor", kv[k][0], kv[k][1])

        def _mean_k(t, k=k):
            last = None
            for nt in range(4):
                last = t.matmul(psStat[0:1, nt * 512:(nt + 1) * 512], onescol_f0[:],
                                u_f32[:, k, nt * 512:(nt + 1) * 512],
                                start=(k == 0), stop=(k == NKT - 1))
            return last
        em.op("tensor", _mean_k, inc=s_p if k == NKT - 1 else None)
        em.wait("scalar", kv[k][0], kv[k][1])
        if k >= 2:
            em.wait("scalar", s_p, sq_mm_v[k - 2])
        vsq = em.op("scalar", lambda e, k=k: e.activation(
            usq[:, k % 2, :], u_f32[:, k, :], AF.Square), inc=s_a)
        em.wait("tensor", s_a, vsq)

        def _sq_k(t, k=k):
            last = None
            for nt in range(4):
                last = t.matmul(psStat2[0:1, nt * 512:(nt + 1) * 512], onescol_f0[:],
                                usq[:, k % 2, nt * 512:(nt + 1) * 512],
                                start=(k == 0), stop=(k == NKT - 1))
            return last
        sq_mm_v[k] = em.op("tensor", _sq_k, inc=s_p)
    v_stat_mms = em.val(s_p)
    mean_row = sbp("mean_row", [1, T], F32)
    em.wait("vector", s_p, v_stat_mms)
    v_meanrow = em.op("vector", lambda e: e.tensor_scalar_mul(mean_row[:], psStat[0:1, :], 1.0 / DM),
                      inc=s_v)
    rstd_row = sbp("rstd_row", [1, T], F32)
    em.op("vector", lambda e: e.scalar_tensor_tensor(
        rstd_row[:], mean_row[:], -1.0, mean_row[:], ALU.mult, ALU.mult),
        pre_drain=True)
    v_var = em.op("vector", lambda e: e.scalar_tensor_tensor(
        rstd_row[:], psStat2[0:1, :], 1.0 / DM, rstd_row[:], ALU.mult, ALU.add), inc=s_v,
        pre_drain=True)
    em.wait("scalar", s_v, v_var)
    em.wait("scalar", s_g, g_setup_done)
    em.op("scalar", lambda e: e.activation(rstd_row[:], rstd_row[:], AF.Ln, bias=EPS))
    v_rstd = em.op("scalar", lambda e: e.activation(rstd_row[:], rstd_row[:], AF.Exp, scale=-0.5),
                   inc=s_a, pre_drain=True)
    em.wait("sync", s_v, v_meanrow)
    em.wait("sync", s_a, v_rstd)
    em.dma("sync", rows_sp[0:1, :], mean_row[:], s_h)
    v_rows = em.dma("sync", rows_sp[1:2, :], rstd_row[:], s_h)
    mean_bc = sbp("mean_bc", [128, T], F32)
    rstd_bc = sbp("rstd_bc", [128, T], F32)
    em.wait("sync", s_h, v_rows)
    em.dma("sync", mean_bc[:], bass.AP(rows_sp, 0, [[0, 128], [1, T]]), s_h)
    v_bc = em.dma("sync", rstd_bc[:], bass.AP(rows_sp, T, [[0, 128], [1, T]]), s_h)

    em.wait("vector", s_h, v_bc)
    em.wait("gpsimd", s_h, v_bc)
    vsubk = {}
    for k in range(NKT):
        vsubk[k] = em.op("vector", lambda e, k=k: e.tensor_sub(
            u_f32[:, k, :], u_f32[:, k, :], mean_bc[:]), inc=s_v)
    for k in range(NKT):
        em.wait("gpsimd", s_v, vsubk[k])
        em.op("gpsimd", lambda e, k=k: e.tensor_mul(
            xn_fm[:, k, :], u_f32[:, k, :], rstd_bc[:]), inc=s_g)
    v_xn_v = em.val(s_v)
    v_xn_g = em.val(s_g)
    em.wait("sync", s_v, v_xn_v)
    em.wait("sync", s_g, v_xn_g)
    if "xn" in dbg:
        for k in range(NKT):
            em.dma("sync", dbg["xn"][k * 128:(k + 1) * 128, :], xn_fm[:, k, :], s_h)
    em.end("ph0_ln")
    pclose()

    mctx = ExitStack()
    sbm = lambda n_, s_, dtp=F32: mctx.enter_context(nc.sbuf_tensor(n_, s_, dtp))
    dt_fm = sbm("dt_fm", [NH, T], F32)
    acs_fm = sbm("acs_fm", [NH, T], F32)
    row16a = sbm("row16a", [NH, T], F32)
    row16b = sbm("row16b", [NH, T], F32)
    chunkmask = sbm("chunkmask", [NH, T], F32)

    popen()
    em.begin()
    GW = 4
    NG = 9
    wbuf = sbp("wbuf", [128, 3, NKT, GW * 128], BF16)
    wstg = sbp("wstg", [128, 2, NKT, GW * 128], F32)
    ztile = sbp("ztile", [128, 2, T], BF16)
    xbc_sb = sbp("xbc_sb", [128, 2, 4 + T], BF16)
    tapp = sbp("tapp", [128, 2, 4, T], BF16)
    xctile = sbp("xctile", [128, 2, T], BF16)
    psA = psp("psA", [128, T], F32)
    psB = psp("psB", [128, T], F32)
    psum = (psA, psB)

    def _cmask(g):
        g.memset(chunkmask[:], 1.0)
        g.drain()
        last = None
        for c in range(NCH):
            last = g.memset(chunkmask[:, c * CH:c * CH + 1], 0.0)
        return last
    v_cmask = em.op("gpsimd", _cmask, inc=s_g)

    em.wait("vector", s_h)
    def _diagd(v):
        last = None
        for h in range(NH):
            last = v.tensor_scalar(diagD[:, h, :], id128_bf[:], Dbc[:, h:h + 1], None, ALU.mult)
        return last
    v_diagd = em.op("vector", _diagd, inc=s_v)

    def wload(gi):
        sl = gi % 2
        j0 = gi * GW * 128
        w = min(GW * 128, DPROJ - j0)
        em.dma("sync", wstg[:, sl, 0:4, 0:w],
               bass.AP(W_inT, j0, [[DPROJ, 128], [DPROJ * 128, 4], [1, w]]),
               "s_q%d" % sl)
        return em.dma("scalar", wstg[:, sl, 4:8, 0:w],
                      bass.AP(W_inT, 4 * 128 * DPROJ + j0,
                              [[DPROJ, 128], [DPROJ * 128, 4], [1, w]]),
                      "s_q%d" % sl)

    wv = {}
    castv = {}
    em.dma("gpsimd", wout_bf.ap(), W_outT.ap(), "s_wc")
    em.dma("gpsimd", wbase_bfd.ap(), W_base.ap(), "s_wc")
    em.dma("gpsimd", wadn_bfd.ap(), Wa_down_r.ap(), "s_wc")
    em.dma("gpsimd", waup_bfd.ap(), Wa_up_r.ap(), "s_wc")
    em.dma("gpsimd", baup_bfd.ap(), ba_up.ap(), "s_wc")

    def wcast(gi):
        sl = gi % 2
        bsl = gi % 3
        w = min(GW * 128, DPROJ - gi * GW * 128)
        em.wait("vector", "s_q%d" % sl, wv[gi])
        if gi >= 3:
            em.wait("vector", s_p, pe_done[(gi - 2) * GW - 1])
        castv[gi] = em.op("vector", lambda e, bsl=bsl, sl=sl, w=w: e.tensor_copy(
            wbuf[:, bsl, :, 0:w], wstg[:, sl, :, 0:w]), inc=s_v)

    wv[0] = wload(0)
    wv[1] = wload(1)
    wcast(0)
    wcast(1)
    em.wait("sync", s_v, castv[0])
    em.wait("scalar", s_v, castv[0])
    wv[2] = wload(2)
    wcast(2)

    pe_done = {}
    evac_done = {}
    silu_done = {}
    tap_done = {}
    v_dtevac_a = v_dtevac_v = None

    def emit_silu(mt):
        esl = mt % 2
        cm = mt - 16
        em.wait("scalar", tap_done[mt][0], tap_done[mt][1])
        if cm >= 2:
            em.wait("scalar", "s_x%d" % esl)
        silu_done[mt] = em.op(
            "scalar", lambda e, esl=esl, cm=cm: e.activation(
                xctile[:, esl, :], tapp[:, esl, 1, :], AF.Silu,
                bias=convb[:, cm:cm + 1]), inc=s_a)
        em.wait("sync", s_a, silu_done[mt])
        em.dma("sync", xc_sp[cm * 128:(cm + 1) * 128, :], xctile[:, esl, :], "s_x%d" % esl)
        if "xconv" in dbg:
            em.dma("sync", dbg["xconv"][cm * 128:(cm + 1) * 128, :], xctile[:, esl, :],
                   "s_x%d" % esl)

    for mt in range(NMT):
        gi = mt // GW
        sl = gi % 2
        moff = (mt % GW) * 128
        mw = 128 if mt < NMT - 1 else 16
        pslot = psum[mt % 2]
        em.wait("tensor", s_v, castv[gi])
        if mt == 0:
            em.wait("tensor", s_v, v_xn_v)
            em.wait("tensor", s_g, v_xn_g)
        if mt >= 2 and evac_done.get(mt - 2):
            sname, val = evac_done[mt - 2]
            em.wait("tensor", sname, val)

        def _mms(t, bsl=gi % 3, moff=moff, mw=mw, pslot=pslot):
            last = None
            for kt in range(NKT):
                for nt in range(4):
                    last = t.matmul(pslot[0:mw, nt * 512:(nt + 1) * 512],
                                    wbuf[:, bsl, kt, moff:moff + mw],
                                    xn_fm[:, kt, nt * 512:(nt + 1) * 512],
                                    start=(kt == 0), stop=(kt == NKT - 1))
            return last
        pe_done[mt] = em.op("tensor", _mms, inc=s_p)
        if mt % GW == 0 and gi >= 1 and gi + 2 < NG:
            em.wait("sync", s_v, castv[gi])
            em.wait("scalar", s_v, castv[gi])
            wv[gi + 2] = wload(gi + 2)
            wcast(gi + 2)

        esl = mt % 2
        if mt < 16:
            em.wait("scalar", s_p, pe_done[mt])
            if mt >= 2:
                em.wait("scalar", "s_z%d" % esl)
            silu_done[mt] = em.op(
                "scalar", lambda e, esl=esl, pslot=pslot: e.activation(
                    ztile[:, esl, :], pslot[:, :], AF.Silu), inc=s_a)
            evac_done[mt] = (s_a, silu_done[mt])
            em.wait("sync", s_a, silu_done[mt])
            em.dma("sync", z_sp[mt * 128:(mt + 1) * 128, :], ztile[:, esl, :], "s_z%d" % esl)
            if "z" in dbg:
                em.dma("sync", dbg["z"][mt * 128:(mt + 1) * 128, :], ztile[:, esl, :],
                       "s_z%d" % esl)
        elif mt < 34:
            cm = mt - 16
            if cm >= 2 and tap_done.get(mt - 2):
                sname, val = tap_done[mt - 2]
                em.wait("vector", sname, val)
            if cm == 0:
                em.op("gpsimd", lambda g: g.memset(xbc_sb[:, 0, 0:4], 0.0))
                em.op("gpsimd", lambda g: g.memset(xbc_sb[:, 1, 0:4], 0.0), inc=s_g)
            em.wait("scalar", s_p, pe_done[mt])
            ev = em.op("scalar", lambda e, esl=esl, pslot=pslot: e.activation(
                xbc_sb[:, esl, 4:4 + T], pslot[0:128, :], AF.Copy), inc=s_a)
            evac_done[mt] = (s_a, ev)
            if cm == 0:
                em.wait("vector", s_g)
            em.wait("vector", s_a, ev)
            for k in range(4):
                em.op("vector", lambda e, esl=esl, cm=cm, k=k: e.tensor_scalar(
                    tapp[:, esl, k, :], xbc_sb[:, esl, 1 + k:1 + k + T],
                    convw[:, k, cm:cm + 1], None, ALU.mult),
                    inc=s_v if k == 3 else None)
            v_pp = em.val(s_v)
            em.wait("gpsimd", s_v, v_pp)
            if cm >= 2:
                em.wait("gpsimd", s_a)
            v_ts = em.op("gpsimd", lambda g, esl=esl: g.tensor_add(
                tapp[:, esl, 0, :], tapp[:, esl, 0, :], tapp[:, esl, 1, :]), inc=s_g)
            em.op("vector", lambda e, esl=esl: e.tensor_add(
                tapp[:, esl, 2, :], tapp[:, esl, 2, :], tapp[:, esl, 3, :]),
                pre_drain=True)
            em.wait("vector", s_g, v_ts)
            tv = em.op("vector", lambda e, esl=esl: e.tensor_add(
                tapp[:, esl, 1, :], tapp[:, esl, 0, :], tapp[:, esl, 2, :]), inc=s_v,
                pre_drain=True)
            tap_done[mt] = (s_v, tv)
            if mt > 16:
                emit_silu(mt - 1)
            if mt == 33:
                emit_silu(33)
        else:
            em.wait("scalar", s_p, pe_done[mt])
            em.wait("scalar", s_h, em.val(s_h))
            v_dtevac_a = em.op("scalar", lambda e, pslot=pslot: e.activation(
                row16a[:, :], pslot[0:16, :], AF.Abs, bias=dtb_col[:, 0:1]), inc=s_a)
            em.wait("vector", s_p, pe_done[mt])
            v_dtevac_v = em.op("vector", lambda e, pslot=pslot: e.tensor_scalar(
                dt_fm[:, :], pslot[0:16, :], dtb_col[:, 0:1], 0.0, ALU.add, ALU.max), inc=s_v)
    for sem_name in ("s_z0", "s_z1", "s_x0", "s_x1"):
        em.wait("sync", sem_name)
    em.end("ph1_inproj")
    pclose()

    popen()
    em.begin()
    psTx = psp("psTx", [128, 16], F32)
    psTy = psp("psTy", [128, 16], F32)
    psTd = (psTx, psTy)
    em.op("scalar", lambda e: e.activation(row16b[:, :], row16a[:, :], AF.Exp, scale=-1.0),
          pre_drain=True)
    va3 = em.op("scalar", lambda e: e.activation(row16b[:, :], row16b[:, :], AF.Ln, bias=1.0),
                inc=s_a, pre_drain=True)
    em.wait("vector", s_a, va3)
    vdt = em.op("vector", lambda e: e.tensor_add(dt_fm[:, :], dt_fm[:, :], row16b[:, :]),
                inc=s_v, pre_drain=True)
    vna = em.op("scalar", lambda e: e.activation(negA[:, :], alog_col[:, :], AF.Exp), inc=s_a)
    em.wait("vector", s_a, vna)
    em.op("vector", lambda e: e.tensor_scalar(
        row16a[:, :], dt_fm[:, :], negA[:, 0:1], -1.0, ALU.mult, ALU.mult), pre_drain=True)
    vacs = em.op("vector", lambda e: e.tensor_tensor_scan(
        acs_fm[:, :], chunkmask[:, :], row16a[:, :], 0.0, ALU.mult, ALU.add), inc=s_v,
        pre_drain=True)
    em.wait("sync", s_v, vacs)
    em.dma("sync", bass.AP(rows_sp, 2 * T, [[CH, NH], [NH * CH, NCH], [1, CH]]),
           acs_fm[:, :].rearrange("p (c q) -> p c q", c=NCH), s_h)
    em.wait("scalar", s_v, vacs)
    for c in range(NCH):
        em.op("scalar", lambda e, c=c: e.activation(
            row16a[:, c * CH:(c + 1) * CH], acs_fm[:, c * CH:(c + 1) * CH],
            AF.Exp, scale=-1.0, bias=acs_fm[:, c * CH + CH - 1:c * CH + CH]))
    vcd = em.op("scalar", lambda e: e.activation(
        cd_fm[:, :], bass.AP(acs_fm, CH - 1, [[T, NH], [CH, NCH]]), AF.Exp), inc=s_a)
    em.wait("sync", s_a, vcd)
    v_cd_sp = em.dma("sync", bass.AP(rows_sp, 34 * T, [[8, NH], [1, NCH]]), cd_fm[:, :], s_h)
    em.wait("sync", s_h, v_cd_sp)
    em.dma("sync", cd_bc[:, :, :], bass.AP(rows_sp, 34 * T, [[0, 128], [8, NH], [1, NCH]]), s_h)
    tsrc = (dt_fm, acs_fm, row16a)
    tdst = (dt_tm, acs_tm, dec_tm)
    em.wait("tensor", s_v, vdt)
    em.wait("tensor", s_v, vacs)
    em.wait("tensor", s_a, vcd)
    tp_done = {}
    for idx in range(NTT * 3):
        tt, wch = idx // 3, idx % 3
        bk = idx % 2
        if idx >= 2:
            em.wait("tensor", s_v, tp_done[idx - 2])
        pv = em.op("tensor", lambda t, tt=tt, wch=wch, bk=bk: t.transpose(
            psTd[bk][:, :], tsrc[wch][:, tt * 128:(tt + 1) * 128],
            id16_f[:]), inc=s_p)
        em.wait("vector", s_p, pv)
        tp_done[idx] = em.op("vector", lambda e, tt=tt, wch=wch, bk=bk: e.tensor_copy(
            tdst[wch][:, tt, :], psTd[bk][:, :]), inc=s_v)
    em.wait("vector", s_v, tp_done[NTT * 3 - 1])
    v_ddc = em.op("vector", lambda v: v.tensor_mul(
        ddc_tm.ap().rearrange("p a b -> p (a b)"),
        dec_tm.ap().rearrange("p a b -> p (a b)"),
        dt_tm.ap().rearrange("p a b -> p (a b)")), inc=s_v, pre_drain=True)
    em.wait("scalar", s_v, v_ddc)
    v_ldt = em.op("scalar", lambda s: s.activation(
        ldt_tm.ap().rearrange("p a b -> p (a b)"),
        dt_tm.ap().rearrange("p a b -> p (a b)"), AF.Ln), inc=s_a)
    if "dtv" in dbg:
        em.wait("sync", s_v, vdt)
        em.dma("sync", dbg["dtv"][:, :], dt_fm[:, :], s_h)
    if "acs" in dbg:
        em.wait("sync", s_v, vacs)
        em.dma("sync", dbg["acs"][:, :], acs_fm[:, :], s_h)
    em.wait("sync", s_h, em.val(s_h))
    em.end("ph1b_prep")
    pclose()
    mctx.close()

    popen()
    em.begin()
    xcm = sbp("xcm", [128, 2, T], BF16)
    xstage = sbp("xstage", [128, 2, NTT, 128], BF16)
    psTq = [psp("psTq%d" % i, [128, 128], BF16) for i in range(4)]
    ld_done = {}
    tr_ev = {}
    st_done = {}
    for m in range(17):
        sl = m % 2
        sem_l = "s_tc%d" % sl
        if m >= 2:
            em.wait("sync", s_p, tr_ev[(m - 2, NTT - 1)][0])
        ld_done[m] = em.dma("sync", xcm[:, sl, :],
                            xc_sp[m * 128:(m + 1) * 128, :], sem_l)
        em.wait("tensor", sem_l, ld_done[m])
        for tt in range(NTT):
            gi = (m * NTT + tt)
            bk = gi % 4
            if gi >= 4:
                pk = ((gi - 4) // NTT, (gi - 4) % NTT)
                em.wait("tensor", tr_ev[pk][2], tr_ev[pk][1])
            pv = em.op("tensor", lambda t, sl=sl, tt=tt, bk=bk: t.transpose(
                psTq[bk][:, :], xcm[:, sl, tt * 128:(tt + 1) * 128], id128_bf[:]),
                inc=s_p)
            if gi % 2 == 0:
                em.wait("scalar", s_p, pv)
                if m >= 2 and tt <= 1:
                    em.wait("scalar", "s_ts%d" % sl)
                ev = em.op("scalar", lambda s, sl=sl, tt=tt, bk=bk: s.activation(
                    xstage[:, sl, tt, :], psTq[bk][:, :], AF.Copy), inc=s_a)
                tr_ev[(m, tt)] = (pv, ev, s_a)
            else:
                em.wait("vector", s_p, pv)
                if m >= 2 and tt <= 1:
                    em.wait("vector", "s_ts%d" % sl)
                ev = em.op("vector", lambda v, sl=sl, tt=tt, bk=bk: v.tensor_copy(
                    xstage[:, sl, tt, :], psTq[bk][:, :]), inc=s_v)
                tr_ev[(m, tt)] = (pv, ev, s_v)
        em.wait("scalar", tr_ev[(m, NTT - 2)][2], tr_ev[(m, NTT - 2)][1])
        em.wait("scalar", tr_ev[(m, NTT - 1)][2], tr_ev[(m, NTT - 1)][1])
        if m < 16:
            st_done[m] = em.dma("scalar",
                bass.AP(xt_sp, m * 128, [[DI, 128], [128 * DI, NTT], [1, 128]]),
                xstage[:, sl, :, :], "s_ts%d" % sl)
        else:
            st_done[m] = em.dma("scalar",
                bass.AP(bt_sp, 0, [[DS, 128], [128 * DS, NTT], [1, 128]]),
                xstage[:, sl, :, :], "s_ts%d" % sl)
    em.wait("sync", "s_ts0")
    em.wait("sync", "s_ts1")
    em.end("ph1c_transpose")
    pclose()

    popen()
    em.begin()
    xt_c = sbp("xt_c", [128, 2, 2, DI], BF16)
    bc_c = sbp("bc_c", [128, 2, 2, CH], BF16)
    bt_c = sbp("bt_c", [128, 2, 2, DS], BF16)
    sz_c = sbp("sz_c", [128, 2, NH, CH], BF16)
    csb_c = sbp("csb_c", [128, 2, NH, CH], F32)
    bd = sbp("bd", [128, 4, 2, 128], BF16)
    mcbt = sbp("mcbt", [128, 2, 2, CH], BF16)
    sc_f32 = sbp("sc_f32", [128, 2, 2, CH], F32)
    exps = sbp("exps", [128, 2, 2, CH], BF16)
    eabx = sbp("eabx", [128, 2, CH], BF16)
    scoresT = sbp("scoresT", [128, 4, 2, CH], BF16)
    cexp = sbp("cexp", [128, 4, CH], BF16)
    prevf = sbp("prevf", [128, NH * 128], F32)
    prevb = sbp("prevb", [128, NH * 128], BF16)
    NY2 = NH if "y2" in dbg else 4
    y2b = sbp("y2b", [128, NY2, CH], BF16)
    y3b = sbp("y3b", [128, 2, NH, CH], BF16)
    sqb = sbp("sqb", [128, NH, CH], BF16)
    rstd_tmp = sbp("rstd_tmp", [128, 1], F32)
    wd1buf = sbp("wd1buf", [128, 2, NKT, 256], BF16)
    wd2buf = sbp("wd2buf", [128, 2, 2, DM], BF16)
    h_sb = sbp("h_sb", [128, 2, T], BF16)
    sqscr = sbp("sqscr", [128, 512], BF16)
    rsub_sb = sbp("rsub_sb", [128, 2, 512], BF16)
    sqacc = sbp("sqacc", [128, 256], F32)
    onescol_f = sbp("onescol_f", [128, 1], F32)
    loss_row = sbp("loss_row", [1, 256], F32)
    losses8 = sbp("losses8", [1, 8], F32)
    minv = sbp("minv", [1, 1], F32)
    maskr = sbp("maskr", [1, 8], F32)
    psYS = psp("psYS", [128, 1024], F32)
    psCBT = psp("psCBT", [128, 512], F32)
    psH2 = psp("psH2", [128, 2, 512], F32)
    psR2 = psp("psR2", [128, 3, 512], F32)

    em.op("gpsimd", lambda e: e.memset(prevf[:], 0.0))
    v_prevmem = em.op("gpsimd", lambda e: e.memset(prevb[:], 0.0), inc=s_g)
    v_of = em.op("gpsimd", lambda e: e.memset(onescol_f[:], 1.0), inc=s_g)
    v_negid = em.op("gpsimd", lambda g: g.tensor_scalar(
        negid[:, :], id128_bf[:, :], -1.0, None, ALU.mult), inc=s_g)

    def chload(c):
        sl = c % 2
        sem = "s_c%d" % sl
        em.dma("sync", xt_c[:, sl, :, :],
               bass.AP(xt_sp, c * CH * DI, [[DI, 128], [128 * DI, 2], [1, DI]]), sem)
        em.dma("sync", bt_c[:, sl, :, :],
               bass.AP(bt_sp, c * CH * DS, [[DS, 128], [128 * DS, 2], [1, DS]]), sem)
        em.dma("sync", bc_c[:, sl, :, :],
               bass.AP(xc_sp, 16 * 128 * T + c * CH, [[T, 128], [128 * T, 2], [1, CH]]),
               sem)
        em.dma("sync", sz_c[:, sl, :, :],
               bass.AP(z_sp, c * CH, [[T, 128], [128 * T, NH], [1, CH]]), sem)
        return em.dma("sync", csb_c[:, sl, :, :],
                      bass.AP(rows_sp, 2 * T + c * NH * CH, [[0, 128], [1, NH * CH]]),
                      sem)

    def wdload(e):
        sl = e % 2
        em.dma("gpsimd", wd1buf[:, sl, :, :],
               bass.AP(Wd1, e * DM * 256, [[256, 128], [128 * 256, NKT], [1, 256]]),
               "s_d1%d" % sl)
        return em.dma("gpsimd", wd2buf[:, sl, :, :],
                      bass.AP(Wd2, e * 256 * DM, [[DM, 128], [128 * DM, 2], [1, DM]]),
                      "s_d2%d" % sl)

    ch_ld = {}
    ch_ld[0] = chload(0)
    ch_ld[1] = chload(1)
    wdv = {0: wdload(0), 1: wdload(1)}

    y2ev = {}
    scan_done = {}
    cast_done = {}
    mm_done = {}
    sqv = {}
    y3v = {}
    scorem = {}
    cexpv = {}
    bdv = {}
    lastexp = []
    lastscorem = []
    v_rstd = {}
    hev3 = {}
    hro = {}
    rro = {}
    rro_sub = {}
    sqv3 = {}

    def ffn_h_round(e, r):
        mtH, th = r % 2, r // 2
        sl = e % 2
        em.wait("tensor", "s_d1%d" % sl, wdv[e])
        for nth in range(2):
            if (e, r) != (0, 0):
                pk = (e, r - 1, nth) if r else (e - 1, 3, nth)
                em.wait("tensor", s_a, hev3[pk])
            if e >= 1 and r == 0 and nth == 0:
                em.wait("tensor", s_p, rro[(e - 1, 31)])

            def _hmm(t, sl=sl, mtH=mtH, th=th, nth=nth):
                last = None
                c0 = th * 1024 + nth * 512
                for kt in range(NKT):
                    last = t.matmul(psH2[:, nth, :],
                                    wd1buf[:, sl, kt, mtH * 128:(mtH + 1) * 128],
                                    xn_fm[:, kt, c0:c0 + 512],
                                    start=(kt == 0), stop=(kt == NKT - 1))
                return last
            hro[(e, r, nth)] = em.op("tensor", _hmm, inc=s_p)
            em.wait("scalar", s_p, hro[(e, r, nth)])
            hev3[(e, r, nth)] = em.op("scalar", lambda s_, e=e, mtH=mtH, th=th,
                                      nth=nth: s_.activation(
                h_sb[:, mtH, th * 1024 + nth * 512:th * 1024 + (nth + 1) * 512],
                psH2[:, nth, :], AF.Relu, bias=bd1_cs[:, mtH, e:e + 1]), inc=s_a)

    def ffn_r_round(e, ri):
        mtD, q = ri // 4, ri % 4
        th = q // 2
        gk = e * 32 + ri
        rsl = gk % 3
        sl = e % 2
        em.wait("tensor", "s_d2%d" % sl, wdv[e])
        va = hev3[(e, 2 * th + 1, q % 2)]
        if gk >= 3:
            va = max(va, sqv3[((gk - 3) // 32, (gk - 3) % 32)])
        em.wait("tensor", s_a, va)
        if gk == 0:
            em.wait("tensor", s_g, v_negid)

        def _rmm(t, sl=sl, mtD=mtD, q=q, rsl=rsl):
            t.matmul(psR2[:, rsl, :], wd2buf[:, sl, 0, mtD * 128:(mtD + 1) * 128],
                     h_sb[:, 0, q * 512:(q + 1) * 512], start=True, stop=False)
            t.matmul(psR2[:, rsl, :], wd2buf[:, sl, 1, mtD * 128:(mtD + 1) * 128],
                     h_sb[:, 1, q * 512:(q + 1) * 512], start=False, stop=False)
            return t.matmul(psR2[:, rsl, :], negid[:, :],
                            xn_fm[:, mtD, q * 512:(q + 1) * 512], start=False, stop=True)
        rro[(e, ri)] = em.op("tensor", _rmm, inc=s_p)
        rro_sub[(e, ri)] = rro[(e, ri)]
        em.wait("scalar", s_p, rro[(e, ri)])
        if gk >= 1:
            pgk = gk - 1
            em.wait("scalar", s_a, sqv3[(pgk // 32, pgk % 32)])
        col = e * 32 + mtD * 4 + q
        sqv3[(e, ri)] = em.op("scalar", lambda s, e=e, mtD=mtD, rsl=rsl, col=col:
            s.activation(sqscr[:, :], psR2[:, rsl, :], AF.Square,
                         bias=bd2_cs[:, mtD, e:e + 1],
                         accum_out=sqacc[:, col:col + 1]), inc=s_a)

    def ffn_step(e, h):
        if h < 4:
            ffn_h_round(e, h)
        else:
            i0 = 3 * (h - 4) if h < 12 else 24 + 2 * (h - 12)
            n = 3 if h < 12 else 2
            for ri in range(i0, i0 + n):
                ffn_r_round(e, ri)

    def emit_consumers(c, h, sl):
        gi = c * NH + h
        em.wait("vector", s_p, mm_done[(c, h)])
        scan_done[(c, h)] = em.op("vector", lambda e, h=h, c=c, ys=(h % 2) * 512:
            e.scalar_tensor_tensor(prevf[:, h * 128:(h + 1) * 128],
                                   prevf[:, h * 128:(h + 1) * 128],
                                   cd_bc[:, h, c:c + 1], psYS[:, ys + 256:ys + 384],
                                   ALU.mult, ALU.add), inc=s_v)
        em.wait("gpsimd", s_v, scan_done[(c, h)])
        cast_done[(c, h)] = em.op("gpsimd", lambda e, h=h: e.tensor_copy(
            prevb[:, h * 128:(h + 1) * 128], prevf[:, h * 128:(h + 1) * 128]), inc=s_g)
        if gi >= NY2:
            pk4 = (gi - NY2)
            em.wait("vector", s_a, sqv[(pk4 // NH, pk4 % NH)])
            em.wait("vector", s_g, y3v[(pk4 // NH, pk4 % NH)])
        y2ev[(c, h)] = em.op("vector", lambda e, sl=sl, h=h, hy=h % NY2, ys=(h % 2) * 512:
            e.tensor_mul(y2b[:, hy, :], psYS[:, ys:ys + CH], sz_c[:, sl, h, :]), inc=s_v)
        em.wait("scalar", s_v, y2ev[(c, h)])
        sqv[(c, h)] = em.op("scalar", lambda e, h=h, hy=h % NY2: e.activation(
            sqb[:, h, :], y2b[:, hy, :], AF.Square), inc=s_a)
        em.wait("gpsimd", s_v, y2ev[(c, h)])
        if h == 0 and c >= 2:
            em.wait("gpsimd", "s_y3")
        y3v[(c, h)] = em.op("gpsimd", lambda e, h=h, hy=h % NY2, ysl=c % 2:
            e.tensor_tensor(y3b[:, ysl, h, :], y2b[:, hy, :],
                            nwcol[:, h:h + 1].broadcast_to([128, CH]), ALU.mult),
            inc=s_g)

    def produce(c, h):
        sl = c % 2
        sem_c = "s_c%d" % sl
        hs = h % 4
        hs2 = h % 2
        gi = c * NH + h
        if h == 0:
            em.wait("vector", sem_c, ch_ld[c])
            em.wait("gpsimd", sem_c, ch_ld[c])
            em.wait("scalar", sem_c, ch_ld[c])
        if gi >= 4:
            pk = gi - 4
            em.wait("vector", s_p, mm_done[(pk // NH, pk % NH)])
            em.wait("gpsimd", s_p, mm_done[(pk // NH, pk % NH)])

        def _bd(v, sl=sl, h=h, hs=hs, c=c):
            v.tensor_scalar(bd[:, hs, 0, :], bt_c[:, sl, 0, :],
                            ddc_tm[:, 2 * c, h:h + 1], None, ALU.mult)
            return v.tensor_scalar(bd[:, hs, 1, :], bt_c[:, sl, 1, :],
                                   ddc_tm[:, 2 * c + 1, h:h + 1], None, ALU.mult)
        bdv[(c, h)] = em.op("vector", _bd, inc=s_v)
        if gi >= 2:
            pk2 = gi - 2
            em.wait("scalar", s_g, cexpv[(pk2 // NH, pk2 % NH)])
        v_eabx = em.op("scalar", lambda s_, sl=sl, h=h, hs2=hs2: s_.activation(
            eabx[:, hs2, :], csb_c[:, sl, h, :], AF.Exp), inc=s_a)
        em.wait("gpsimd", s_a, v_eabx)
        cexpv[(c, h)] = em.op("gpsimd", lambda e_, sl=sl, h=h, hs=hs, hs2=hs2:
            e_.tensor_mul(cexp[:, hs, :], eabx[:, hs2, :], bc_c[:, sl, 1, :]),
            inc=s_g)
        if len(lastexp) >= 2:
            em.wait("vector", s_a, lastexp[-2])
        em.op("vector", lambda e_, sl=sl, h=h, c=c, hs2=hs2: e_.tensor_scalar(
            sc_f32[:, hs2, 0, :], csb_c[:, sl, h, :], acs_tm[:, 2 * c, h:h + 1], 0.0,
            ALU.subtract, ALU.min))
        v_diff = em.op("vector", lambda e_, sl=sl, h=h, c=c, hs2=hs2: e_.tensor_scalar(
            sc_f32[:, hs2, 1, :], csb_c[:, sl, h, :], acs_tm[:, 2 * c + 1, h:h + 1],
            0.0, ALU.subtract, ALU.min), inc=s_v)
        em.wait("scalar", s_v, v_diff)
        if len(lastscorem) >= 2:
            em.wait("scalar", s_g, lastscorem[-2])
        if c == 0 and h == 0:
            em.wait("scalar", s_a, v_ldt)
        em.op("scalar", lambda e_, hs2=hs2, h=h, c=c: e_.activation(
            exps[:, hs2, 0, :], sc_f32[:, hs2, 0, :], AF.Exp,
            bias=ldt_tm[:, 2 * c, h:h + 1]))
        v_exp = em.op("scalar", lambda e_, hs2=hs2, h=h, c=c: e_.activation(
            exps[:, hs2, 1, :], sc_f32[:, hs2, 1, :], AF.Exp,
            bias=ldt_tm[:, 2 * c + 1, h:h + 1]), inc=s_a)
        lastexp.append(v_exp)
        em.wait("gpsimd", s_a, v_exp)
        if h == 0:
            em.wait("gpsimd", s_v, cbt_mult[c])
        scorem[(c, h)] = em.op("gpsimd", lambda e_, hs=hs, hs2=hs2, msl=c % 2:
            e_.tensor_mul(scoresT[:, hs, :, :], exps[:, hs2, :, :], mcbt[:, msl, :, :]),
            inc=s_g)
        lastscorem.append(scorem[(c, h)])

    def emit_cbt(c):
        sl = c % 2
        sem_c = "s_c%d" % sl
        em.wait("tensor", sem_c, ch_ld[c])
        if c >= 2:
            em.wait("vector", s_g, scorem[(c - 2, NH - 1)])
        for st in range(2):
            if c >= 1 or st == 1:
                em.wait("tensor", s_v, cbt_mult[c] if st == 1 else cbt_mult[c - 1])

            def _cbt1(t, sl=sl, st=st):
                return t.matmul(psCBT[:, 0:CH], bc_c[:, sl, 0, st * 128:(st + 1) * 128],
                                bc_c[:, sl, 1, :], start=True, stop=True)
            v_cbt = em.op("tensor", _cbt1, inc=s_p)
            em.wait("vector", s_p, v_cbt)
            cbt_mult[c] = em.op("vector", lambda e_, st=st, msl=sl: e_.tensor_mul(
                mcbt[:, msl, st, :], psCBT[:, 0:CH], binmask[:, st, :]), inc=s_v)

    def emit_mms(c, h):
        sl = c % 2
        sem_c = "s_c%d" % sl
        hs = h % 4
        gi = c * NH + h
        vg = max(scorem[(c, h)], cexpv[(c, h)],
                 cast_done[(c - 1, h)] if c >= 1 else 0)
        em.wait("tensor", s_g, vg)
        if gi >= 2:
            pk = (gi - 2) // NH, (gi - 2) % NH
            em.wait("tensor", s_v, max(y2ev[pk], scan_done[pk]))
        if h == 0:
            em.wait("tensor", sem_c, ch_ld[c])
            em.wait("tensor", s_v, v_diagd)
            em.wait("tensor", s_v, v_ddc)
        if h == 1 and c >= 1:
            em.wait("tensor", s_a, v_rstd[c - 1])
        if h == 0 and c == 0:
            em.wait("tensor", s_g, v_prevmem)

        def _hmms(t, sl=sl, h=h, hs=hs, ys=(h % 2) * 512):
            t.matmul(psYS[:, ys:ys + CH], xt_c[:, sl, 0, h * 128:(h + 1) * 128],
                     scoresT[:, hs, 0, :], start=True, stop=False)
            t.matmul(psYS[:, ys:ys + 128], xt_c[:, sl, 0, h * 128:(h + 1) * 128],
                     diagD[:, h, :], start=False, stop=False)
            t.matmul(psYS[:, ys + 256:ys + 384], bd[:, hs, 0, :],
                     xt_c[:, sl, 0, h * 128:(h + 1) * 128], start=False, stop=False)
            t.matmul(psYS[:, ys:ys + CH], xt_c[:, sl, 1, h * 128:(h + 1) * 128],
                     scoresT[:, hs, 1, :], start=False, stop=False)
            t.matmul(psYS[:, ys + 128:ys + CH], xt_c[:, sl, 1, h * 128:(h + 1) * 128],
                     diagD[:, h, :], start=False, stop=False)
            t.matmul(psYS[:, ys + 256:ys + 384], bd[:, hs, 1, :],
                     xt_c[:, sl, 1, h * 128:(h + 1) * 128], start=False, stop=False)
            return t.matmul(psYS[:, ys:ys + CH], prevb[:, h * 128:(h + 1) * 128],
                            cexp[:, hs, :], start=False, stop=True)
        mm_done[(c, h)] = em.op("tensor", _hmms, inc=s_p)

    def chunk_end(c):
        em.wait("tensor", s_a, sqv[(c, NH - 1)])
        em.wait("tensor", s_v, y2ev[(c, NH - 1)])
        em.wait("tensor", s_v, scan_done[(c, NH - 1)])
        for t2 in range(2):
            if c + t2 >= 1:
                em.wait("tensor", s_a, v_rstd.get(c - 1) if t2 == 0 else rstd0_done)

            def _rms(t, t2=t2):
                last = None
                for h in range(NH):
                    last = t.matmul(psYS[:, 896 + t2:897 + t2],
                                    sqb[:, h, t2 * 128:(t2 + 1) * 128],
                                    onescol_bf[:], start=(h == 0), stop=(h == NH - 1))
                return last
            v_rmsmm = em.op("tensor", _rms, inc=s_p)
            em.wait("scalar", s_p, v_rmsmm)
            em.op("scalar", lambda e_, t2=t2: e_.activation(
                rstd_tmp[:, 0:1], psYS[:, 896 + t2:897 + t2], AF.Ln,
                scale=1.0 / DI, bias=EPS))
            vv = em.op("scalar", lambda e_, t2=t2, c=c: e_.activation(
                rstd_all[:, c * 2 + t2:c * 2 + t2 + 1], rstd_tmp[:, 0:1], AF.Exp,
                scale=-0.5), inc=s_a, pre_drain=True)
            if t2 == 0:
                rstd0_done = vv
            else:
                v_rstd[c] = vv
        em.wait("sync", s_g, y3v[(c, NH - 1)])
        em.dma("sync", bass.AP(y3_sp, c * CH, [[T, 128], [128 * T, NH], [1, CH]]),
               y3b[:, c % 2, :, :], "s_y3")
        if "y2" in dbg:
            em.wait("sync", s_v, y2ev[(c, NH - 1)])
            em.dma("sync", bass.AP(dbg["y2"], c * CH, [[T, 128], [128 * T, NH], [1, CH]]),
                   y2b[:, :, :], "s_y2d")
        if c + 2 < NCH:
            em.wait("sync", s_p, mm_done[(c, NH - 1)])
            em.wait("sync", s_v, em.val(s_v))
            em.wait("sync", s_g, em.val(s_g))
            ch_ld[c + 2] = chload(c + 2)
        if c + 2 < 8:
            em.wait("gpsimd", s_p, rro[(c, 31)])
            wdv[c + 2] = wdload(c + 2)

    cbt_mult = {}
    rstd0_done = None
    emit_cbt(0)
    NGI = NCH * NH
    for gi in range(NGI + 2):
        if gi < NGI:
            if gi % NH == 13 and gi // NH + 1 < NCH:
                emit_cbt(gi // NH + 1)
            produce(gi // NH, gi % NH)
        gim = gi - 2
        if gim >= 0:
            c_m, h_m = divmod(gim, NH)
            emit_mms(c_m, h_m)
            if gim >= 1 and (gim - 1) % NH != NH - 1:
                pc, ph = divmod(gim - 1, NH)
                emit_consumers(pc, ph, pc % 2)
            ffn_step(c_m, h_m)
            if h_m == NH - 1:
                emit_consumers(c_m, h_m, c_m % 2)
                chunk_end(c_m)

    em.wait("tensor", s_a, sqv3[(7, 31)])
    em.wait("tensor", s_a, hev3[(7, 3, 1)])
    em.wait("tensor", s_g, v_of)
    v_lmm = em.op("tensor", lambda t: t.matmul(psH2[0:1, 0, 0:256], onescol_f[:],
                                               sqacc[:, :], start=True, stop=True),
                  inc=s_p)
    em.wait("vector", s_p, v_lmm)
    em.op("vector", lambda v: v.tensor_copy(loss_row[:], psH2[0:1, 0, 0:256]))
    em.op("vector", lambda v: v.tensor_reduce(
        losses8[:], loss_row.ap().rearrange("p (a b) -> p a b", a=8), AX.X, ALU.add),
        pre_drain=True)
    em.op("vector", lambda v: v.tensor_reduce(minv[:], losses8[:], AX.X, ALU.min),
          pre_drain=True)
    em.op("vector", lambda v: v.tensor_scalar(maskr[:], losses8[:], minv[0:1, 0:1], None,
                                              ALU.subtract), pre_drain=True)
    em.op("vector", lambda v: v.tensor_scalar(maskr[:], maskr[:], -1.0, 1.0,
                                              ALU.mult, ALU.add), pre_drain=True)
    v_mask = em.op("vector", lambda v: v.tensor_scalar_max(maskr[:], maskr[:], 0.0), inc=s_v,
                   pre_drain=True)
    em.wait("sync", s_v, v_mask)
    v_mask_sp = em.dma("sync", rows_sp[35:36, 0:8], maskr[:], s_h)
    em.wait("sync", s_h, v_mask_sp)
    em.wait("sync", "s_y3")
    em.end("ph2_ssm")
    pclose()

    popen()
    em.begin()
    y3h = sbp("y3h", [128, NJT, 1024], BF16)
    wout3 = sbp("wout3", [128, NJT, DM], BF16)
    wadn = sbp("wadn", [128, NKT, 8, 64], BF16)
    waup = sbp("waup", [64, 8, DM], BF16)
    baup_bf = sbp("baup_bf", [8, DM], BF16)
    mask_bc = sbp("mask_bc", [128, 8], F32)
    mask_col = sbp("mask_col", [8, 1], BF16)
    wadn_sel = sbp("wadn_sel", [128, NKT, 64], BF16)
    waup_sel = sbp("waup_sel", [64, DM], BF16)
    badn_col = sbp("badn_col", [64, 1], F32)
    badn_in = sbp("badn_in", [64, 8], F32)
    bsum = sbp("bsum", [1, DM], BF16)
    bbase_bf = sbp("bbase_bf", [1, DM], BF16)
    a_fm = sbp("a_fm", [64, T], BF16)
    wbase_sb = sbp("wbase_sb", [128, NKT, DM], BF16)
    g_rot = sbp("g_rot", [128, 2, DM], BF16)
    s_rot = sbp("s_rot", [128, 4, DM], BF16)
    u_rot = sbp("u_rot", [128, 2, DM], F32)
    o_rot = sbp("o_rot", [128, 2, DM], F32)
    psG = psp("psG", [128, 2, DM], F32)
    psA2 = psp("psA2", [64, 512], F32)
    psO = psp("psO", [128, 1024], F32)

    em.wait("sync", "s_wc")
    v_wo3 = em.dma("sync", wout3[:, :, :],
                   bass.AP(wout_bf, 0, [[DM, 128], [128 * DM, NJT], [1, DM]]), s_h)
    v_y3l = em.dma("sync", y3h[:, :, :],
                   bass.AP(y3_sp, 0, [[T, 128], [128 * T, NJT], [1, 1024]]), "s_yl")

    em.wait("sync", "s_wc")
    em.wait("scalar", "s_wc")
    v_wb = em.dma("scalar", wbase_sb[:, :, :],
                  bass.AP(wbase_bfd, 0, [[DM, 128], [128 * DM, NKT], [1, DM]]), "s_ts0")
    em.dma("scalar", wadn[:, :, :, :],
           bass.AP(wadn_bfd, 0, [[512, 128], [128 * 512, NKT], [64, 8], [1, 64]]),
           "s_ts0")
    em.dma("scalar", waup[:, :, :],
           bass.AP(waup_bfd, 0, [[8 * DM, 64], [DM, 8], [1, DM]]), "s_ts0")
    v_baup = em.dma("scalar", baup_bf[:, :], baup_bfd.ap(), "s_ts0")
    v_mbc = em.dma("sync", mask_bc[:, :], bass.AP(rows_sp, 35 * T, [[0, 128], [1, 8]]), s_h)
    v_mcol = em.dma("gpsimd", mask_col[:, :], bass.AP(rows_sp, 35 * T, [[1, 8], [1, 1]]), s_d)
    em.dma("sync", badn_in[:, :], ba_down_c[:, :], s_h)
    v_bb = em.dma("gpsimd", bbase_bf[:, :], b_base[:, :], s_d)

    sevb = {}

    def emit_s(tt):
        em.wait("tensor", "s_yl", 16 if tt < 8 else 32)
        em.wait("tensor", s_h)
        if tt >= 1:
            em.wait("tensor", s_v, sevb[tt - 1])
        if tt == 2:
            em.wait("tensor", s_v, v_bsum)

        def _omm(t, tq=tt % 8):
            last = None
            for jt in range(NJT):
                for dc2 in range(2):
                    last = t.matmul(psO[:, dc2 * 512:(dc2 + 1) * 512],
                                    y3h[:, jt, tq * 128:(tq + 1) * 128],
                                    wout3[:, jt, dc2 * 512:(dc2 + 1) * 512],
                                    start=(jt == 0), stop=(jt == NJT - 1))
            return last
        v_om = em.op("tensor", _omm, inc=s_p)
        em.wait("vector", s_p, v_om)
        sevb[tt] = em.op("vector", lambda v, tt=tt: v.tensor_scalar(
            s_rot[:, tt % 4, :], psO[:, :], rstd_all[:, tt:tt + 1], None, ALU.mult),
            inc=s_v)
        if "s" in dbg:
            em.wait("sync", s_v, sevb[tt])
            em.dma("sync", dbg["s"][tt * 128:(tt + 1) * 128, :], s_rot[:, tt % 4, :],
                   "s_y2d")
        if tt == 7:
            em.wait("sync", s_p, v_om)
            em.dma("sync", y3h[:, :, :],
                   bass.AP(y3_sp, 1024, [[T, 128], [128 * T, NJT], [1, 1024]]), "s_yl")

    emit_s(0)
    emit_s(1)
    bumm_emitted = [False]
    em.wait("vector", s_d, em.val(s_d))
    em.wait("vector", "s_ts0")
    em.wait("vector", s_h, em.val(s_h))
    em.op("vector", lambda v: v.tensor_scalar(
        wadn_sel[:, :, :], wadn[:, :, 0, :], mask_bc[:, 0:1], None, ALU.mult))
    for e in range(1, 8):
        v_wsel = em.op("vector", lambda v, e=e: v.scalar_tensor_tensor(
            wadn_sel[:, :, :], wadn[:, :, e, :], mask_bc[:, e:e + 1],
            wadn_sel[:, :, :], ALU.mult, ALU.add),
            inc=s_v if e == 7 else None, pre_drain=True)
    em.op("vector", lambda v: v.tensor_scalar(
        waup_sel[:, :], waup[:, 0, :], mask_bc[0:64, 0:1], None, ALU.mult))
    for e in range(1, 8):
        em.op("vector", lambda v, e=e: v.scalar_tensor_tensor(
            waup_sel[:, :], waup[:, e, :], mask_bc[0:64, e:e + 1],
            waup_sel[:, :], ALU.mult, ALU.add), pre_drain=True)
    em.op("vector", lambda v: v.tensor_scalar(
        badn_col[:, :], badn_in[:, 0:1], mask_bc[0:64, 0:1], None, ALU.mult))
    for e in range(1, 8):
        v_gsel = em.op("vector", lambda v, e=e: v.scalar_tensor_tensor(
            badn_col[:, :], badn_in[:, e:e + 1], mask_bc[0:64, e:e + 1],
            badn_col[:, :], ALU.mult, ALU.add), inc=s_v if e == 7 else None,
            pre_drain=True)
    em.wait("tensor", s_d)
    em.wait("tensor", "s_ts0")
    em.wait("tensor", s_v, sevb[1])
    def _bumm(t):
        t.matmul(psO[0:1, 0:512], mask_col[:, :], baup_bf[:, 0:512], start=True, stop=True)
        return t.matmul(psO[0:1, 512:1024], mask_col[:, :], baup_bf[:, 512:1024],
                        start=True, stop=True)
    v_bu = em.op("tensor", _bumm, inc=s_p)
    em.wait("vector", s_p, v_bu)
    em.wait("vector", s_d, em.val(s_d))
    v_bsum = em.op("vector", lambda v: v.tensor_add(bsum[:], psO[0:1, :], bbase_bf[:]),
                   inc=s_v)

    for nt in range(4):
        em.wait("tensor", s_v, v_wsel)

        def _amm(t, nt=nt):
            last = None
            for kt in range(NKT):
                last = t.matmul(psA2[:, :], wadn_sel[:, kt, :],
                                xn_fm[:, kt, nt * 512:(nt + 1) * 512],
                                start=(kt == 0), stop=(kt == NKT - 1))
            return last
        if nt >= 1:
            em.wait("tensor", s_v, v_aev)
        v_am = em.op("tensor", _amm, inc=s_p)
        em.wait("vector", s_p, v_am)
        em.wait("vector", s_v, v_gsel)
        v_aev = em.op("vector", lambda v, nt=nt: v.tensor_scalar(
            a_fm[:, nt * 512:(nt + 1) * 512], psA2[:, :], badn_col[:, 0:1], 0.0,
            ALU.add, ALU.max), inc=s_v)

    vo_done = [None, None]
    gst = {}
    ldv = {}

    def _fld(tt):
        slt = tt % 2
        su = "s_fu%d" % slt
        if vo_done[slt] is not None:
            em.wait("sync", s_v, vo_done[slt])
        ldv[tt] = em.dma("sync", u_rot[:, slt, :], u[tt * 128:(tt + 1) * 128, :], su)
    _fld(0)
    for tt in range(NTT):
        slt = tt % 2
        su, so = "s_fu%d" % slt, "s_fo%d" % slt
        if tt + 1 < NTT:
            _fld(tt + 1)
        v_s = ldv[tt]
        if tt + 2 < NTT:
            emit_s(tt + 2)
        em.wait("tensor", s_d)
        em.wait("tensor", s_v, v_aev)
        em.wait("tensor", s_v, v_bsum)
        if tt >= 2:
            em.wait("tensor", s_a, gst[tt - 2])

        def _gmm(t, tt=tt, slt=slt):
            last = None
            for oc in range(2):
                for kt in range(NKT):
                    t.matmul(psG[:, slt, oc * 512:(oc + 1) * 512],
                             xn_fm[:, kt, tt * 128:(tt + 1) * 128],
                             wbase_sb[:, kt, oc * 512:(oc + 1) * 512],
                             start=(kt == 0), stop=False)
                t.matmul(psG[:, slt, oc * 512:(oc + 1) * 512],
                         a_fm[:, tt * 128:(tt + 1) * 128],
                         waup_sel[:, oc * 512:(oc + 1) * 512], start=False, stop=False)
                last = t.matmul(psG[:, slt, oc * 512:(oc + 1) * 512],
                                onesrow_bf[:, :], bsum[0:1, oc * 512:(oc + 1) * 512],
                                start=False, stop=True)
            return last
        v_gm = em.op("tensor", _gmm, inc=s_p)
        em.wait("scalar", s_p, v_gm)
        if tt >= 2:
            em.wait("scalar", s_v, vo_done[slt])
        if "g" in dbg and tt >= 2:
            em.wait("scalar", so)
        gst[tt] = em.op("scalar", lambda s, slt=slt: s.activation(
            g_rot[:, slt, :], psG[:, slt, :], AF.Silu), inc=s_a)
        em.wait("vector", s_a, gst[tt])
        em.wait("vector", su, v_s)
        if tt >= 2:
            em.wait("vector", so)
        em.op("vector", lambda v, slt=slt, tt=tt: v.tensor_mul(
            o_rot[:, slt, :], s_rot[:, tt % 4, :], g_rot[:, slt, :]), pre_drain=True)
        vo_done[slt] = em.op("vector", lambda v, slt=slt: v.tensor_add(
            o_rot[:, slt, :], o_rot[:, slt, :], u_rot[:, slt, :]), inc=s_v,
            pre_drain=True)
        em.wait("sync", s_v, vo_done[slt])
        em.dma("sync", out[tt * 128:(tt + 1) * 128, :], o_rot[:, slt, :], so)
        if "g" in dbg:
            em.dma("sync", dbg["g"][tt * 128:(tt + 1) * 128, :], g_rot[:, slt, :], so)
    em.wait("sync", "s_fo0")
    em.wait("sync", "s_fo1")
    em.end("ph3b_adapter")
    pclose()

    ctx.close()
    return nc


def _stage_inputs(inputs):
    asC = np.ascontiguousarray
    W_inT = asC(inputs["L0_attn_W_in"].T)
    W_outT = asC(inputs["L0_attn_W_out"].T)
    conv_w_c = asC(inputs["L0_attn_conv_w"].reshape(18, 128, DCONV).transpose(1, 2, 0))
    conv_b_c = asC(inputs["L0_attn_conv_b"].reshape(18, 128).T)
    dt_bias = asC(inputs["L0_attn_dt_bias"].reshape(NH, 1))
    A_log = asC(inputs["L0_attn_A_log"].reshape(NH, 1))
    Dvec = asC(inputs["L0_attn_D"].reshape(1, NH))
    nw_c = asC(inputs["L0_attn_norm_weight"].reshape(NJT, 128).T)
    b_base = asC(inputs["L0_ffn_b_base"].reshape(1, DM))
    bd1_c = asC(inputs["L0_ffn_bd1"].reshape(8, 2, 128).transpose(2, 1, 0))
    bd2_c = asC(inputs["L0_ffn_bd2"].reshape(8, 8, 128).transpose(2, 1, 0))
    Wa_down_r = asC(inputs["L0_ffn_Wa_down"].transpose(1, 0, 2))
    Wa_up_r = asC(inputs["L0_ffn_Wa_up"].transpose(1, 0, 2))
    ba_down_c = asC(inputs["L0_ffn_ba_down"].T)
    ba_up = asC(inputs["L0_ffn_ba_up"])
    shared = dict(
        W_inT=W_inT, W_outT=W_outT, conv_w_c=conv_w_c, conv_b_c=conv_b_c,
        dt_bias=dt_bias, A_log=A_log, Dvec=Dvec, nw_c=nw_c,
        W_base=asC(inputs["L0_ffn_W_base"]), b_base=b_base,
        Wd1=asC(inputs["L0_ffn_Wd1"]), Wd2=asC(inputs["L0_ffn_Wd2"]),
        bd1_c=bd1_c, bd2_c=bd2_c, Wa_down_r=Wa_down_r, Wa_up_r=Wa_up_r,
        ba_down_c=ba_down_c, ba_up=ba_up,
    )
    in_maps = []
    for i in range(8):
        ui = asC(inputs["L0_attn_u"][i])
        m = dict(shared)
        m["u"] = ui
        m["uT"] = asC(ui.T)
        in_maps.append(m)
    return in_maps


def kernel(**inputs):
    in_maps = _stage_inputs(inputs)
    nc = build_program()
    res = run_bass_kernel_spmd(nc, in_maps, core_ids=list(range(8)))
    return np.stack([r["out"] for r in res.results], axis=0)


if __name__ == "__main__":
    pass
